# version 12
# baseline (speedup 1.0000x reference)
import os
import numpy as np
import ml_dtypes
import concourse.bass as bass
import concourse.mybir as mybir
from concourse.bass_utils import run_bass_kernel_spmd

F32 = mybir.dt.float32
BF16 = mybir.dt.bfloat16
AF = mybir.ActivationFunctionType
ALU = mybir.AluOpType
AX = mybir.AxisListType.X
bfdt = ml_dtypes.bfloat16

_FP64_SIM = os.environ.get("BASS_SIM_FP64") == "1"
RND_C = float(1.5 * 2.0 ** 52) if _FP64_SIM else float(1.5 * 2.0 ** 23)

C = 2048
NCC = C // 128
HEADS = 32
HD = 64
DFF = 8192
NOC = DFF // 128
TT = 512
LCH = 128
NCH = TT // LCH
LN_EPS = 1e-5
GN_EPS = 1e-5 * 64.0
RMS_EPS = 1e-8
ATANH_HALF = 0.5493061443340548
QCLIP = 1e-4

PF = dict(maa_w=0, maa_k=1, maa_v=2, maa_r=3, maa_g=4, maa_x=5,
          g1=6, g2=7, g1a=8, g2a=9)
NPF = 10
MAA_F = ["maa_w", "maa_k", "maa_v", "maa_r", "maa_g"]


class Sem:
    __slots__ = ("h", "n", "name")

    def __init__(self, nc, name):
        self.h = nc.alloc_semaphore(name)
        self.n = 0
        self.name = name

    def inc(self, k=1):
        self.n += k
        return (self, k)

    def at(self):
        return (self, self.n)


class Builder:
    ENGINES = ("sync", "gpsimd", "tensor", "vector", "scalar")

    def __init__(self, nc, strict=False):
        self.nc = nc
        self.strict = strict
        self.steps = {e: [] for e in self.ENGINES}
        self.all_sems = []

    def emit(self, eng, fn, waits=(), incs=()):
        waits = list(waits)
        if self.strict:
            have = {id(s) for (s, v) in waits}
            for s in self.all_sems:
                if id(s) not in have:
                    waits.append(s.at())
        self.steps[eng].append((fn, waits, list(incs)))

    def finish(self):
        with self.nc.Block() as block:
            def mk_body(name):
                steps = self.steps[name]
                def body(e):
                    for fn, waits, incs in steps:
                        for (s, v) in waits:
                            if v > 0:
                                e.wait_ge(s.h, v)
                        inst = fn(e)
                        for (s, k) in incs:
                            assert inst is not None, f"no inst to tag on {name}"
                            inst = inst.then_inc(s.h, k)
                return body
            block.sync(mk_body("sync"))
            block.gpsimd(mk_body("gpsimd"))
            block.tensor(mk_body("tensor"))
            block.vector(mk_body("vector"))
            block.scalar(mk_body("scalar"))


def build_program(n_sum_tiles=4, n_main_tiles=4, taps=(), strict=False,
                  stop_after=None, skip=()):
    skip = set(skip)
    taps = set(taps)
    T_MAIN = n_main_tiles * TT
    T_SUM = n_sum_tiles * TT
    NSL = n_main_tiles
    NQ = T_MAIN // 512
    Y3P = max(1, T_MAIN // 1024)
    Y3T = T_MAIN // Y3P

    nc = bass.Bass("TRN2", target_bir_lowering=False,
                   detect_race_conditions=False)
    b = Builder(nc, strict=strict)

    DI = lambda name, shape, dt=F32: nc.dram_tensor(name, list(shape), dt, kind="ExternalInput")
    DO = lambda name, shape, dt=F32: nc.dram_tensor(name, list(shape), dt, kind="ExternalOutput")
    DS = lambda name, shape, dt=F32: nc.dram_tensor(name, list(shape), dt, kind="Internal")
    SB = lambda name, shape, dt=F32: nc.alloc_sbuf_tensor(name, list(shape), dt)
    PS = lambda name, shape, dt=F32: nc.alloc_psum_tensor(name, list(shape), dt)

    d_x = DI("xT", (C, T_MAIN))
    d_xs = DI("xsT", (C, max(T_SUM, 1)))
    d_xprev = DI("xprev", (128, NCC))
    d_hmask = DI("hmask", (128, 1))
    d_smask = DI("smask", (128, 1))
    d_cvec = DI("cvec", (128, NCC))
    d_wmod = DI("wmod", (C, 6 * C))
    d_wproj = {m: DI("w" + m, (C, C)) for m in "rkvgo"}
    d_w1 = DI("w1", (C, DFF))
    d_w2 = DI("w2", (C, DFF))
    d_w3 = DI("w3", (DFF, C))
    d_maaw1 = DI("maa_w1", (C, 160))
    d_maaw2 = DI("maa_w2", (160, C))
    d_dw1 = DI("dec_w1", (C, 64))
    d_dw2 = DI("dec_w2", (65, C))
    d_pf = DI("pfeat", (128, NCC, NPF))
    d_g3 = DI("g3vec", (128, NOC))
    d_u = DI("uvec", (128, NCC))
    d_gb2 = DI("gambet", (2, C))
    d_cbias = DI("cbias", (128, 3))
    d_ones1 = DI("ones1", (128, 1), BF16)
    d_onesf = DI("onesf", (128, 1))
    d_onesr = DI("onesr", (1, 128), BF16)
    d_onesrf = DI("onesrf", (1, 128))
    d_id128 = DI("id128", (128, 128), BF16)
    d_mst4 = DI("mst4", (128, 512), BF16)
    d_ect = DI("ect", (128, NCC, 32), BF16)
    d_ectt = DI("ectt", (32, NCC * 128), BF16)
    s_x2 = DS("x2scr", (C, T_MAIN))
    s_hm = DS("hmscr", (DFF, T_MAIN), BF16)
    s_w3q = DS("w3qscr", (64, 128, 16, 128), BF16)
    s_wc = {m: DS("wc" + m, (16, 128, NCC, 128), BF16) for m in "rkvgo"}
    d_out = DO("outT", (C, T_MAIN))

    t_pf = SB("t_pf", (128, NCC, NPF))
    t_g3 = SB("t_g3", (128, NOC))
    t_g3a = SB("t_g3a", (128, NOC))
    t_u = SB("t_u", (128, NCC))
    t_cvec = SB("t_cvec", (128, NCC))
    t_cvb = SB("t_cvb", (128, NCC), BF16)
    t_xprev = SB("t_xprev", (128, NCC))
    t_hmask = SB("t_hmask", (128, 1))
    t_smask = SB("t_smask", (128, 1))
    t_hpcol = SB("t_hpcol", (128, NCC))
    t_hpc2 = SB("t_hpc2", (128, NCC))
    t_ones = SB("t_ones", (128, 1), BF16)
    t_onesf = SB("t_onesf", (128, 1))
    t_onesr = SB("t_onesr", (1, 128), BF16)
    t_onesrf = SB("t_onesrf", (1, 128))
    t_id = SB("t_id", (128, 128), BF16)
    t_mst = SB("t_mst", (128, 512), BF16)
    t_ect = SB("t_ect", (128, NCC, 32), BF16)
    t_egb = SB("t_egb", (33, NCC * 128), BF16)
    t_gb2 = SB("t_gb2", (1, C), BF16)
    t_cbias = SB("t_cbias", (128, 3))
    t_mod = SB("t_mod", (128, 6, NCC))
    t_tanhd = SB("t_tanhd", (65, TT), BF16)
    t_a1 = SB("t_a1", (128, TT), BF16)
    t_ptot = SB("t_ptot", (128, NCC))
    t_small = SB("t_small", (128, 72))
    t_qs = SB("t_qs", (128, 8))
    t_wacc = SB("t_wacc", (128, 64))
    t_row = SB("t_row", (1, 1536))
    t_rowb = SB("t_rowb", (1, 1024), BF16)
    t_rs33 = SB("t_rs33", (33, 512), BF16)
    t_mu33 = SB("t_mu33", (33, 512), BF16)
    t_sqb = SB("t_sqb", (128, 2, 512), BF16)
    t_ev = SB("t_ev", (128, 2, 512))
    t_x32 = SB("t_x32", (128, 2, 512))

    AR_BYTES = 172064
    t_arena = SB("t_arena", (128, AR_BYTES // 2), BF16)

    def addr(h):
        return nc.lookup_mloc(h).addr
    A0 = addr(t_arena)

    def AT(name, shape, dt, off):
        return nc.alloc_sbuf_tensor_at(name, list(shape), dt, offset=A0 + off)

    t_xbf = AT("t_xbf", (128, NCC, TT), BF16, 0)
    t_hext = AT("t_hext", (128, NCC, TT + 1), BF16, 16384)
    t_hh = AT("t_hh", (128, NCC, TT), BF16, 32800)
    t_xm = AT("t_xm", (128, NCC, TT), BF16, 49184)
    t_y = AT("t_y", (128, NCC, TT), BF16, 49184)
    t_r = AT("t_r", (128, NCC, TT), BF16, 65568)
    t_k = AT("t_k", (128, NCC, TT), BF16, 81952)
    t_v = AT("t_v", (128, NCC, TT), BF16, 98336)
    t_g = AT("t_g", (128, NCC, TT), BF16, 114720)
    t_wbuf = AT("t_wbuf", (128, 2, NCC, 128), BF16, 131104)
    t_mw1 = AT("t_mw1", (128, NCC, 160), BF16, 139296)
    t_mw2a = AT("t_mw2a", (96, C), BF16, 144416)
    t_mw2b = AT("t_mw2b", (64, C), BF16, 148512)
    t_dw1 = AT("t_dw1", (128, NCC, 64), BF16, 152608)
    t_dw2 = AT("t_dw2", (65, C), BF16, 154656)
    t_sall = AT("t_sall", (128, NCC * HD), F32, 158752)
    t_wmp = AT("t_wmp", (128, 4, 256), F32, 162848)

    t_d = AT("t_d", (128, NCC, LCH), BF16, 0)
    t_rku = AT("t_rku", (128, NCC, LCH), BF16, 0)
    t_fr = AT("t_fr", (128, NCC, LCH), BF16, 4096)
    t_ktr = AT("t_ktr", (128, NCC, LCH), BF16, 4096)
    t_rtE = AT("t_rtE", (128, NCC, LCH), BF16, 8192)
    t_rtO = AT("t_rtO", (128, NCC, LCH), BF16, 12288)
    t_kt = AT("t_kt", (128, NCC, LCH), BF16, 16384)
    t_kp = AT("t_kp", (128, NCC, LCH), BF16, 20480)
    t_fk = AT("t_fk", (128, NCC, LCH), BF16, 20480)
    t_vtr = AT("t_vtr", (128, NCC, LCH), BF16, 24576)
    t_am = AT("t_am", (128, HEADS, LCH), BF16, 28672)
    t_sbf = AT("t_sbf", (128, NCC * HD), BF16, 36864)
    t_cs = AT("t_cs", (128, NCC, LCH + 1), F32, 38912)
    t_a2 = AT("t_a2", (64, TT), BF16, 166944)
    t_rkuhb = AT("t_rkuhb", (32, LCH), BF16, 167968)
    t_wacc2 = AT("t_wacc2", (128, 324), F32, 168224)

    t_xq1 = AT("t_xq1", (128, NCC, T_MAIN), BF16, 0)
    t_xq2 = AT("t_xq2", (128, NCC, T_MAIN), BF16, 65568)
    t_x2f = AT("t_x2f", (128, NCC, 512), F32, 131104)
    t_wf32 = AT("t_wf32", (128, NCC, 128), F32, 131104)
    t_wq = AT("t_wq", (128, 2, NCC, 128), BF16, 139296)
    t_hmst = AT("t_hmst", (128, 512), BF16, 147488)
    t_sqst = AT("t_sqst", (128, 512), BF16, 148512)
    t_bc1 = AT("t_bc1", (128, 2048), BF16, 149536)
    t_bc2 = AT("t_bc2", (128, 2048), BF16, 153632)
    t_m3 = AT("t_m3", (128, 2048), BF16, 157728)
    t_inv1 = AT("t_inv1", (1, 2048), BF16, 163872)
    t_inv2 = AT("t_inv2", (1, 2048), BF16, 167968)
    t_xq3 = AT("t_xq3", (128, NOC, 1024 if T_MAIN >= 1024 else T_MAIN), BF16, 0)
    t_s3row = AT("t_s3row", (1, 2048), BF16, 147488)
    t_mx3row = AT("t_mx3row", (1, 2048), BF16, 151584)
    t_bc3 = AT("t_bc3", (128, 1024), BF16, 155680)
    t_q3bc = AT("t_q3bc", (128, 1024), BF16, 157728)
    t_inv3 = AT("t_inv3", (1, 2048), BF16, 159776)

    p_a = PS("p_a", (128, 512))
    p_b = PS("p_b", (128, 512))
    p_y1 = PS("p_y1", (128, 512))
    p_y2 = PS("p_y2", (128, 512))
    p_t1 = PS("p_t1", (128, 512))
    p_t2 = PS("p_t2", (128, 512))
    p_m1 = PS("p_m1", (128, 512))
    p_m2 = PS("p_m2", (128, 1024), BF16)

    s_dw = Sem(nc, "s_dw")
    s_dx = Sem(nc, "s_dx")
    s_dc = Sem(nc, "s_dc")
    s_t = Sem(nc, "s_t")
    s_v = Sem(nc, "s_v")
    s_s = Sem(nc, "s_s")
    s_g = Sem(nc, "s_g")
    s_o = Sem(nc, "s_o")
    b.all_sems = [s_dw, s_dx, s_dc, s_t, s_v, s_s, s_g, s_o]

    TAP = {}
    def deftap(name, shape, dt=F32):
        if name in taps:
            TAP[name] = DO("tap_" + name, shape, dt)
    deftap("mod", (128, 6, NCC))
    deftap("modraw", (128, 96))
    deftap("h", (128, NCC, TT), BF16)
    deftap("hh", (128, NCC, TT), BF16)
    deftap("xm", (128, NCC, TT), BF16)
    deftap("tanhd", (65, TT), BF16)
    deftap("a1", (96, TT), BF16)
    deftap("xmg", (128, NCC, TT), BF16)
    deftap("xmw", (128, NCC, TT), BF16)
    deftap("r", (128, NCC, TT), BF16)
    deftap("k", (128, NCC, TT), BF16)
    deftap("v", (128, NCC, TT), BF16)
    deftap("g", (128, NCC, TT), BF16)
    deftap("d", (128, NCC, LCH), BF16)
    deftap("cs", (128, NCC, LCH + 1))
    deftap("am", (128, HEADS, LCH), BF16)
    deftap("sall", (128, NCC * HD))
    deftap("y", (128, NCC, TT), BF16)
    deftap("qs", (128, 6))
    deftap("cells", (1, 24))
    deftap("cells2", (1, 24))
    deftap("wacc", (128, 324))
    deftap("wacc1", (128, 64))
    deftap("xq1", (128, NCC, T_MAIN), BF16)
    deftap("xq2", (128, NCC, T_MAIN), BF16)
    deftap("inv1", (1, T_MAIN), BF16)
    deftap("inv2", (1, T_MAIN), BF16)
    deftap("m3", (128, T_MAIN), BF16)
    deftap("s3row", (1, T_MAIN), BF16)
    deftap("mx3row", (1, T_MAIN), BF16)
    deftap("inv3", (1, T_MAIN), BF16)
    deftap("xq3", (128, NOC, 1024 if T_MAIN >= 1024 else T_MAIN), BF16)

    QUEUE = dict(sync=s_dx, gpsimd=s_dw, scalar=s_dc)

    def dma(q, dst, src, sem=None, waits=()):
        sem_ = sem if sem is not None else QUEUE[q]
        b.emit(q, (lambda dst=dst, src=src: (lambda e: e.dma_start(out=dst, in_=src)))(),
               waits, [sem_.inc(16)])
        return sem_.at()

    def ve(fn, waits=()):
        b.emit("vector", fn, waits, [s_v.inc()])
        return s_v.at()

    def sc(fn, waits=()):
        b.emit("scalar", fn, waits, [s_s.inc()])
        return s_s.at()

    def te(fn, waits=()):
        b.emit("tensor", fn, waits, [s_t.inc()])
        return s_t.at()

    def ge(fn, waits=()):
        b.emit("gpsimd", fn, waits, [s_g.inc()])
        return s_g.at()

    def pfc(name, cc):
        return t_pf[:, cc, PF[name]:PF[name] + 1]

    def modc(m, cc):
        return t_mod[:, m, cc:cc + 1]

    def mw2_block(f, oc):
        if f < 3:
            return t_mw2a[32 * f:32 * f + 32, oc * 128:(oc + 1) * 128]
        return t_mw2b[32 * (f - 3):32 * (f - 3) + 32, oc * 128:(oc + 1) * 128]

    def a_slice(f):
        if f < 3:
            return t_a1[32 * f:32 * f + 32, :]
        return t_a2[32 * (f - 3):32 * (f - 3) + 32, :]

    for dst, src in [
        (t_pf[:, :, 0:8], d_pf[:, :, 0:8]), (t_g3[:, :], d_g3[:, :]),
        (t_u[:, :], d_u[:, :]),
        (t_cvec[:, :], d_cvec[:, :]), (t_xprev[:, :], d_xprev[:, :]),
        (t_hmask[:, :], d_hmask[:, :]), (t_smask[:, :], d_smask[:, :]),
        (t_ones[:, :], d_ones1[:, :]), (t_onesf[:, :], d_onesf[:, :]),
        (t_onesr[:, :], d_onesr[:, :]), (t_onesrf[:, :], d_onesrf[:, :]),
        (t_id[:, :], d_id128[:, :]),
        (t_mst[:, :], d_mst4[:, :]), (t_ect[:, :, :], d_ect[:, :, :]),
        (t_egb[0:32, :], d_ectt[:, :]), (t_cbias[:, :], d_cbias[:, :]),
    ]:
        dma("sync", dst, src)
    X_CONST = s_dx.at()
    GB2 = dma("gpsimd", t_gb2[0:1, :], d_gb2[0:1, :])
    GB2B = dma("gpsimd", t_egb[32:33, :], d_gb2[1:2, :])
    dma("gpsimd", t_mw1[:, :, :], d_maaw1.ap().rearrange("(cc p) m -> p cc m", p=128))
    dma("gpsimd", t_mw2a[0:96, :], d_maaw2[0:96, :])
    dma("gpsimd", t_mw2b[0:64, :], d_maaw2[96:160, :])
    dma("gpsimd", t_dw1[:, :, :], d_dw1.ap().rearrange("(cc p) m -> p cc m", p=128))
    dma("gpsimd", t_dw2[:, :], d_dw2[:, :])
    W_CONST = s_dw.at()

    ge(lambda e: e.memset(t_cs[:, :, 0:1], 0.0))
    ge(lambda e: e.memset(t_sall[:, :], 0.0))
    ge(lambda e: e.memset(t_tanhd[64:65, :], 1.0))
    ge(lambda e: e.memset(t_rs33[32:33, :], 0.0))
    ge(lambda e: e.memset(t_mu33[32:33, :], 1.0))
    ge(lambda e: e.memset(t_wacc[:, :], 0.0))
    ge(lambda e: e.memset(t_wacc2[:, :], 0.0))
    MEMSET = s_g.at()

    G1A = sc(lambda e: e.activation(out=t_pf[:, :, 8:10], in_=t_pf[:, :, 6:8],
                                    func=AF.Abs), waits=[X_CONST])
    G3A = sc(lambda e: e.activation(out=t_g3a[:, :], in_=t_g3[:, :],
                                    func=AF.Abs), waits=[X_CONST])
    def gbc_round(h):
        def fn(e):
            return e.matmul(p_m1[0:32, 0:512], t_onesr[0:1, 0:32],
                            t_gb2[0:1, h * 512:(h + 1) * 512],
                            start=True, stop=True)
        return fn
    def egb_round(h):
        def fn(e):
            return e.tensor_tensor(out=t_egb[0:32, h * 512:(h + 1) * 512],
                                   in0=t_egb[0:32, h * 512:(h + 1) * 512],
                                   in1=p_m1[0:32, 0:512], op=ALU.mult)
        return fn
    for h in range(4):
        m = te(gbc_round(h), waits=[X_CONST, MEMSET, GB2] if h == 0 else [s_v.at()])
        ve(egb_round(h), waits=[m])
    EGB = ve(lambda e: e.tensor_scalar(out=t_egb[32:33, :],
                                       in0=t_egb[32:33, :],
                                       scalar1=-1.0, scalar2=None, op0=ALU.mult),
             waits=[GB2B])

    SILU_C = sc(lambda e: e.activation(out=t_small[:, 0:NCC], in_=t_cvec[:, :],
                                       func=AF.Silu), waits=[X_CONST])
    CVB = ve(lambda e: e.tensor_copy(out=t_cvb[:, :], in_=t_small[:, 0:NCC]),
             waits=[SILU_C])
    wmod_r = d_wmod.ap().rearrange("(cc p) m -> p cc m", p=128)
    mm_done = {}
    ld_done = {}
    for jc in range(96):
        pb = jc % 2
        wts = [mm_done[jc - 2]] if jc >= 2 else []
        ld_done[jc] = dma("gpsimd", t_wbuf[:, pb, :, :],
                          wmod_r[:, :, jc * 128:(jc + 1) * 128], waits=wts)
        def mkmm(jc=jc, pb=pb):
            def fn(e):
                last = None
                for ic in range(NCC):
                    last = e.matmul(p_m1[:, jc:jc + 1], t_wbuf[:, pb, ic, :],
                                    t_cvb[:, ic:ic + 1],
                                    start=(ic == 0), stop=(ic == NCC - 1))
                return last
            return fn
        w_mm = [ld_done[jc]]
        if jc == 0:
            w_mm.append(CVB)
        mm_done[jc] = te(mkmm(), waits=w_mm)
    MODMM = mm_done[95]
    MODEV = ve(lambda e: e.tensor_copy(
        out=t_mod[:, :, :].rearrange("p m cc -> p (m cc)"), in_=p_m1[:, 0:96]),
        waits=[MODMM])
    if "modraw" in TAP:
        dma("sync", TAP["modraw"].ap(), t_mod[:, :, :]
            .rearrange("p m cc -> p (m cc)"), sem=s_o, waits=[MODEV])

    def mp1(e):
        e.tensor_scalar(out=t_mod[:, 1, :], in0=t_mod[:, 1, :], scalar1=1.0,
                        scalar2=None, op0=ALU.add)
        e.tensor_scalar(out=t_mod[:, 4, :], in0=t_mod[:, 4, :], scalar1=1.0,
                        scalar2=None, op0=ALU.add)
        last = None
        for i_, idx in enumerate((2, 5)):
            e.tensor_scalar(out=t_small[:, i_ * 32:i_ * 32 + 16],
                            in0=t_mod[:, idx, :], scalar1=0.0,
                            scalar2=2.0, op0=ALU.is_ge, op1=ALU.mult)
            last = e.tensor_scalar(out=t_small[:, i_ * 32 + 16:i_ * 32 + 32],
                                   in0=t_mod[:, idx, :],
                                   scalar1=-1.0, scalar2=None, op0=ALU.mult)
        return last
    MP1 = ve(mp1, waits=[MODEV])
    def mp2(e):
        last = None
        for i_, idx in enumerate((2, 5)):
            e.tensor_scalar(out=t_small[:, i_ * 32:i_ * 32 + 16],
                            in0=t_small[:, i_ * 32:i_ * 32 + 16], scalar1=-1.0,
                            scalar2=None, op0=ALU.add)
            last = e.tensor_tensor(out=t_small[:, i_ * 32 + 16:i_ * 32 + 32],
                                   in0=t_mod[:, idx, :],
                                   in1=t_small[:, i_ * 32 + 16:i_ * 32 + 32],
                                   op=ALU.max)
        return last
    MP2 = ve(mp2, waits=[MP1])
    def mp3(e):
        last = None
        for i_ in range(2):
            last = e.tensor_scalar(out=t_small[:, i_ * 32 + 16:i_ * 32 + 32],
                                   in0=t_small[:, i_ * 32 + 16:i_ * 32 + 32],
                                   scalar1=0.25, scalar2=None, op0=ALU.max)
        return last
    MP3 = ve(mp3, waits=[MP2])
    def mp4(e):
        last = None
        for i_, idx in enumerate((2, 5)):
            last = e.tensor_tensor(out=t_mod[:, idx, :],
                                   in0=t_small[:, i_ * 32:i_ * 32 + 16],
                                   in1=t_small[:, i_ * 32 + 16:i_ * 32 + 32],
                                   op=ALU.mult)
        return last
    MOD_DONE = ve(mp4, waits=[MP3])

    if "mod" in TAP:
        dma("sync", TAP["mod"][:, :, :], t_mod[:, :, :], sem=s_o, waits=[MOD_DONE])

    def hprev_build():
        def sq(e):
            return e.activation(out=t_small[:, 48:48 + NCC], in_=t_xprev[:, :],
                                func=AF.Square)
        sv = sc(sq, waits=[X_CONST])
        def mm(e):
            e.matmul(p_b[0:1, 0:NCC], t_onesf[:, :], t_xprev[:, :],
                     start=True, stop=True)
            return e.matmul(p_b[0:1, NCC:2 * NCC], t_onesf[:, :],
                            t_small[:, 48:48 + NCC], start=True, stop=True)
        mmv = te(mm, waits=[sv])
        def smath0(e):
            e.reduce_sum(out=t_row[0:1, 0:1], in_=p_b[0:1, 0:NCC], axis=AX)
            return e.reduce_sum(out=t_row[0:1, 1:2], in_=p_b[0:1, NCC:2 * NCC],
                                axis=AX)
        sm0 = ve(smath0, waits=[mmv])
        sma = ve(lambda e: e.tensor_scalar(out=t_row[0:1, 0:1],
                                           in0=t_row[0:1, 0:1],
                                           scalar1=1.0 / C, scalar2=None,
                                           op0=ALU.mult), waits=[sm0])
        smb = ve(lambda e: e.tensor_tensor(out=t_row[0:1, 2:3],
                                           in0=t_row[0:1, 0:1],
                                           in1=t_row[0:1, 0:1], op=ALU.mult),
                 waits=[sma])
        sm = ve(lambda e: e.scalar_tensor_tensor(out=t_row[0:1, 1:2],
                                                 in0=t_row[0:1, 1:2],
                                                 scalar=1.0 / C,
                                                 in1=t_row[0:1, 2:3],
                                                 op0=ALU.mult,
                                                 op1=ALU.subtract),
                waits=[smb])
        def rsf(e):
            e.activation(out=t_row[0:1, 1:2], in_=t_row[0:1, 1:2],
                         func=AF.Ln, bias=t_cbias[0:1, 0:1])
            return e.activation(out=t_row[0:1, 1:2], in_=t_row[0:1, 1:2],
                                func=AF.Exp, scale=-0.5)
        rf = sc(rsf, waits=[sm])
        def bc(e):
            return e.matmul(p_b[:, 2:4], t_onesrf[:, :], t_row[0:1, 0:2],
                            start=True, stop=True)
        bcv = te(bc, waits=[rf])
        h1 = ve(lambda e: e.tensor_scalar(out=t_hpcol[:, :], in0=t_xprev[:, :],
                                          scalar1=p_b[:, 2:3], scalar2=None,
                                          op0=ALU.subtract),
                waits=[bcv, MOD_DONE])
        h2_ = ve(lambda e: e.tensor_scalar(out=t_hpcol[:, :], in0=t_hpcol[:, :],
                                           scalar1=p_b[:, 3:4], scalar2=None,
                                           op0=ALU.mult), waits=[h1])
        h3 = ve(lambda e: e.tensor_tensor(out=t_hpcol[:, :], in0=t_hpcol[:, :],
                                          in1=t_mod[:, 1, :], op=ALU.mult),
                waits=[h2_])
        h4 = ve(lambda e: e.tensor_tensor(out=t_hpcol[:, :], in0=t_hpcol[:, :],
                                          in1=t_mod[:, 0, :], op=ALU.add),
                waits=[h3])
        return ve(lambda e: e.tensor_scalar(out=t_hpcol[:, :], in0=t_hpcol[:, :],
                                            scalar1=t_hmask[:, 0:1], scalar2=None,
                                            op0=ALU.mult), waits=[h4])
    HPREV = hprev_build()

    wmean_state = dict(i=0, lastred=None)
    WSRC = [d_w1, d_w2, d_w3]
    NWP = [DFF // 256, DFF // 256, C // 256]
    def emit_wmean(npieces):
        for _ in range(npieces):
            i = wmean_state["i"]
            if i >= 384:
                return
            mi, pi = i // 128, i % 128
            src = WSRC[mi].ap().rearrange("(cc p) m -> p cc m", p=128)
            ncol = NWP[mi]
            cc0, off = (pi // ncol) * 4, (pi % ncol) * 256
            piece = src[:, cc0:cc0 + 4, off:off + 256]
            wts = [wmean_state["lastred"]] if wmean_state["lastred"] else [MEMSET]
            ld = dma("scalar", t_wmp[:, :, :], piece, waits=wts)
            def red(e, i=i):
                dst = t_wacc[:, i:i + 1] if i < 64 else \
                    t_wacc2[:, i - 64:i - 63]
                return e.reduce_sum(out=dst,
                                    in_=t_wmp[:, :, :]
                                    .rearrange("p a bb -> p (a bb)"), axis=AX,
                                    apply_absolute_value=True)
            wmean_state["lastred"] = ve(red, waits=[ld])
            wmean_state["i"] = i + 1

    def wmean_finalize():
        emit_wmean(384)
        def redf(e):
            e.reduce_sum(out=t_wacc2[:, 320:321], in_=t_wacc[:, 0:64], axis=AX)
            e.reduce_sum(out=t_wacc2[:, 323:324], in_=t_wacc2[:, 0:64], axis=AX)
            e.reduce_sum(out=t_wacc2[:, 321:322], in_=t_wacc2[:, 64:192], axis=AX)
            return e.reduce_sum(out=t_wacc2[:, 322:323],
                                in_=t_wacc2[:, 192:320], axis=AX)
        rf0 = ve(redf, waits=[wmean_state["lastred"]])
        rf_ = ve(lambda e: e.tensor_tensor(out=t_wacc2[:, 320:321],
                                           in0=t_wacc2[:, 320:321],
                                           in1=t_wacc2[:, 323:324], op=ALU.add),
                 waits=[rf0])
        mm = te(lambda e: e.matmul(p_m1[0:1, 0:3], t_onesf[:, :],
                                   t_wacc2[:, 320:323], start=True, stop=True),
                waits=[rf_])
        NW = [float(C * DFF), float(C * DFF), float(DFF * C)]
        def wsc1(e):
            last = None
            for i in range(3):
                last = e.tensor_scalar(out=t_small[0:1, 54 + i:55 + i],
                                       in0=p_m1[0:1, i:i + 1],
                                       scalar1=1.0 / NW[i], scalar2=None,
                                       op0=ALU.mult)
            return last
        w1_ = ve(wsc1, waits=[mm])
        def wsc2(e):
            for i in range(3):
                e.tensor_scalar(out=t_small[0:1, 57 + i:58 + i],
                                in0=t_small[0:1, 54 + i:55 + i], scalar1=RMS_EPS,
                                scalar2=ATANH_HALF, op0=ALU.add, op1=ALU.mult)
            e.tensor_scalar(out=t_small[0:1, 63:66],
                            in0=t_small[0:1, 54:57], scalar1=1.0,
                            scalar2=None, op0=ALU.add)
            return e.tensor_scalar(out=t_small[0:1, 66:69],
                                   in0=t_small[0:1, 54:57], scalar1=-1.0,
                                   scalar2=-1.0, op0=ALU.mult, op1=ALU.subtract)
        w2_ = ve(wsc2, waits=[w1_])
        def wsc3(e):
            e.tensor_scalar(out=t_small[0:1, 60:63],
                            in0=t_small[0:1, 57:60],
                            scalar1=-1.0, scalar2=None, op0=ALU.mult)
            return e.reciprocal(out=t_small[0:1, 66:69],
                                in_=t_small[0:1, 66:69])
        w3_ = ve(wsc3, waits=[w2_])
        ws = ve(lambda e: e.tensor_tensor(out=t_small[0:1, 63:66],
                                          in0=t_small[0:1, 63:66],
                                          in1=t_small[0:1, 66:69], op=ALU.mult),
                waits=[w3_])
        AH = sc(lambda e: e.activation(out=t_small[0:1, 69:72],
                                       in_=t_small[0:1, 63:66],
                                       func=AF.Ln), waits=[ws])
        AH2 = ve(lambda e: e.tensor_scalar(out=t_small[0:1, 69:72],
                                           in0=t_small[0:1, 69:72], scalar1=0.5,
                                           scalar2=None, op0=ALU.mult), waits=[AH])
        NT = w3_
        def qbc(e):
            e.matmul(p_m1[:, 8:11], t_onesrf[:, :], t_small[0:1, 57:60],
                     start=True, stop=True)
            return e.matmul(p_m1[:, 11:14], t_onesrf[:, :], t_small[0:1, 60:63],
                            start=True, stop=True)
        qb = te(qbc, waits=[NT, AH2])
        QS = ve(lambda e: e.tensor_copy(out=t_qs[:, 0:6], in_=p_m1[:, 8:14]),
                waits=[qb])
        if "qs" in TAP:
            dma("sync", TAP["qs"][:, :], t_qs[:, 0:6], sem=s_o, waits=[QS])
        if "cells" in TAP:
            dma("sync", TAP["cells"].ap(), t_small[0:1, 48:72], sem=s_o,
                waits=[QS])
        if "wacc" in TAP:
            dma("sync", TAP["wacc"].ap(), t_wacc2[:, :], sem=s_o, waits=[QS])
        if "wacc1" in TAP:
            dma("sync", TAP["wacc1"].ap(), t_wacc[:, :], sem=s_o, waits=[QS])
        return QS, AH2

    wbuf_last_use = {0: [mm_done[94]], 1: [mm_done[95]]}
    CACHE_DONE = {}

    def ln_h_stage(xin_dram, toff, first_tile, sum_phase, xbf_free,
                   last_tile=False):
        xsl = xin_dram.ap().rearrange("(cc p) t -> p cc t", p=128)[:, :, toff:toff + TT]
        XL = dma("gpsimd", t_xbf[:, :, :], xsl, waits=xbf_free)
        if first_tile:
            if sum_phase:
                cp = ve(lambda e: e.memset(t_hext[:, :, 0:1], 0.0), waits=[MEMSET] + xbf_free)
            else:
                cp = ve(lambda e: e.tensor_copy(out=t_hext[:, :, 0:1],
                                                in_=t_hpcol[:, :]),
                        waits=[HPREV] + xbf_free)
        else:
            cp = ve(lambda e: e.tensor_copy(out=t_hext[:, :, 0:1], in_=t_hpc2[:, :]),
                    waits=xbf_free)
        for cc in range(NCC):
            def sqf(e, cc=cc):
                return e.activation(out=t_sqb[:, cc % 2, :], in_=t_xbf[:, cc, :],
                                    func=AF.Square)
            w = [XL] if cc < 2 else [s_t.at()]
            smk = sc(sqf, waits=w)
            def mmf(e, cc=cc):
                e.matmul(p_m1[0:1, 0:512], t_ones[:, :], t_xbf[:, cc, :],
                         start=(cc == 0), stop=(cc == NCC - 1))
                return e.matmul(p_a[0:1, 0:512], t_ones[:, :], t_sqb[:, cc % 2, :],
                                start=(cc == 0), stop=(cc == NCC - 1))
            te(mmf, waits=[smk] + ([XL] if cc == 0 else []))
        STATS_MM = s_t.at()
        def stat_math(e):
            e.tensor_scalar(out=t_row[0:1, 1024 - 512:1024], in0=p_m1[0:1, 0:512],
                            scalar1=1.0 / C, scalar2=None, op0=ALU.mult)
            e.tensor_tensor(out=t_row[0:1, 1024:1536], in0=t_row[0:1, 512:1024],
                            in1=t_row[0:1, 512:1024], op=ALU.mult)
            return e.scalar_tensor_tensor(out=t_row[0:1, 0:512], in0=p_a[0:1, 0:512],
                                          scalar=1.0 / C, in1=t_row[0:1, 1024:1536],
                                          op0=ALU.mult, op1=ALU.subtract)
        sm = ve(stat_math, waits=[STATS_MM])
        def raf(e):
            e.activation(out=t_row[0:1, 0:512], in_=t_row[0:1, 0:512],
                         func=AF.Ln, bias=t_cbias[0:1, 0:1])
            return e.activation(out=t_row[0:1, 0:512], in_=t_row[0:1, 0:512],
                                func=AF.Exp, scale=-0.5)
        ra = sc(raf, waits=[sm])
        def bcf(e):
            e.matmul(p_m1[:, 0:512], t_onesrf[:, :], t_row[0:1, 0:512],
                     start=True, stop=True)
            return e.matmul(p_a[:, 0:512], t_onesrf[:, :], t_row[0:1, 512:1024],
                            start=True, stop=True)
        bc = te(bcf, waits=[ra])
        for cc in range(NCC):
            def hf(e, cc=cc):
                e.tensor_tensor(out=t_ev[:, 0, :], in0=t_xbf[:, cc, :],
                                in1=p_a[:, 0:512], op=ALU.subtract)
                e.tensor_tensor(out=t_ev[:, 0, :], in0=t_ev[:, 0, :],
                                in1=p_m1[:, 0:512], op=ALU.mult)
                return e.tensor_scalar(out=t_hext[:, cc, 1:TT + 1], in0=t_ev[:, 0, :],
                                       scalar1=modc(1, cc), scalar2=modc(0, cc),
                                       op0=ALU.mult, op1=ALU.add)
            ve(hf, waits=[bc, cp, MOD_DONE] if cc == 0 else [])
        H_DONE = s_v.at()
        BND = ve(lambda e: e.tensor_copy(out=t_hpc2[:, :],
                                         in_=t_hext[:, :, TT:TT + 1]))
        HH_DONE = ve(lambda e: e.tensor_tensor(out=t_hh[:, :, :],
                                               in0=t_hext[:, :, 0:TT],
                                               in1=t_hext[:, :, 1:TT + 1],
                                               op=ALU.subtract))
        lntap = []
        if last_tile and not sum_phase:
            if "h" in TAP:
                lntap.append(dma("sync", TAP["h"].ap(), t_hext[:, :, 1:TT + 1],
                                 sem=s_o, waits=[H_DONE]))
            if "hh" in TAP:
                lntap.append(dma("sync", TAP["hh"].ap(), t_hh[:, :, :],
                                 sem=s_o, waits=[HH_DONE]))
        return dict(h=H_DONE, hh=HH_DONE, x_loaded=XL, bnd=BND, lntap=lntap)

    def maa_stage(marks):
        for cc in range(NCC):
            def xf(e, cc=cc):
                return e.scalar_tensor_tensor(out=t_xm[:, cc, :], in0=t_hh[:, cc, :],
                                              scalar=pfc("maa_x", cc),
                                              in1=t_hext[:, cc, 1:TT + 1],
                                              op0=ALU.mult, op1=ALU.add)
            ve(xf, waits=[marks["hh"], W_CONST] if cc == 0 else [])
        XXX = s_v.at()
        def amm(e):
            last = None
            for ic in range(NCC):
                e.matmul(p_b[:, 0:512], t_mw1[:, ic, 0:128], t_xm[:, ic, :],
                         start=(ic == 0), stop=(ic == NCC - 1))
            for ic in range(NCC):
                last = e.matmul(p_a[0:32, 0:512], t_mw1[:, ic, 128:160],
                                t_xm[:, ic, :],
                                start=(ic == 0), stop=(ic == NCC - 1))
            return last
        AMM = te(amm, waits=[XXX])
        def atanh(e):
            e.activation(out=t_a1[0:96, :], in_=p_b[0:96, 0:512], func=AF.Tanh)
            e.activation(out=t_a2[0:32, :], in_=p_b[96:128, 0:512], func=AF.Tanh)
            return e.activation(out=t_a2[32:64, :], in_=p_a[0:32, 0:512],
                                func=AF.Tanh)
        return sc(atanh, waits=[AMM])

    def xm_build(f_idx, ATANH, after=()):
        marks = []
        for oc in range(NCC):
            bank = p_a if oc % 2 == 0 else p_b
            def mmf(e, oc=oc, bank=bank):
                return e.matmul(bank[:, 0:512], mw2_block(f_idx, oc), a_slice(f_idx),
                                start=True, stop=True)
            wts = ([ATANH] + list(after)) if oc < 2 else [marks[oc - 2]]
            mmm = te(mmf, waits=wts)
            def xmf(e, oc=oc, bank=bank):
                e.scalar_tensor_tensor(out=t_ev[:, oc % 2, :], in0=bank[:, 0:512],
                                       scalar=pfc(MAA_F[f_idx], oc),
                                       in1=t_hh[:, oc, :], op0=ALU.add, op1=ALU.mult)
                return e.tensor_tensor(out=t_xm[:, oc, :], in0=t_ev[:, oc % 2, :],
                                       in1=t_hext[:, oc, 1:TT + 1], op=ALU.add)
            marks.append(ve(xmf, waits=[mmm]))
        return s_v.at()

    def decay_proj(XM):
        def dmm(e):
            last = None
            for ic in range(NCC):
                last = e.matmul(p_a[0:64, 0:512], t_dw1[:, ic, :], t_xm[:, ic, :],
                                start=(ic == 0), stop=(ic == NCC - 1))
            return last
        DM = te(dmm, waits=[XM])
        return sc(lambda e: e.activation(out=t_tanhd[0:64, :], in_=p_a[0:64, 0:512],
                                         func=AF.Tanh), waits=[DM])

    def full_proj(m, dst, XM, silu=False, first_use=False):
        wrr = d_wproj[m].ap().rearrange("(cc p) o -> p cc o", p=128)
        ev_marks = []
        for pc in range(NCC):
            wts = list(wbuf_last_use.get(pc % 2, []))
            if first_use:
                src = wrr[:, :, pc * 128:(pc + 1) * 128]
            else:
                src = s_wc[m][pc]
            piece = dma("gpsimd", t_wbuf[:, pc % 2, :, :], src,
                        waits=wts + ([] if first_use else [CACHE_DONE[m]]))
            cw = None
            if first_use:
                cw = dma("sync", s_wc[m][pc], t_wbuf[:, pc % 2, :, :],
                         waits=[piece])
            bank = p_a if pc % 2 == 0 else p_b
            def pmm(e, pc=pc, bank=bank):
                last = None
                for fi in range(NCC):
                    last = e.matmul(bank[:, 0:512], t_wbuf[:, pc % 2, fi, :],
                                    t_xm[:, fi, :], start=(fi == 0),
                                    stop=(fi == NCC - 1))
                return last
            wts2 = [piece, XM]
            if pc >= 2:
                wts2.append(ev_marks[pc - 2])
            mmdone = te(pmm, waits=wts2)
            wbuf_last_use[pc % 2] = [mmdone] + ([cw] if cw else [])
            if silu:
                def evf(e, pc=pc, bank=bank, dst=dst):
                    return e.activation(out=dst[:, pc, :], in_=bank[:, 0:512],
                                        func=AF.Silu)
                ev_marks.append(sc(evf, waits=[mmdone]))
            else:
                def evf(e, pc=pc, bank=bank, dst=dst):
                    return e.tensor_copy(out=dst[:, pc, :], in_=bank[:, 0:512])
                ev_marks.append(ve(evf, waits=[mmdone]))
        if first_use:
            CACHE_DONE[m] = s_dc.at()
        return ev_marks[-1]

    def wkv_chunk(c0, sum_phase, PD, prev_yev):
        for half in range(2):
            def vtf(e, half=half):
                last = None
                for j in range(8):
                    cc = half * 8 + j
                    last = e.transpose(p_m2[:, j * 128:(j + 1) * 128],
                                       t_v[:, cc, c0:c0 + LCH], t_id[:, :])
                return last
            tt_ = te(vtf, waits=[PD["v"], s_s.at()] + prev_yev)
            def vev(e, half=half):
                return e.copy(out=t_vtr[:, half * 8:(half + 1) * 8, :],
                              in_=p_m2[:, 0:1024])
            sc(vev, waits=[tt_] + prev_yev)
        VTR = s_s.at()
        for q in range(4):
            bank = p_a if q % 2 == 0 else p_b
            def dmm(e, q=q, bank=bank):
                last = None
                for j in range(4):
                    cc = q * 4 + j
                    last = e.matmul(bank[:, j * 128:(j + 1) * 128],
                                    t_dw2[:, cc * 128:(cc + 1) * 128],
                                    t_tanhd[:, c0:c0 + LCH], start=True, stop=True)
                return last
            lastp = PD.get("g", PD.get("v"))
            dm = te(dmm, waits=([PD["w"], lastp] + prev_yev) if q < 2
                    else [s_s.at()])
            def dev(e, q=q, bank=bank):
                return e.activation(out=t_d[:, q * 4:(q + 1) * 4, :],
                                    in_=bank[:, 0:512], func=AF.Exp)
            sc(dev, waits=[dm] + (prev_yev if q == 0 else []))
        DEXP = s_s.at()
        def scans(e):
            e.memset(t_cs[:, :, 0:1], 0.0)
            last = None
            for cc in range(NCC):
                last = e.tensor_tensor_scan(out=t_cs[:, cc, 1:LCH + 1],
                                            data0=t_d[:, cc, :], data1=t_d[:, cc, :],
                                            initial=0.0, op0=ALU.add, op1=ALU.bypass)
            return last
        SCAN = ve(scans, waits=[DEXP, MEMSET] + (PD.get("lntap", [])
                                                 if c0 == 0 else []))
        def facts(e):
            e.activation(out=t_ptot[:, :],
                         in_=t_cs[:, :, LCH:LCH + 1]
                         .rearrange("p c one -> p (c one)"),
                         func=AF.Exp, scale=-1.0)
            if not sum_phase:
                e.activation(out=t_fr[:, :, :], in_=t_cs[:, :, 0:LCH], func=AF.Exp,
                             scale=-1.0)
            return e.activation(out=t_fk[:, :, :], in_=t_cs[:, :, 1:LCH + 1],
                                func=AF.Exp)
        FACT = sc(facts, waits=[SCAN])
        def velem(e):
            if not sum_phase:
                if c0 == 0:
                    e.memset(t_rtE[64:128, :, :], 0.0)
                    e.memset(t_rtO[0:64, :, :], 0.0)
                e.tensor_tensor(out=t_rtE[0:64, :, :],
                                in0=t_r[0:64, :, c0:c0 + LCH],
                                in1=t_fr[0:64, :, :], op=ALU.mult)
                e.tensor_tensor(out=t_rtO[64:128, :, :],
                                in0=t_r[64:128, :, c0:c0 + LCH],
                                in1=t_fr[64:128, :, :], op=ALU.mult)
            e.tensor_tensor(out=t_kt[:, :, :], in0=t_k[:, :, c0:c0 + LCH],
                            in1=t_fk[:, :, :], op=ALU.mult)
            for cc in range(NCC):
                e.tensor_scalar(out=t_kp[:, cc, :], in0=t_kt[:, cc, :],
                                scalar1=t_ptot[:, cc:cc + 1], scalar2=None,
                                op0=ALU.mult)
            return e.tensor_copy(out=t_sbf[:, :], in_=t_sall[:, :])
        wts = [FACT] + ([PD["r"], PD["k"]] if not sum_phase else [PD["k"]]) + prev_yev
        if c0 == 0:
            wts += PD.get("lntap", [])
        VEL = ve(velem, waits=wts)
        RKU = None
        if not sum_phase:
            def velem2(e):
                for cc in range(NCC):
                    e.tensor_scalar(out=t_rku[:, cc, :], in0=t_k[:, cc, c0:c0 + LCH],
                                    scalar1=t_u[:, cc:cc + 1], scalar2=None,
                                    op0=ALU.mult)
                return e.tensor_tensor(out=t_rku[:, :, :], in0=t_rku[:, :, :],
                                       in1=t_r[:, :, c0:c0 + LCH], op=ALU.mult)
            RKU = ve(velem2)
        for half in range(2):
            def ktf(e, half=half):
                last = None
                for j in range(8):
                    cc = half * 8 + j
                    last = e.transpose(p_m2[:, j * 128:(j + 1) * 128],
                                       t_kp[:, cc, :], t_id[:, :])
                return last
            tt_ = te(ktf, waits=[VEL, s_s.at()])
            def kev(e, half=half):
                return e.copy(out=t_ktr[:, half * 8:(half + 1) * 8, :],
                              in_=p_m2[:, 0:1024])
            sc(kev, waits=[tt_])
        KTR = s_s.at()
        sup_marks = []
        for part in range(2):
            pt = p_t1 if part == 0 else p_t2
            def tmm(e, part=part, pt=pt):
                last = None
                for pr in range(8):
                    pair = part * 8 + pr
                    for hh_ in range(2):
                        hof = hh_ * 64
                        last = e.matmul(pt[hof:hof + 64, pr * 64:(pr + 1) * 64],
                                        t_ktr[:, pair, hof:hof + 64],
                                        t_vtr[:, pair, hof:hof + 64],
                                        start=True, stop=True,
                                        tile_position=(0, hof))
                return last
            TMM = te(tmm, waits=[KTR] + (PD.get("gn", []) if part == 0 else []))
            def supd(e, part=part, pt=pt):
                last = None
                for pr in range(8):
                    pair = part * 8 + pr
                    last = e.scalar_tensor_tensor(
                        out=t_sall[:, pair * HD:(pair + 1) * HD],
                        in0=t_sall[:, pair * HD:(pair + 1) * HD],
                        scalar=t_ptot[:, pair:pair + 1],
                        in1=pt[:, pr * 64:(pr + 1) * 64],
                        op0=ALU.mult, op1=ALU.add)
                return last
            sup_marks.append(ve(supd, waits=[TMM]))
        SUPD = s_v.at()
        if sum_phase:
            return dict(supd=SUPD, yev=[SUPD])
        amk = []
        for rnd in range(8):
            bank = p_a if rnd % 2 == 0 else p_b
            def af(e, rnd=rnd, bank=bank):
                last = None
                for j in range(4):
                    head = rnd * 4 + j
                    pair, hh_ = head // 2, head % 2
                    rte = t_rtE if hh_ == 0 else t_rtO
                    last = e.matmul(bank[:, j * 128:(j + 1) * 128],
                                    t_kt[:, pair, :],
                                    rte[:, pair, :],
                                    start=True, stop=True)
                return last
            wts = [VEL]
            if rnd >= 2:
                wts.append(amk[rnd - 2])
            AMM_ = te(af, waits=wts)
            def amev(e, rnd=rnd, bank=bank):
                return e.tensor_tensor(out=t_am[:, rnd * 4:(rnd + 1) * 4, :],
                                       in0=bank[:, 0:512]
                                       .rearrange("p (c l) -> p c l", c=4),
                                       in1=t_mst[:, :]
                                       .rearrange("p (c l) -> p c l", c=4),
                                       op=ALU.mult)
            amk.append(ve(amev, waits=[AMM_]))
        AMDONE = s_v.at()
        def emm(e):
            last = None
            for cc in range(NCC):
                last = e.matmul(p_m1[0:32, 0:128], t_ect[:, cc, :], t_rku[:, cc, :],
                                start=(cc == 0), stop=(cc == NCC - 1))
            return last
        EMM = te(emm, waits=[RKU])
        RKH = ve(lambda e: e.tensor_copy(out=t_rkuhb[:, :], in_=p_m1[0:32, 0:128]),
                 waits=[EMM])
        last_grp_ev = {}
        for grp in range(4):
            py = p_y1 if grp % 2 == 0 else p_y2
            pu = p_a if grp % 2 == 0 else p_b
            def bcg(e, grp=grp, pu=pu):
                last = None
                for j in range(4):
                    cc = grp * 4 + j
                    last = e.matmul(pu[:, j * 128:(j + 1) * 128],
                                    t_ecttp[:, cc * 128:(cc + 1) * 128],
                                    t_rkuhb[:, :], start=True, stop=True)
                return last
            wts = [RKH, ECTTP, AMDONE]
            if grp >= 1:
                wts.append(last_grp_ev[grp - 1])
            BCG = te(bcg, waits=wts)
            def ydg(e, grp=grp, pu=pu):
                return e.tensor_tensor(out=t_ev[:, grp % 2, :]
                                       .rearrange("p (c l) -> p c l", c=4),
                                       in0=pu[:, 0:512]
                                       .rearrange("p (c l) -> p c l", c=4),
                                       in1=t_v[:, grp * 4:(grp + 1) * 4, c0:c0 + LCH],
                                       op=ALU.mult)
            YDG = ve(ydg, waits=[BCG, PD["v"]])
            gmk = []
            for pr in range(4):
                pair = grp * 4 + pr
                def ymm(e, pair=pair, pr=pr, py=py):
                    last = None
                    for hh_ in range(2):
                        hof = hh_ * 64
                        rte = t_rtE if hh_ == 0 else t_rtO
                        e.matmul(py[hof:hof + 64, pr * 128:(pr + 1) * 128],
                                 t_sbf[:, pair * HD:(pair + 1) * HD],
                                 rte[:, pair, :],
                                 start=True, stop=False,
                                 tile_position=(0, hof))
                        last = e.matmul(py[hof:hof + 64, pr * 128:(pr + 1) * 128],
                                        t_vtr[:, pair, hof:hof + 64],
                                        t_am[:, 2 * pair + hh_, :],
                                        start=False, stop=True,
                                        tile_position=(0, hof))
                    return last
                wts2 = [AMDONE, BCG]
                if grp >= 2 and pr == 0:
                    wts2.append(last_grp_ev[grp - 2])
                gmk.append(te(ymm, waits=wts2))
            def yev(e, grp=grp, py=py):
                return e.tensor_tensor(out=t_y[:, grp * 4:(grp + 1) * 4, c0:c0 + LCH],
                                       in0=py[:, 0:512]
                                       .rearrange("p (c l) -> p c l", c=4),
                                       in1=t_ev[:, grp % 2, :]
                                       .rearrange("p (c l) -> p c l", c=4),
                                       op=ALU.add)
            last_grp_ev[grp] = ve(yev, waits=[gmk[-1], YDG])
        return dict(supd=SUPD, yev=[last_grp_ev[3]])

    t_ecttp = SB("t_ecttp", (32, NCC * 128), BF16)
    d_ecttp = DI("ecttp", (32, NCC * 128), BF16)
    dma("sync", t_ecttp[:, :], d_ecttp[:, :])
    ECTTP = s_dx.at()

    def post_stage(toff, PD, YEV, first_use_o):
        for cc in range(NCC):
            def sqf(e, cc=cc):
                return e.activation(out=t_sqb[:, cc % 2, :], in_=t_y[:, cc, :],
                                    func=AF.Square)
            w = list(YEV) if cc < 2 else [s_t.at()]
            smk = sc(sqf, waits=w)
            def mmf(e, cc=cc):
                e.matmul(p_t1[0:32, 0:512], t_ect[:, cc, :], t_y[:, cc, :],
                         start=(cc == 0), stop=(cc == NCC - 1))
                return e.matmul(p_t2[0:32, 0:512], t_ect[:, cc, :],
                                t_sqb[:, cc % 2, :],
                                start=(cc == 0), stop=(cc == NCC - 1))
            te(mmf, waits=[smk] + (list(YEV) if cc == 0 else []))
        GN_MM = s_t.at()
        def gmath(e):
            e.tensor_scalar(out=t_ev[0:32, 0, :], in0=p_t1[0:32, 0:512],
                            scalar1=1.0 / HD, scalar2=None, op0=ALU.mult)
            e.tensor_tensor(out=t_ev[0:32, 1, :], in0=t_ev[0:32, 0, :],
                            in1=t_ev[0:32, 0, :], op=ALU.mult)
            return e.scalar_tensor_tensor(out=t_ev[0:32, 1, :],
                                          in0=p_t2[0:32, 0:512],
                                          scalar=1.0 / HD, in1=t_ev[0:32, 1, :],
                                          op0=ALU.mult, op1=ALU.subtract)
        gm = ve(gmath, waits=[GN_MM])
        def gaf(e):
            e.activation(out=t_ev[0:32, 1, :], in_=t_ev[0:32, 1, :],
                         func=AF.Ln, bias=t_cbias[0:32, 1:2])
            return e.activation(out=t_ev[0:32, 1, :], in_=t_ev[0:32, 1, :],
                                func=AF.Exp, scale=-0.5)
        ga_ = sc(gaf, waits=[gm])
        def grows(e):
            e.tensor_copy(out=t_rs33[0:32, :], in_=t_ev[0:32, 1, :])
            e.tensor_tensor(out=t_ev[0:32, 0, :], in0=t_ev[0:32, 0, :],
                            in1=t_ev[0:32, 1, :], op=ALU.mult)
            return e.tensor_copy(out=t_mu33[0:32, :], in_=t_ev[0:32, 0, :])
        gr = ve(grows, waits=[ga_])
        prev_mm = {}
        for cc in range(NCC):
            def bmm(e, cc=cc):
                e.matmul(p_m1[:, 0:512], t_egb[:, cc * 128:(cc + 1) * 128],
                         t_rs33[:, :], start=True, stop=True)
                return e.matmul(p_b[:, 0:512], t_egb[:, cc * 128:(cc + 1) * 128],
                                t_mu33[:, :], start=True, stop=True)
            wts = [gr, EGB] if cc == 0 else [prev_mm[cc - 1]]
            bm = te(bmm, waits=wts)
            def yng(e, cc=cc):
                e.tensor_tensor(out=t_ev[:, 0, :], in0=t_y[:, cc, :], in1=p_m1[:, 0:512],
                                op=ALU.mult)
                e.tensor_tensor(out=t_ev[:, 0, :], in0=t_ev[:, 0, :], in1=p_b[:, 0:512],
                                op=ALU.subtract)
                return e.tensor_tensor(out=t_y[:, cc, :], in0=t_ev[:, 0, :],
                                       in1=t_g[:, cc, :], op=ALU.mult)
            prev_mm[cc] = ve(yng, waits=[bm, PD["g"]])
        YNG = s_v.at()
        wrr = d_wproj["o"].ap().rearrange("(cc p) o -> p cc o", p=128)
        xj = {}
        x2w = {}
        ev_marks = []
        for pc in range(NCC):
            wts = list(wbuf_last_use.get(pc % 2, []))
            if first_use_o:
                src = wrr[:, :, pc * 128:(pc + 1) * 128]
            else:
                src = s_wc["o"][pc]
            piece = dma("gpsimd", t_wbuf[:, pc % 2, :, :], src,
                        waits=wts + ([] if first_use_o
                                     else [CACHE_DONE["o"]]))
            cw = None
            if first_use_o:
                cw = dma("sync", s_wc["o"][pc], t_wbuf[:, pc % 2, :, :],
                         waits=[piece])
            xj[pc] = dma("sync",
                         t_x32[:, pc % 2, :],
                         d_x.ap().rearrange("(cc p) t -> p cc t", p=128)
                         [:, pc, toff:toff + TT],
                         waits=[x2w[pc - 2]] if pc >= 2 else list(YEV))
            bank = p_a if pc % 2 == 0 else p_b
            def pmm(e, pc=pc, bank=bank):
                last = None
                for fi in range(NCC):
                    last = e.matmul(bank[:, 0:512], t_wbuf[:, pc % 2, fi, :],
                                    t_y[:, fi, :], start=(fi == 0),
                                    stop=(fi == NCC - 1))
                return last
            wts2 = [piece, YNG]
            if pc >= 2:
                wts2.append(ev_marks[pc - 2])
            mmdone = te(pmm, waits=wts2)
            wbuf_last_use[pc % 2] = [mmdone] + ([cw] if cw else [])
            def evf(e, pc=pc, bank=bank):
                return e.scalar_tensor_tensor(out=t_ev[:, pc % 2, :],
                                              in0=bank[:, 0:512],
                                              scalar=modc(2, pc),
                                              in1=t_x32[:, pc % 2, :],
                                              op0=ALU.mult, op1=ALU.add)
            evm = ve(evf, waits=[mmdone, xj[pc]])
            ev_marks.append(evm)
            x2w[pc] = dma("sync",
                          s_x2.ap().rearrange("(cc p) t -> p cc t", p=128)
                          [:, pc, toff:toff + TT],
                          t_ev[:, pc % 2, :], waits=[evm])
        if first_use_o:
            CACHE_DONE["o"] = s_dc.at()
        GN_VE = s_v.at()
        return [x2w[NCC - 1], x2w[NCC - 2]], GN_VE

    def finish_early():
        dm = dma("sync", d_out[0:128, 0:512], t_ev[:, 0, :], sem=s_o,
                 waits=[s_v.at(), s_t.at(), s_s.at(), s_g.at(), s_dw.at(),
                        s_dc.at()])
        b.emit("sync", lambda e: None,
               waits=[s_o.at(), s_dw.at(), s_dx.at(), s_dc.at()], incs=[])
        b.finish()
        return nc, TAP

    def attn_phase(xin, n_tiles, sum_phase, entry_tail, entry_yev):
        prev_tail = list(entry_tail)
        post_marks = []
        prev_yev = list(entry_yev)
        gn_mark = []
        for it in range(n_tiles):
            toff = it * TT
            marks = ln_h_stage(xin, toff, it == 0, sum_phase, prev_tail,
                               last_tile=(it == n_tiles - 1))
            ATANH = maa_stage(marks)
            PD = {"gn": gn_mark, "lntap": marks["lntap"]}
            XMw = xm_build(0, ATANH)
            if it == n_tiles - 1 and not sum_phase and "a1" in TAP:
                PD["lntap"].append(dma("sync", TAP["a1"].ap(), t_a1[0:96, :],
                                       sem=s_o, waits=[ATANH]))
            if it == n_tiles - 1 and not sum_phase and "xmw" in TAP:
                PD["lntap"].append(dma("sync", TAP["xmw"].ap(), t_xm[:, :, :],
                                       sem=s_o, waits=[XMw]))
            PD["w"] = decay_proj(XMw)
            XMk = xm_build(1, ATANH, [PD["w"]])
            first_kv = sum_phase and it == 0
            PD["k"] = full_proj("k", t_k, XMk, first_use=first_kv)
            XMv = xm_build(2, ATANH, [PD["k"]])
            PD["v"] = full_proj("v", t_v, XMv, first_use=first_kv)
            if not sum_phase:
                first_rg = it == 0
                XMr = xm_build(3, ATANH, [PD["v"]])
                PD["r"] = full_proj("r", t_r, XMr, first_use=first_rg)
                XMg = xm_build(4, ATANH, [PD["r"]])
                if it == n_tiles - 1 and "xmg" in TAP:
                    PD["lntap"].append(dma("sync", TAP["xmg"].ap(),
                                           t_xm[:, :, :], sem=s_o, waits=[XMg]))
                PD["g"] = full_proj("g", t_g, XMg, silu=True, first_use=first_rg)
            if not sum_phase and stop_after == "mproj":
                return post_marks, prev_tail, prev_yev
            chunk_marks = None
            for ch in range(NCH):
                chunk_marks = wkv_chunk(ch * LCH, sum_phase, PD, prev_yev)
                prev_yev = chunk_marks["yev"]
            if not sum_phase and stop_after == "mwkv":
                return post_marks, prev_tail, prev_yev
            if sum_phase:
                prev_tail = [chunk_marks["supd"]]
                if it == n_tiles - 1:
                    def smsk(e):
                        return e.tensor_scalar(out=t_sall[:, :], in0=t_sall[:, :],
                                               scalar1=t_smask[:, 0:1],
                                               scalar2=None, op0=ALU.mult)
                    SM = ve(smsk, waits=[chunk_marks["supd"]])
                    prev_tail = [SM]
                    prev_yev = [SM]
            else:
                pm, gnv = post_stage(toff, PD, chunk_marks["yev"],
                                     first_use_o=(it == 0))
                gn_mark = [gnv]
                post_marks.append(pm)
                prev_tail = [chunk_marks["supd"], pm[0]]
                if it == n_tiles - 1:
                    for nm, buf in [("r", t_r),
                                    ("k", t_k), ("v", t_v), ("g", t_g),
                                    ("tanhd", t_tanhd),
                                    ("d", t_d), ("cs", t_cs), ("am", t_am),
                                    ("sall", t_sall), ("y", t_y)]:
                        if nm in TAP:
                            if nm == "h":
                                src = buf[:, :, 1:TT + 1]
                            elif nm == "sall":
                                src = buf[:, :]
                            elif nm == "tanhd":
                                src = buf[0:65, :]
                            else:
                                src = buf[:, :, :]
                            wq_ = [s_v.at(), s_s.at(), s_t.at()]
                            dma("sync", TAP[nm].ap(), src, sem=s_o, waits=wq_)
            emit_wmean(48)
        return post_marks, prev_tail, prev_yev

    if stop_after == "p1":
        return finish_early()
    entry_t, entry_y = [MOD_DONE], []
    if n_sum_tiles > 0:
        _, entry_t, entry_y = attn_phase(d_xs, n_sum_tiles, True,
                                         entry_t, entry_y)
    if stop_after == "sum":
        return finish_early()
    post_marks, _, _ = attn_phase(d_x, n_main_tiles, False, entry_t, entry_y)
    if stop_after in ("attn", "mproj", "mwkv"):
        return finish_early()
    QS_MARK, AW_MARK = wmean_finalize()

    def ffn_b0():
        B0_last = []
        for sl in range(NSL):
            t0 = sl * 512
            x2r = s_x2.ap().rearrange("(cc p) t -> p cc t", p=128)[:, :, t0:t0 + 512]
            if B0_last:
                w0 = list(B0_last)
            else:
                w0 = list(post_marks[-1]) + [s_v.at(), s_s.at(), s_t.at(),
                                             s_g.at(), s_o.at()]
            XL32 = dma("sync", t_x2f[:, :, :], x2r, waits=w0)
            for cc in range(NCC):
                def sqf(e, cc=cc):
                    return e.activation(out=t_ev[:, cc % 2, :], in_=t_x2f[:, cc, :],
                                        func=AF.Square)
                w = [XL32] if cc < 2 else [s_t.at()]
                smk = sc(sqf, waits=w)
                def mmf(e, cc=cc):
                    e.matmul(p_m1[0:1, 0:512], t_onesf[:, :], t_x2f[:, cc, :],
                             start=(cc == 0), stop=(cc == NCC - 1))
                    return e.matmul(p_a[0:1, 0:512], t_onesf[:, :],
                                    t_ev[:, cc % 2, :],
                                    start=(cc == 0), stop=(cc == NCC - 1))
                te(mmf, waits=[smk] + ([XL32] if cc == 0 else []))
            ST2 = s_t.at()
            def sm2(e):
                e.tensor_scalar(out=t_row[0:1, 512:1024], in0=p_m1[0:1, 0:512],
                                scalar1=1.0 / C, scalar2=None, op0=ALU.mult)
                e.tensor_tensor(out=t_row[0:1, 1024:1536], in0=t_row[0:1, 512:1024],
                                in1=t_row[0:1, 512:1024], op=ALU.mult)
                return e.scalar_tensor_tensor(out=t_row[0:1, 0:512],
                                              in0=p_a[0:1, 0:512],
                                              scalar=1.0 / C,
                                              in1=t_row[0:1, 1024:1536],
                                              op0=ALU.mult, op1=ALU.subtract)
            s2 = ve(sm2, waits=[ST2])
            def r2f(e):
                e.activation(out=t_row[0:1, 0:512], in_=t_row[0:1, 0:512],
                             func=AF.Ln, bias=t_cbias[0:1, 0:1])
                return e.activation(out=t_row[0:1, 0:512], in_=t_row[0:1, 0:512],
                                    func=AF.Exp, scale=-0.5)
            r2 = sc(r2f, waits=[s2])
            def bc2(e):
                e.matmul(p_m1[:, 0:512], t_onesrf[:, :], t_row[0:1, 0:512],
                         start=True, stop=True)
                return e.matmul(p_a[:, 0:512], t_onesrf[:, :], t_row[0:1, 512:1024],
                                start=True, stop=True)
            b2 = te(bc2, waits=[r2])
            def zmx(e):
                e.memset(t_x32[:, :, :], 0.0)
                return e.memset(t_sqb[:, :, :], 0.0)
            zm = ve(zmx)
            for cc in range(NCC):
                def h2f(e, cc=cc):
                    e.tensor_tensor(out=t_ev[:, 0, :], in0=t_x2f[:, cc, :],
                                    in1=p_a[:, 0:512], op=ALU.subtract)
                    e.tensor_tensor(out=t_ev[:, 0, :], in0=t_ev[:, 0, :],
                                    in1=p_m1[:, 0:512], op=ALU.mult)
                    e.tensor_scalar(out=t_x2f[:, cc, :], in0=t_ev[:, 0, :],
                                    scalar1=modc(4, cc), scalar2=modc(3, cc),
                                    op0=ALU.mult, op1=ALU.add)
                    e.tensor_scalar(out=t_ev[:, 1, :], in0=t_x2f[:, cc, :],
                                    scalar1=pfc("g1a", cc), scalar2=None,
                                    op0=ALU.mult)
                    e.tensor_tensor(out=t_x32[:, 0, :], in0=t_ev[:, 1, :],
                                    in1=t_x32[:, 0, :], op=ALU.max)
                    e.tensor_tensor(out=t_sqb[:, 0, :], in0=t_ev[:, 1, :],
                                    in1=t_sqb[:, 0, :], op=ALU.min)
                    e.tensor_scalar(out=t_ev[:, 1, :], in0=t_x2f[:, cc, :],
                                    scalar1=pfc("g2a", cc), scalar2=None,
                                    op0=ALU.mult)
                    e.tensor_tensor(out=t_x32[:, 1, :], in0=t_ev[:, 1, :],
                                    in1=t_x32[:, 1, :], op=ALU.max)
                    return e.tensor_tensor(out=t_sqb[:, 1, :], in0=t_ev[:, 1, :],
                                           in1=t_sqb[:, 1, :], op=ALU.min)
                ve(h2f, waits=[b2, zm, MOD_DONE, G1A] if cc == 0 else [])
            H2 = s_v.at()
            for cc in range(NCC):
                def sq2(e, cc=cc):
                    return e.activation(out=t_ev[:, cc % 2, :], in_=t_x2f[:, cc, :],
                                        func=AF.Square)
                w = [H2] if cc < 2 else [s_t.at()]
                sk = sc(sq2, waits=w)
                def mm2(e, cc=cc):
                    return e.matmul(p_b[0:1, 0:512], t_onesf[:, :],
                                    t_ev[:, cc % 2, :],
                                    start=(cc == 0), stop=(cc == NCC - 1))
                te(mm2, waits=[sk])
            RMS_MM = s_t.at()
            TRM = {}
            for q in range(2):
                def cmf(e, q=q):
                    e.tensor_scalar(out=t_a1[:, :], in0=t_sqb[:, q, :],
                                    scalar1=-1.0, scalar2=None, op0=ALU.mult)
                    return e.tensor_tensor(out=t_sqb[:, q, :],
                                           in0=t_x32[:, q, :],
                                           in1=t_a1[:, 0:512], op=ALU.max)
                cm = ve(cmf, waits=[H2])
                trs = []
                for tb in range(4):
                    def trf(e, q=q, tb=tb):
                        return e.transpose(p_m2[:, tb * 128:(tb + 1) * 128],
                                           t_sqb[:, q, tb * 128:(tb + 1) * 128],
                                           t_id[:, :])
                    trs.append(te(trf, waits=[cm]))
                def rmx(e, q=q):
                    last = None
                    for tb in range(4):
                        last = e.reduce_max(out=t_small[:, 48 + tb:49 + tb],
                                            in_=p_m2[:, tb * 128:(tb + 1) * 128],
                                            axis=AX)
                    return last
                rm = ve(rmx, waits=[trs[-1]])
                cpk = ve(lambda e: e.tensor_copy(out=t_sqb[:, 0, 0:4],
                                                 in_=t_small[:, 48:52]),
                         waits=[rm])
                def tb2f(e):
                    last = None
                    for g in range(4):
                        last = e.matmul(p_m1[0:1, g * 128:(g + 1) * 128],
                                        t_sqb[:, 0, g:g + 1], t_id[:, :],
                                        start=True, stop=True)
                    return last
                tb2 = te(tb2f, waits=[cpk])
                def rrow(e, q=q):
                    return e.tensor_copy(
                        out=t_row[0:1, (1 + q) * 512:(2 + q) * 512],
                        in_=p_m1[0:1, 0:512])
                TRM[q] = ve(rrow, waits=[tb2])
            qv = ve(lambda e: e.tensor_scalar(out=t_row[0:1, 0:512],
                                              in0=p_b[0:1, 0:512],
                                              scalar1=1.0 / C, scalar2=None,
                                              op0=ALU.mult), waits=[RMS_MM])
            def qaf(e):
                e.activation(out=t_row[0:1, 0:512], in_=t_row[0:1, 0:512],
                             func=AF.Ln, bias=t_cbias[0:1, 2:3])
                return e.activation(out=t_row[0:1, 0:512], in_=t_row[0:1, 0:512],
                                    func=AF.Exp, scale=-0.5)
            qa = sc(qaf, waits=[qv])
            def srow(e, t0=t0):
                last = None
                for q in range(2):
                    e.tensor_tensor(out=t_row[0:1, (1 + q) * 512:(2 + q) * 512],
                                    in0=t_row[0:1, (1 + q) * 512:(2 + q) * 512],
                                    in1=t_row[0:1, 0:512], op=ALU.mult)
                    e.tensor_scalar(out=t_row[0:1, (1 + q) * 512:(2 + q) * 512],
                                    in0=t_row[0:1, (1 + q) * 512:(2 + q) * 512],
                                    scalar1=QCLIP, scalar2=None, op0=ALU.max)
                    e.tensor_scalar(out=(t_inv1 if q == 0 else t_inv2)
                                    [0:1, t0:t0 + 512],
                                    in0=t_row[0:1, (1 + q) * 512:(2 + q) * 512],
                                    scalar1=t_small[0:1, 69 + q:70 + q],
                                    scalar2=-1.0 / 127.0,
                                    op0=ALU.mult, op1=ALU.mult)
                    e.reciprocal(out=t_row[0:1, (1 + q) * 512:(2 + q) * 512],
                                 in_=t_row[0:1, (1 + q) * 512:(2 + q) * 512])
                    e.tensor_tensor(out=t_row[0:1, (1 + q) * 512:(2 + q) * 512],
                                    in0=t_row[0:1, 0:512],
                                    in1=t_row[0:1, (1 + q) * 512:(2 + q) * 512],
                                    op=ALU.mult)
                    last = e.tensor_scalar(
                        out=t_row[0:1, (1 + q) * 512:(2 + q) * 512],
                        in0=t_row[0:1, (1 + q) * 512:(2 + q) * 512],
                        scalar1=127.0, scalar2=None, op0=ALU.mult)
                return last
            sr = ve(srow, waits=[qa, TRM[0], TRM[1], AW_MARK])
            def bq(e):
                e.matmul(p_m1[:, 0:512], t_onesrf[:, :], t_row[0:1, 512:1024],
                         start=True, stop=True)
                return e.matmul(p_a[:, 0:512], t_onesrf[:, :],
                                t_row[0:1, 1024:1536], start=True, stop=True)
            bq_ = te(bq, waits=[sr])
            last_xq = None
            for cc in range(NCC):
                def xqf(e, cc=cc, sl=sl, t0=t0):
                    e.scalar_tensor_tensor(out=t_ev[:, 0, :], in0=t_x2f[:, cc, :],
                                           scalar=pfc("g1", cc), in1=p_m1[:, 0:512],
                                           op0=ALU.mult, op1=ALU.mult)
                    e.tensor_scalar(out=t_xq1[:, cc, t0:t0 + 512],
                                    in0=t_ev[:, 0, :], scalar1=RND_C, scalar2=RND_C,
                                    op0=ALU.add, op1=ALU.subtract)
                    e.scalar_tensor_tensor(out=t_ev[:, 0, :], in0=t_x2f[:, cc, :],
                                           scalar=pfc("g2", cc), in1=p_a[:, 0:512],
                                           op0=ALU.mult, op1=ALU.mult)
                    return e.tensor_scalar(out=t_xq2[:, cc, t0:t0 + 512],
                                           in0=t_ev[:, 0, :], scalar1=RND_C,
                                           scalar2=RND_C,
                                           op0=ALU.add, op1=ALU.subtract)
                last_xq = ve(xqf, waits=[bq_] if cc == 0 else [])
            B0_last = [last_xq]
        return s_v.at()

    if stop_after == "wm":
        return finish_early()
    B0_DONE = ffn_b0()
    if stop_after == "b0":
        return finish_early()
    if "cells2" in TAP:
        dma("sync", TAP["cells2"].ap(), t_small[0:1, 48:72], sem=s_o,
            waits=[B0_DONE])
    if "xq1" in TAP:
        dma("sync", TAP["xq1"].ap(), t_xq1[:, :, :], sem=s_o, waits=[B0_DONE])
    if "xq2" in TAP:
        dma("sync", TAP["xq2"].ap(), t_xq2[:, :, :], sem=s_o, waits=[B0_DONE])
    if "inv1" in TAP:
        dma("sync", TAP["inv1"].ap(), t_inv1[0:1, 0:T_MAIN], sem=s_o,
            waits=[B0_DONE])
    if "inv2" in TAP:
        dma("sync", TAP["inv2"].ap(), t_inv2[0:1, 0:T_MAIN], sem=s_o,
            waits=[B0_DONE])

    def ffn_y12():
        bm = []
        for q in range(NQ):
            def bcf(e, q=q):
                e.matmul(p_m1[:, 0:512], t_onesr[:, :],
                         t_inv1[0:1, q * 512:(q + 1) * 512], start=True, stop=True)
                return e.matmul(p_a[:, 0:512], t_onesr[:, :],
                                t_inv2[0:1, q * 512:(q + 1) * 512],
                                start=True, stop=True)
            m1 = te(bcf, waits=[B0_DONE] if q == 0 else [bm[-1]])
            def bev(e, q=q):
                e.tensor_copy(out=t_bc1[:, q * 512:(q + 1) * 512], in_=p_m1[:, 0:512])
                return e.tensor_copy(out=t_bc2[:, q * 512:(q + 1) * 512],
                                     in_=p_a[:, 0:512])
            bm.append(ve(bev, waits=[m1]))
        BC12 = bm[-1]
        ZM3 = ve(lambda e: e.memset(t_m3[:, :], 0.0))

        q1_done = {}
        q2_done = {}
        mm_last = {}
        hmw = {}
        sqm = {}
        m3m = {}
        prev_e1 = {0: None, 1: None}
        prev_e2 = {0: None, 1: None}
        prev_hm2 = {0: None, 1: None}
        prev_sl = None
        prev_hm = None
        prev_m3 = None
        prev_sq = None
        prev_ssmm = None
        prev_hmdma = None
        prev_m3sq = None
        prev_m3max = None
        w1r = d_w1.ap().rearrange("(cc p) m -> p cc m", p=128)
        w2r = d_w2.ap().rearrange("(cc p) m -> p cc m", p=128)
        for blk in range(NOC):
            wts1 = [q1_done[blk - 1], q2_done[blk - 1]] if blk \
                else [QS_MARK, B0_DONE]
            l1 = dma("gpsimd", t_wf32[:, :, :],
                     w1r[:, :, blk * 128:(blk + 1) * 128], waits=wts1)
            def qa1(e, blk=blk):
                return e.tensor_scalar(out=t_wq[:, 0, :, :], in0=t_wf32[:, :, :],
                                       scalar1=t_qs[:, 0:1], scalar2=None,
                                       op0=ALU.is_ge)
            m = ve(qa1, waits=[l1] + ([mm_last[blk - 1]] if blk else []))
            def qb1(e, blk=blk):
                return e.scalar_tensor_tensor(out=t_wq[:, 0, :, :],
                                              in0=t_wf32[:, :, :],
                                              scalar=t_qs[:, 3:4],
                                              in1=t_wq[:, 0, :, :],
                                              op0=ALU.is_le, op1=ALU.subtract)
            q1_done[blk] = ve(qb1, waits=[m])
            l2 = dma("gpsimd", t_wf32[:, :, :],
                     w2r[:, :, blk * 128:(blk + 1) * 128], waits=[q1_done[blk]])
            def qa2(e, blk=blk):
                return e.tensor_scalar(out=t_wq[:, 1, :, :], in0=t_wf32[:, :, :],
                                       scalar1=t_qs[:, 1:2], scalar2=None,
                                       op0=ALU.is_ge)
            m2_ = ve(qa2, waits=[l2] + ([mm_last[blk - 1]] if blk else []))
            def qb2(e, blk=blk):
                return e.scalar_tensor_tensor(out=t_wq[:, 1, :, :],
                                              in0=t_wf32[:, :, :],
                                              scalar=t_qs[:, 4:5],
                                              in1=t_wq[:, 1, :, :],
                                              op0=ALU.is_le, op1=ALU.subtract)
            q2_done[blk] = ve(qb2, waits=[m2_])
            for q in range(NQ):
                py = p_y1 if q % 2 == 0 else p_y2
                pz = p_a if q % 2 == 0 else p_b
                def mmy1(e, q=q, py=py):
                    last = None
                    for ic in range(NCC):
                        last = e.matmul(py[:, 0:512], t_wq[:, 0, ic, :],
                                        t_xq1[:, ic, q * 512:(q + 1) * 512],
                                        start=(ic == 0), stop=(ic == NCC - 1))
                    return last
                wA = [q1_done[blk]]
                if blk and q < 2:
                    wA.append(sqm[(blk - 1, NQ - 1)])
                if prev_e1[q % 2] is not None:
                    wA.append(prev_e1[q % 2])
                mA = te(mmy1, waits=wA)
                def mmy2(e, q=q, pz=pz):
                    last = None
                    for ic in range(NCC):
                        last = e.matmul(pz[:, 0:512], t_wq[:, 1, ic, :],
                                        t_xq2[:, ic, q * 512:(q + 1) * 512],
                                        start=(ic == 0), stop=(ic == NCC - 1))
                    return last
                wB = [q2_done[blk]]
                if prev_e2[q % 2] is not None:
                    wB.append(prev_e2[q % 2])
                mB = te(mmy2, waits=wB)
                mm_last[blk] = mB
                def e1f(e, q=q, py=py):
                    return e.tensor_tensor(out=t_ev[:, 0, :], in0=py[:, 0:512],
                                           in1=t_bc1[:, q * 512:(q + 1) * 512],
                                           op=ALU.mult)
                wE1 = [mA, BC12]
                if prev_sl is not None:
                    wE1.append(prev_sl)
                if prev_m3 is not None:
                    wE1.append(prev_m3)
                E1 = ve(e1f, waits=wE1)
                wSL = [E1]
                if prev_hm is not None:
                    wSL.append(prev_hm)
                SL = sc(lambda e: e.activation(out=t_ev[:, 1, :], in_=t_ev[:, 0, :],
                                               func=AF.Silu), waits=wSL)
                prev_sl = SL
                def e2f(e, q=q, pz=pz):
                    return e.tensor_tensor(out=t_x32[:, q % 2, :], in0=pz[:, 0:512],
                                           in1=t_bc2[:, q * 512:(q + 1) * 512],
                                           op=ALU.mult)
                wE2 = [mB, BC12]
                if prev_hm2[q % 2] is not None:
                    wE2.append(prev_hm2[q % 2])
                E2 = ve(e2f, waits=wE2)
                prev_e1[q % 2] = E1
                prev_e2[q % 2] = E2
                def hmf(e, q=q, blk=blk):
                    return e.tensor_tensor(out=t_hmst[:, :], in0=t_ev[:, 1, :],
                                           in1=t_x32[:, q % 2, :], op=ALU.mult)
                wH = [SL, E2]
                if prev_hmdma is not None:
                    wH.append(prev_hmdma)
                if prev_sq is not None:
                    wH.append(prev_sq)
                if prev_m3 is not None:
                    wH.append(prev_m3)
                HM = ge(hmf, waits=wH)
                prev_hm = HM
                prev_hm2[q % 2] = HM
                def sqf(e, q=q):
                    return e.activation(out=t_sqst[:, :], in_=t_hmst[:, :],
                                        func=AF.Square)
                SQ = sc(sqf, waits=[HM] + ([prev_ssmm] if prev_ssmm else []))
                prev_sq = SQ
                def ssmm(e, q=q, blk=blk):
                    pt = p_t1 if q < 2 else p_t2
                    r0 = (q % 2) * 32
                    return e.matmul(pt[r0:r0 + 1, 0:512], t_ones[:, :],
                                    t_sqst[:, :], start=(blk == 0),
                                    stop=(blk == NOC - 1), tile_position=(0, r0))
                sqm[(blk, q)] = te(ssmm, waits=[SQ])
                prev_ssmm = sqm[(blk, q)]
                M3A = ge(lambda e, blk=blk: e.tensor_scalar(
                    out=t_sqb[:, 1, :], in0=t_hmst[:, :],
                    scalar1=t_g3a[:, blk:blk + 1], scalar2=None, op0=ALU.mult),
                    waits=[HM, G3A] + ([prev_m3sq] if prev_m3sq else []))
                M3B = sc(lambda e: e.activation(out=t_a1[:, :], in_=t_sqb[:, 1, :],
                                                func=AF.Square),
                         waits=[M3A] + ([prev_m3max] if prev_m3max else []))
                prev_m3sq = M3B
                def m3f(e, q=q):
                    return e.tensor_tensor(out=t_m3[:, q * 512:(q + 1) * 512],
                                           in0=t_a1[:, 0:512],
                                           in1=t_m3[:, q * 512:(q + 1) * 512],
                                           op=ALU.max)
                m3m[(blk, q)] = ve(m3f, waits=[M3B, ZM3])
                prev_m3max = m3m[(blk, q)]
                prev_m3 = m3m[(blk, q)]
                prev_hmdma = dma("sync",
                                 s_hm[blk * 128:(blk + 1) * 128,
                                      q * 512:(q + 1) * 512],
                                 t_hmst[:, :], waits=[HM])
                hmw[(blk, q)] = prev_hmdma
        return hmw[(NOC - 1, NQ - 1)], sqm, m3m

    HMW_LAST, sqm_marks, m3m_marks = ffn_y12()
    if stop_after == "y12":
        return finish_early()

    def ffn_y3_prep():
        def ssrow(e):
            last = None
            for q in range(NQ):
                pt = p_t1 if q < 2 else p_t2
                r0 = (q % 2) * 32
                last = e.tensor_copy(out=t_s3row[0:1, q * 512:(q + 1) * 512],
                                     in_=pt[r0:r0 + 1, 0:512])
            return last
        SS = ve(ssrow, waits=[sqm_marks[(NOC - 1, q)] for q in range(NQ)])
        def rs3f(e):
            e.activation(out=t_s3row[0:1, 0:T_MAIN], in_=t_s3row[0:1, 0:T_MAIN],
                         func=AF.Ln, bias=t_cbias[0:1, 2:3], scale=1.0 / DFF)
            return e.activation(out=t_s3row[0:1, 0:T_MAIN],
                                in_=t_s3row[0:1, 0:T_MAIN],
                                func=AF.Exp, scale=-0.5)
        RS3 = sc(rs3f, waits=[SS])
        trm = []
        for g2_ in range(T_MAIN // 512):
            cm = ve(lambda e, g2_=g2_: e.tensor_copy(out=t_sqb[:, 0, :],
                                                     in_=t_m3[:, g2_ * 512:
                                                              (g2_ + 1) * 512]),
                    waits=[m3m_marks[(NOC - 1, g2_)]] + ([HMW_LAST] if g2_ == 0 else []))
            trs = []
            for tb in range(4):
                def trf(e, tb=tb):
                    return e.transpose(p_m2[:, tb * 128:(tb + 1) * 128],
                                       t_sqb[:, 0, tb * 128:(tb + 1) * 128],
                                       t_id[:, :])
                trs.append(te(trf, waits=[cm]))
            def rmx(e):
                last = None
                for tb in range(4):
                    last = e.reduce_max(out=t_small[:, 48 + tb:49 + tb],
                                        in_=p_m2[:, tb * 128:(tb + 1) * 128],
                                        axis=AX)
                return last
            rm = ve(rmx, waits=[trs[-1]])
            cpk = ve(lambda e: e.tensor_copy(out=t_sqb[:, 0, 0:4],
                                             in_=t_small[:, 48:52]), waits=[rm])
            def tb2f(e):
                last = None
                for g in range(4):
                    last = e.matmul(p_m1[0:1, g * 128:(g + 1) * 128],
                                    t_sqb[:, 0, g:g + 1], t_id[:, :],
                                    start=True, stop=True)
                return last
            tb2 = te(tb2f, waits=[cpk])
            def rrow(e, g2_=g2_):
                return e.tensor_copy(out=t_mx3row[0:1, g2_ * 512:(g2_ + 1) * 512],
                                     in_=p_m1[0:1, 0:512])
            trm.append(ve(rrow, waits=[tb2]))
        m3tap = []
        if "m3" in TAP:
            m3tap.append(dma("sync", TAP["m3"].ap(), t_m3[:, 0:T_MAIN],
                             sem=s_o, waits=trm))
        SQRT3 = sc(lambda e: e.activation(out=t_mx3row[0:1, 0:T_MAIN],
                                          in_=t_mx3row[0:1, 0:T_MAIN],
                                          func=AF.Sqrt), waits=trm)
        def r3(e):
            e.tensor_tensor(out=t_mx3row[0:1, 0:T_MAIN], in0=t_mx3row[0:1, 0:T_MAIN],
                            in1=t_s3row[0:1, 0:T_MAIN], op=ALU.mult)
            e.tensor_scalar(out=t_mx3row[0:1, 0:T_MAIN], in0=t_mx3row[0:1, 0:T_MAIN],
                            scalar1=QCLIP, scalar2=None, op0=ALU.max)
            e.tensor_scalar(out=t_inv3[0:1, 0:T_MAIN], in0=t_mx3row[0:1, 0:T_MAIN],
                            scalar1=t_small[0:1, 71:72], scalar2=-1.0 / 127.0,
                            op0=ALU.mult, op1=ALU.mult)
            last = None
            for qq in range(T_MAIN // 512):
                sl_ = slice(qq * 512, (qq + 1) * 512)
                e.tensor_copy(out=t_row[0:1, 0:512], in_=t_mx3row[0:1, sl_])
                e.reciprocal(out=t_row[0:1, 0:512], in_=t_row[0:1, 0:512])
                e.tensor_tensor(out=t_row[0:1, 512:1024],
                                in0=t_row[0:1, 0:512],
                                in1=t_s3row[0:1, sl_], op=ALU.mult)
                last = e.tensor_scalar(out=t_s3row[0:1, sl_],
                                       in0=t_row[0:1, 512:1024],
                                       scalar1=127.0, scalar2=None, op0=ALU.mult)
            return last
        return ve(r3, waits=trm + [RS3, AW_MARK, SQRT3] + m3tap)
    ROWS3 = ffn_y3_prep()
    if stop_after == "prep":
        return finish_early()
    y3tapm = []
    for nm, buf in [("s3row", t_s3row), ("mx3row", t_mx3row),
                    ("inv3", t_inv3)]:
        if nm in TAP:
            y3tapm.append(dma("sync", TAP[nm].ap(),
                              buf[0:1, 0:T_MAIN] if nm != "m3"
                              else buf[:, 0:T_MAIN],
                              sem=s_o, waits=[ROWS3]))

    def ffn_y3():
        w3r = d_w3.ap().rearrange("(dc p) c -> p dc c", p=128)
        out_r = d_out.ap().rearrange("(cc p) t -> p cc t", p=128)
        x2rr = s_x2.ap().rearrange("(cc p) t -> p cc t", p=128)
        hmr = s_hm.ap().rearrange("(oc p) t -> p oc t", p=128)
        prev_pass = [ROWS3] + y3tapm
        final_marks = []
        q_last = {}
        prev_evm = {}
        prev_q3 = None
        xq3tapm = []
        for ps in range(Y3P):
            tp0 = ps * Y3T
            NQP = Y3T // 512
            bcm = []
            for q in range(NQP):
                def bcf2(e, q=q, tp0=tp0):
                    return e.matmul(p_m1[:, 0:512], t_onesr[:, :],
                                    t_s3row[0:1, tp0 + q * 512:tp0 + (q + 1) * 512],
                                    start=True, stop=True)
                m1 = te(bcf2, waits=prev_pass if q == 0 else [bcm[-1]])
                def bcf3(e, q=q, tp0=tp0):
                    return e.matmul(p_a[:, 0:512], t_onesr[:, :],
                                    t_inv3[0:1, tp0 + q * 512:tp0 + (q + 1) * 512],
                                    start=True, stop=True)
                m2_ = te(bcf3)
                def bev(e, q=q):
                    e.tensor_copy(out=t_q3bc[:, q * 512:(q + 1) * 512],
                                  in_=p_m1[:, 0:512])
                    return e.tensor_copy(out=t_bc3[:, q * 512:(q + 1) * 512],
                                         in_=p_a[:, 0:512])
                bcm.append(ve(bev, waits=[m2_]))
            BCP = bcm[-1]
            xl = dma("gpsimd", t_xq3[:, :, 0:Y3T], hmr[:, :, tp0:tp0 + Y3T],
                     waits=prev_pass + xq3tapm)
            xq_marks = []
            for oc in range(NOC):
                def xq3f(e, oc=oc):
                    e.scalar_tensor_tensor(out=t_xq3[:, oc, 0:Y3T],
                                           in0=t_xq3[:, oc, 0:Y3T],
                                           scalar=t_g3[:, oc:oc + 1],
                                           in1=t_q3bc[:, 0:Y3T],
                                           op0=ALU.mult, op1=ALU.mult)
                    return e.tensor_scalar(out=t_xq3[:, oc, 0:Y3T],
                                           in0=t_xq3[:, oc, 0:Y3T],
                                           scalar1=RND_C, scalar2=RND_C,
                                           op0=ALU.add, op1=ALU.subtract)
                xq_marks.append(ve(xq3f, waits=[xl, BCP] if oc == 0 else []))
            XQ3 = xq_marks[-1]
            if ps == 0 and "xq3" in TAP:
                xq3tapm.append(dma("sync", TAP["xq3"].ap(), t_xq3[:, :, :],
                                   sem=s_o, waits=[XQ3]))
            for pc in range(NCC):
                mm_chain = []
                for sub in range(4):
                    pi = pc * 4 + sub
                    slot = pi % 2
                    wts = list(q_last[slot]) if slot in q_last else [ROWS3]
                    if ps == 0:
                        wld = wts + ([prev_q3] if prev_q3 is not None else [])
                        ld = dma("gpsimd", t_wf32[:, :, :],
                                 w3r[:, sub * 16:(sub + 1) * 16,
                                     pc * 128:(pc + 1) * 128], waits=wld)
                        def qa3(e, slot=slot):
                            return e.tensor_scalar(out=t_wq[:, slot, :, :],
                                                   in0=t_wf32[:, :, :],
                                                   scalar1=t_qs[:, 2:3],
                                                   scalar2=None, op0=ALU.is_ge)
                        mq = ve(qa3, waits=[ld])
                        def qb3(e, slot=slot):
                            return e.scalar_tensor_tensor(out=t_wq[:, slot, :, :],
                                                          in0=t_wf32[:, :, :],
                                                          scalar=t_qs[:, 5:6],
                                                          in1=t_wq[:, slot, :, :],
                                                          op0=ALU.is_le,
                                                          op1=ALU.subtract)
                        QD = ve(qb3, waits=[mq])
                        prev_q3 = QD
                        cw = dma("sync", s_w3q[pi], t_wq[:, slot, :, :],
                                 waits=[QD])
                        ready = QD
                    else:
                        ld = dma("gpsimd", t_wq[:, slot, :, :], s_w3q[pi],
                                 waits=wts + [CACHE3_DONE])
                        ready = ld
                    for q in range(NQP):
                        py = p_y1 if q % 2 == 0 else p_y2
                        def mmy3(e, sub=sub, slot=slot, q=q, py=py):
                            last = None
                            for dcc in range(16):
                                last = e.matmul(
                                    py[:, 0:512], t_wq[:, slot, dcc, :],
                                    t_xq3[:, sub * 16 + dcc, q * 512:(q + 1) * 512],
                                    start=(sub == 0 and dcc == 0),
                                    stop=(sub == 3 and dcc == 15))
                            return last
                        wm = [ready, XQ3]
                        if sub == 0 and prev_evm.get(q) is not None:
                            wm.append(prev_evm[q])
                        mm_chain.append(te(mmy3, waits=wm))
                    q_last[slot] = [mm_chain[-1]] + ([cw] if ps == 0 else [])
                for q in range(NQP):
                    py = p_y1 if q % 2 == 0 else p_y2
                    xld = dma("sync", t_x32[:, q % 2, :],
                              x2rr[:, pc, tp0 + q * 512:tp0 + (q + 1) * 512],
                              waits=[final_marks[-2]] if len(final_marks) >= 2
                              else [B0_DONE])
                    def evf(e, pc=pc, q=q, py=py):
                        e.tensor_tensor(out=t_ev[:, q % 2, :], in0=py[:, 0:512],
                                        in1=t_bc3[:, q * 512:(q + 1) * 512],
                                        op=ALU.mult)
                        return e.scalar_tensor_tensor(out=t_ev[:, q % 2, :],
                                                      in0=t_ev[:, q % 2, :],
                                                      scalar=modc(5, pc),
                                                      in1=t_x32[:, q % 2, :],
                                                      op0=ALU.mult, op1=ALU.add)
                    EVM = ve(evf, waits=[mm_chain[-(NQP - q)], xld])
                    prev_evm[q] = EVM
                    final_marks.append(dma("sync",
                                           out_r[:, pc, tp0 + q * 512:
                                                 tp0 + (q + 1) * 512],
                                           t_ev[:, q % 2, :], sem=s_o,
                                           waits=[EVM]))
            if ps == 0:
                CACHE3_DONE = s_dc.at()
            prev_pass = [final_marks[-1]]
        return final_marks

    FINALS = ffn_y3()

    b.emit("sync", lambda e: None,
           waits=[s_o.at(), s_dw.at(), s_dx.at(), s_dc.at()], incs=[])
    b.finish()
    return nc, TAP


def _feat_cols(v):
    return np.ascontiguousarray(v.reshape(NCC, 128).T)


def _prep_inputs(inputs, n_sum_tiles=4, n_main_tiles=4):
    f32 = lambda a: np.ascontiguousarray(np.asarray(a), dtype=np.float32)
    x = f32(inputs["L0_attn_x"])
    T_MAIN = n_main_tiles * TT
    T_SUM = n_sum_tiles * TT

    wmod = f32(inputs["L0_wmod"])
    wT = {m: np.ascontiguousarray(f32(inputs["L0_attn_W_" + m]).T) for m in "rkvgo"}
    w1T = np.ascontiguousarray(f32(inputs["L0_ffn_w1"]).T)
    w2T = np.ascontiguousarray(f32(inputs["L0_ffn_w2"]).T)
    w3T = np.ascontiguousarray(f32(inputs["L0_ffn_w3"]).T)
    maa_w1 = f32(inputs["L0_attn_time_maa_w1"])
    maa_w2 = np.ascontiguousarray(f32(inputs["L0_attn_time_maa_w2"]).reshape(160, C))
    dw1 = f32(inputs["L0_attn_time_decay_w1"])
    dw2 = np.ascontiguousarray(np.concatenate(
        [f32(inputs["L0_attn_time_decay_w2"]),
         f32(inputs["L0_attn_time_decay"]).reshape(1, C)], axis=0))
    pf = np.zeros((128, NCC, NPF), np.float32)
    for nm, key in [("maa_w", "time_maa_w"), ("maa_k", "time_maa_k"),
                    ("maa_v", "time_maa_v"), ("maa_r", "time_maa_r"),
                    ("maa_g", "time_maa_g"), ("maa_x", "time_maa_x")]:
        pf[:, :, PF[nm]] = _feat_cols(f32(inputs["L0_attn_" + key]).reshape(C))
    pf[:, :, PF["g1"]] = _feat_cols(f32(inputs["L0_ffn_g1"]))
    pf[:, :, PF["g2"]] = _feat_cols(f32(inputs["L0_ffn_g2"]))
    g3c = np.ascontiguousarray(f32(inputs["L0_ffn_g3"]).reshape(NOC, 128).T)
    u = _feat_cols(f32(inputs["L0_attn_time_faaaa"]).reshape(C))
    gb2 = np.ascontiguousarray(np.stack(
        [f32(inputs["L0_attn_ln_gamma"]), f32(inputs["L0_attn_ln_beta"])], axis=0))
    cmat = f32(inputs["L0_c"])

    ones1 = np.ones((128, 1), bfdt)
    onesf = np.ones((128, 1), np.float32)
    onesr = np.ones((1, 128), bfdt)
    onesrf = np.ones((1, 128), np.float32)
    id128 = np.eye(128, dtype=bfdt)
    mst = np.triu(np.ones((128, 128), np.float32), 1).astype(bfdt)
    mst4 = np.ascontiguousarray(np.tile(mst, (1, 4)))
    ect = np.zeros((128, NCC, 32), bfdt)
    for cc in range(NCC):
        ect[0:64, cc, 2 * cc] = 1
        ect[64:128, cc, 2 * cc + 1] = 1
    ectt = np.zeros((32, NCC * 128), bfdt)
    for cc in range(NCC):
        ectt[2 * cc, cc * 128:cc * 128 + 64] = 1
        ectt[2 * cc + 1, cc * 128 + 64:cc * 128 + 128] = 1

    cbias = np.tile(np.array([[LN_EPS, GN_EPS, RMS_EPS]], np.float32), (128, 1))
    shared = dict(cbias=cbias, wmod=wmod, wr=wT["r"], wk=wT["k"], wv=wT["v"], wg=wT["g"],
                  wo=wT["o"], w1=w1T, w2=w2T, w3=w3T, maa_w1=maa_w1,
                  maa_w2=maa_w2, dec_w1=dw1, dec_w2=dw2, pfeat=pf, g3vec=g3c,
                  uvec=u, gambet=gb2, ones1=ones1, onesf=onesf, onesr=onesr,
                  onesrf=onesrf, id128=id128, mst4=mst4, ect=ect, ectt=ectt,
                  ecttp=ectt)
    per_core = []
    for core in range(8):
        bb, half = core // 2, core % 2
        t0 = half * 2048
        xT = np.ascontiguousarray(x[bb, t0:t0 + T_MAIN, :].T)
        xsT = np.ascontiguousarray(x[bb, 0:T_SUM, :].T) if T_SUM > 0 else \
            np.zeros((C, 1), np.float32)
        if half == 1:
            xprev = _feat_cols(x[bb, t0 - 1, :])
            hmask = np.ones((128, 1), np.float32)
            smask = np.ones((128, 1), np.float32)
        else:
            xprev = np.zeros((128, NCC), np.float32)
            hmask = np.zeros((128, 1), np.float32)
            smask = np.zeros((128, 1), np.float32)
        im = dict(shared)
        im.update(xT=xT, xsT=xsT, xprev=xprev, hmask=hmask, smask=smask,
                  cvec=_feat_cols(cmat[bb]))
        per_core.append(im)
    return per_core


def kernel(**inputs):
    n_sum, n_main = 4, 4
    per_core = _prep_inputs(inputs, n_sum, n_main)
    nc, _ = build_program(n_sum, n_main)
    res = run_bass_kernel_spmd(nc, per_core, core_ids=list(range(8)))
    B, T = 4, 4096
    T_MAIN = n_main * TT
    out = np.zeros((B, T, C), np.float32)
    for core in range(8):
        bb, half = core // 2, core % 2
        out[bb, half * T_MAIN:(half + 1) * T_MAIN, :] = \
            np.asarray(res.results[core]["outT"]).T
    return out


# version 14
# speedup vs baseline: 1.0004x; 1.0004x over previous
import os
import numpy as np
import ml_dtypes
import concourse.bass as bass
import concourse.mybir as mybir
from concourse.bass_utils import run_bass_kernel_spmd

F32 = mybir.dt.float32
BF16 = mybir.dt.bfloat16
AF = mybir.ActivationFunctionType
ALU = mybir.AluOpType
AX = mybir.AxisListType.X
bfdt = ml_dtypes.bfloat16

_FP64_SIM = os.environ.get("BASS_SIM_FP64") == "1"
RND_C = float(1.5 * 2.0 ** 52) if _FP64_SIM else float(1.5 * 2.0 ** 23)

C = 2048
NCC = C // 128
HEADS = 32
HD = 64
DFF = 8192
NOC = DFF // 128
TT = 512
LCH = 128
NCH = TT // LCH
LN_EPS = 1e-5
GN_EPS = 1e-5 * 64.0
RMS_EPS = 1e-8
ATANH_HALF = 0.5493061443340548
QCLIP = 1e-4

PF = dict(maa_w=0, maa_k=1, maa_v=2, maa_r=3, maa_g=4, maa_x=5,
          g1=6, g2=7, g1a=8, g2a=9)
NPF = 10
MAA_F = ["maa_w", "maa_k", "maa_v", "maa_r", "maa_g"]


class Sem:
    __slots__ = ("h", "n", "name")

    def __init__(self, nc, name):
        self.h = nc.alloc_semaphore(name)
        self.n = 0
        self.name = name

    def inc(self, k=1):
        self.n += k
        return (self, k)

    def at(self):
        return (self, self.n)


class Builder:
    ENGINES = ("sync", "gpsimd", "tensor", "vector", "scalar")

    def __init__(self, nc, strict=False):
        self.nc = nc
        self.strict = strict
        self.steps = {e: [] for e in self.ENGINES}
        self.all_sems = []

    def emit(self, eng, fn, waits=(), incs=()):
        waits = list(waits)
        if self.strict:
            have = {id(s) for (s, v) in waits}
            for s in self.all_sems:
                if id(s) not in have:
                    waits.append(s.at())
        self.steps[eng].append((fn, waits, list(incs)))

    def finish(self):
        with self.nc.Block() as block:
            def mk_body(name):
                steps = self.steps[name]
                def body(e):
                    for fn, waits, incs in steps:
                        for (s, v) in waits:
                            if v > 0:
                                e.wait_ge(s.h, v)
                        inst = fn(e)
                        for (s, k) in incs:
                            assert inst is not None, f"no inst to tag on {name}"
                            inst = inst.then_inc(s.h, k)
                return body
            block.sync(mk_body("sync"))
            block.gpsimd(mk_body("gpsimd"))
            block.tensor(mk_body("tensor"))
            block.vector(mk_body("vector"))
            block.scalar(mk_body("scalar"))


def build_program(n_sum_tiles=4, n_main_tiles=4, taps=(), strict=False,
                  stop_after=None, skip=()):
    skip = set(skip)
    taps = set(taps)
    T_MAIN = n_main_tiles * TT
    T_SUM = n_sum_tiles * TT
    NSL = n_main_tiles
    NQ = T_MAIN // 512
    Y3P = max(1, T_MAIN // 1024)
    Y3T = T_MAIN // Y3P

    nc = bass.Bass("TRN2", target_bir_lowering=False,
                   detect_race_conditions=False)
    b = Builder(nc, strict=strict)

    DI = lambda name, shape, dt=F32: nc.dram_tensor(name, list(shape), dt, kind="ExternalInput")
    DO = lambda name, shape, dt=F32: nc.dram_tensor(name, list(shape), dt, kind="ExternalOutput")
    DS = lambda name, shape, dt=F32: nc.dram_tensor(name, list(shape), dt, kind="Internal")
    SB = lambda name, shape, dt=F32: nc.alloc_sbuf_tensor(name, list(shape), dt)
    PS = lambda name, shape, dt=F32: nc.alloc_psum_tensor(name, list(shape), dt)

    d_x = DI("xT", (C, T_MAIN))
    d_xs = DI("xsT", (C, max(T_SUM, 1)))
    d_xprev = DI("xprev", (128, NCC))
    d_hmask = DI("hmask", (128, 1))
    d_smask = DI("smask", (128, 1))
    d_cvec = DI("cvec", (128, NCC))
    d_wmod = DI("wmod", (C, 6 * C))
    d_wproj = {m: DI("w" + m, (C, C)) for m in "rkvgo"}
    d_w1 = DI("w1", (C, DFF))
    d_w2 = DI("w2", (C, DFF))
    d_w3 = DI("w3", (DFF, C))
    d_maaw1 = DI("maa_w1", (C, 160))
    d_maaw2 = DI("maa_w2", (160, C))
    d_dw1 = DI("dec_w1", (C, 64))
    d_dw2 = DI("dec_w2", (65, C))
    d_pf = DI("pfeat", (128, NCC, NPF))
    d_g3 = DI("g3vec", (128, NOC))
    d_u = DI("uvec", (128, NCC))
    d_gb2 = DI("gambet", (2, C))
    d_cbias = DI("cbias", (128, 3))
    d_ones1 = DI("ones1", (128, 1), BF16)
    d_onesf = DI("onesf", (128, 1))
    d_onesr = DI("onesr", (1, 128), BF16)
    d_onesrf = DI("onesrf", (1, 128))
    d_id128 = DI("id128", (128, 128), BF16)
    d_mst4 = DI("mst4", (128, 512), BF16)
    d_ect = DI("ect", (128, NCC, 32), BF16)
    d_ectt = DI("ectt", (32, NCC * 128), BF16)
    s_x2 = DS("x2scr", (C, T_MAIN))
    s_hm = DS("hmscr", (DFF, T_MAIN), BF16)
    s_w3q = DS("w3qscr", (64, 128, 16, 128), BF16)
    s_wc = {m: DS("wc" + m, (16, 128, NCC, 128), BF16) for m in "rkvgo"}
    d_out = DO("outT", (C, T_MAIN))

    t_pf = SB("t_pf", (128, NCC, NPF))
    t_g3 = SB("t_g3", (128, NOC))
    t_g3a = SB("t_g3a", (128, NOC))
    t_u = SB("t_u", (128, NCC))
    t_cvec = SB("t_cvec", (128, NCC))
    t_cvb = SB("t_cvb", (128, NCC), BF16)
    t_xprev = SB("t_xprev", (128, NCC))
    t_hmask = SB("t_hmask", (128, 1))
    t_smask = SB("t_smask", (128, 1))
    t_hpcol = SB("t_hpcol", (128, NCC))
    t_hpc2 = SB("t_hpc2", (128, NCC))
    t_ones = SB("t_ones", (128, 1), BF16)
    t_onesf = SB("t_onesf", (128, 1))
    t_onesr = SB("t_onesr", (1, 128), BF16)
    t_onesrf = SB("t_onesrf", (1, 128))
    t_id = SB("t_id", (128, 128), BF16)
    t_mst = SB("t_mst", (128, 512), BF16)
    t_ect = SB("t_ect", (128, NCC, 32), BF16)
    t_egb = SB("t_egb", (33, NCC * 128), BF16)
    t_gb2 = SB("t_gb2", (1, C), BF16)
    t_cbias = SB("t_cbias", (128, 3))
    t_mod = SB("t_mod", (128, 6, NCC))
    t_tanhd = SB("t_tanhd", (65, TT), BF16)
    t_a1 = SB("t_a1", (128, TT), BF16)
    t_ptot = SB("t_ptot", (128, NCC))
    t_small = SB("t_small", (128, 72))
    t_qs = SB("t_qs", (128, 8))
    t_wacc = SB("t_wacc", (128, 64))
    t_row = SB("t_row", (1, 1536))
    t_rowb = SB("t_rowb", (1, 1024), BF16)
    t_rs33 = SB("t_rs33", (33, 512), BF16)
    t_mu33 = SB("t_mu33", (33, 512), BF16)
    t_sqb = SB("t_sqb", (128, 2, 512), BF16)
    t_ev = SB("t_ev", (128, 2, 512))
    t_x32 = SB("t_x32", (128, 2, 512))

    AR_BYTES = 172064
    t_arena = SB("t_arena", (128, AR_BYTES // 2), BF16)

    def addr(h):
        return nc.lookup_mloc(h).addr
    A0 = addr(t_arena)

    def AT(name, shape, dt, off):
        return nc.alloc_sbuf_tensor_at(name, list(shape), dt, offset=A0 + off)

    t_xbf = AT("t_xbf", (128, NCC, TT), BF16, 0)
    t_hext = AT("t_hext", (128, NCC, TT + 1), BF16, 16384)
    t_hh = AT("t_hh", (128, NCC, TT), BF16, 32800)
    t_xm = AT("t_xm", (128, NCC, TT), BF16, 49184)
    t_y = AT("t_y", (128, NCC, TT), BF16, 49184)
    t_r = AT("t_r", (128, NCC, TT), BF16, 65568)
    t_k = AT("t_k", (128, NCC, TT), BF16, 81952)
    t_v = AT("t_v", (128, NCC, TT), BF16, 98336)
    t_g = AT("t_g", (128, NCC, TT), BF16, 114720)
    t_wbuf = AT("t_wbuf", (128, 2, NCC, 128), BF16, 131104)
    t_mw1 = AT("t_mw1", (128, NCC, 160), BF16, 139296)
    t_mw2a = AT("t_mw2a", (96, C), BF16, 144416)
    t_mw2b = AT("t_mw2b", (64, C), BF16, 148512)
    t_dw1 = AT("t_dw1", (128, NCC, 64), BF16, 152608)
    t_dw2 = AT("t_dw2", (65, C), BF16, 154656)
    t_sall = AT("t_sall", (128, NCC * HD), F32, 158752)
    t_wmp = AT("t_wmp", (128, 4, 256), F32, 162848)

    t_d = AT("t_d", (128, NCC, LCH), BF16, 0)
    t_rku = AT("t_rku", (128, NCC, LCH), BF16, 0)
    t_fr = AT("t_fr", (128, NCC, LCH), BF16, 4096)
    t_ktr = AT("t_ktr", (128, NCC, LCH), BF16, 4096)
    t_rtE = AT("t_rtE", (128, NCC, LCH), BF16, 8192)
    t_rtO = AT("t_rtO", (128, NCC, LCH), BF16, 12288)
    t_kt = AT("t_kt", (128, NCC, LCH), BF16, 16384)
    t_kp = AT("t_kp", (128, NCC, LCH), BF16, 20480)
    t_fk = AT("t_fk", (128, NCC, LCH), BF16, 20480)
    t_vtr = AT("t_vtr", (128, NCC, LCH), BF16, 24576)
    t_am = AT("t_am", (128, HEADS, LCH), BF16, 28672)
    t_sbf = AT("t_sbf", (128, NCC * HD), BF16, 36864)
    t_cs = AT("t_cs", (128, NCC, LCH + 1), F32, 38912)
    t_a2 = AT("t_a2", (64, TT), BF16, 166944)
    t_rkuhb = AT("t_rkuhb", (32, LCH), BF16, 167968)
    t_wacc2 = AT("t_wacc2", (128, 324), F32, 168224)

    t_xq1 = AT("t_xq1", (128, NCC, T_MAIN), BF16, 0)
    t_xq2 = AT("t_xq2", (128, NCC, T_MAIN), BF16, 65568)
    t_x2f = AT("t_x2f", (128, NCC, 512), F32, 131104)
    t_wf32 = AT("t_wf32", (128, NCC, 128), F32, 131104)
    t_wq = AT("t_wq", (128, 2, NCC, 128), BF16, 139296)
    t_hmst = AT("t_hmst", (128, 512), BF16, 147488)
    t_sqst = AT("t_sqst", (128, 512), BF16, 148512)
    t_bc1 = AT("t_bc1", (128, 2048), BF16, 149536)
    t_bc2 = AT("t_bc2", (128, 2048), BF16, 153632)
    t_m3 = AT("t_m3", (128, 2048), BF16, 157728)
    t_inv1 = AT("t_inv1", (1, 2048), BF16, 163872)
    t_inv2 = AT("t_inv2", (1, 2048), BF16, 167968)
    t_xq3 = AT("t_xq3", (128, NOC, 1024 if T_MAIN >= 1024 else T_MAIN), BF16, 0)
    t_s3row = AT("t_s3row", (1, 2048), BF16, 147488)
    t_mx3row = AT("t_mx3row", (1, 2048), BF16, 151584)
    t_bc3 = AT("t_bc3", (128, 1024), BF16, 155680)
    t_q3bc = AT("t_q3bc", (128, 1024), BF16, 157728)
    t_inv3 = AT("t_inv3", (1, 2048), BF16, 159776)

    p_a = PS("p_a", (128, 512))
    p_b = PS("p_b", (128, 512))
    p_y1 = PS("p_y1", (128, 512))
    p_y2 = PS("p_y2", (128, 512))
    p_t1 = PS("p_t1", (128, 512))
    p_t2 = PS("p_t2", (128, 512))
    p_m1 = PS("p_m1", (128, 512))
    p_m2 = PS("p_m2", (128, 1024), BF16)

    s_dw = Sem(nc, "s_dw")
    s_dx = Sem(nc, "s_dx")
    s_dc = Sem(nc, "s_dc")
    s_t = Sem(nc, "s_t")
    s_v = Sem(nc, "s_v")
    s_s = Sem(nc, "s_s")
    s_g = Sem(nc, "s_g")
    s_o = Sem(nc, "s_o")
    b.all_sems = [s_dw, s_dx, s_dc, s_t, s_v, s_s, s_g, s_o]

    TAP = {}
    def deftap(name, shape, dt=F32):
        if name in taps:
            TAP[name] = DO("tap_" + name, shape, dt)
    deftap("mod", (128, 6, NCC))
    deftap("modraw", (128, 96))
    deftap("h", (128, NCC, TT), BF16)
    deftap("hh", (128, NCC, TT), BF16)
    deftap("xm", (128, NCC, TT), BF16)
    deftap("tanhd", (65, TT), BF16)
    deftap("a1", (96, TT), BF16)
    deftap("xmg", (128, NCC, TT), BF16)
    deftap("xmw", (128, NCC, TT), BF16)
    deftap("r", (128, NCC, TT), BF16)
    deftap("k", (128, NCC, TT), BF16)
    deftap("v", (128, NCC, TT), BF16)
    deftap("g", (128, NCC, TT), BF16)
    deftap("d", (128, NCC, LCH), BF16)
    deftap("cs", (128, NCC, LCH + 1))
    deftap("am", (128, HEADS, LCH), BF16)
    deftap("sall", (128, NCC * HD))
    deftap("y", (128, NCC, TT), BF16)
    deftap("qs", (128, 6))
    deftap("cells", (1, 24))
    deftap("cells2", (1, 24))
    deftap("wacc", (128, 324))
    deftap("wacc1", (128, 64))
    deftap("xq1", (128, NCC, T_MAIN), BF16)
    deftap("xq2", (128, NCC, T_MAIN), BF16)
    deftap("inv1", (1, T_MAIN), BF16)
    deftap("inv2", (1, T_MAIN), BF16)
    deftap("m3", (128, T_MAIN), BF16)
    deftap("s3row", (1, T_MAIN), BF16)
    deftap("mx3row", (1, T_MAIN), BF16)
    deftap("inv3", (1, T_MAIN), BF16)
    deftap("xq3", (128, NOC, 1024 if T_MAIN >= 1024 else T_MAIN), BF16)

    QUEUE = dict(sync=s_dx, gpsimd=s_dw, scalar=s_dc)

    def dma(q, dst, src, sem=None, waits=()):
        sem_ = sem if sem is not None else QUEUE[q]
        b.emit(q, (lambda dst=dst, src=src: (lambda e: e.dma_start(out=dst, in_=src)))(),
               waits, [sem_.inc(16)])
        return sem_.at()

    def ve(fn, waits=()):
        b.emit("vector", fn, waits, [s_v.inc()])
        return s_v.at()

    def sc(fn, waits=()):
        b.emit("scalar", fn, waits, [s_s.inc()])
        return s_s.at()

    def te(fn, waits=()):
        b.emit("tensor", fn, waits, [s_t.inc()])
        return s_t.at()

    def ge(fn, waits=()):
        b.emit("gpsimd", fn, waits, [s_g.inc()])
        return s_g.at()

    def pfc(name, cc):
        return t_pf[:, cc, PF[name]:PF[name] + 1]

    def modc(m, cc):
        return t_mod[:, m, cc:cc + 1]

    def mw2_block(f, oc):
        if f < 3:
            return t_mw2a[32 * f:32 * f + 32, oc * 128:(oc + 1) * 128]
        return t_mw2b[32 * (f - 3):32 * (f - 3) + 32, oc * 128:(oc + 1) * 128]

    def a_slice(f):
        if f < 3:
            return t_a1[32 * f:32 * f + 32, :]
        return t_a2[32 * (f - 3):32 * (f - 3) + 32, :]

    for dst, src in [
        (t_pf[:, :, 0:8], d_pf[:, :, 0:8]), (t_g3[:, :], d_g3[:, :]),
        (t_u[:, :], d_u[:, :]),
        (t_cvec[:, :], d_cvec[:, :]), (t_xprev[:, :], d_xprev[:, :]),
        (t_hmask[:, :], d_hmask[:, :]), (t_smask[:, :], d_smask[:, :]),
        (t_ones[:, :], d_ones1[:, :]), (t_onesf[:, :], d_onesf[:, :]),
        (t_onesr[:, :], d_onesr[:, :]), (t_onesrf[:, :], d_onesrf[:, :]),
        (t_id[:, :], d_id128[:, :]),
        (t_mst[:, :], d_mst4[:, :]), (t_ect[:, :, :], d_ect[:, :, :]),
        (t_egb[0:32, :], d_ectt[:, :]), (t_cbias[:, :], d_cbias[:, :]),
    ]:
        dma("sync", dst, src)
    X_CONST = s_dx.at()
    GB2 = dma("gpsimd", t_gb2[0:1, :], d_gb2[0:1, :])
    GB2B = dma("gpsimd", t_egb[32:33, :], d_gb2[1:2, :])
    dma("gpsimd", t_mw1[:, :, :], d_maaw1.ap().rearrange("(cc p) m -> p cc m", p=128))
    dma("gpsimd", t_mw2a[0:96, :], d_maaw2[0:96, :])
    dma("gpsimd", t_mw2b[0:64, :], d_maaw2[96:160, :])
    dma("gpsimd", t_dw1[:, :, :], d_dw1.ap().rearrange("(cc p) m -> p cc m", p=128))
    dma("gpsimd", t_dw2[:, :], d_dw2[:, :])
    W_CONST = s_dw.at()

    ge(lambda e: e.memset(t_cs[:, :, 0:1], 0.0))
    ge(lambda e: e.memset(t_sall[:, :], 0.0))
    ge(lambda e: e.memset(t_tanhd[64:65, :], 1.0))
    ge(lambda e: e.memset(t_rs33[32:33, :], 0.0))
    ge(lambda e: e.memset(t_mu33[32:33, :], 1.0))
    ge(lambda e: e.memset(t_wacc[:, :], 0.0))
    ge(lambda e: e.memset(t_wacc2[:, :], 0.0))
    MEMSET = s_g.at()

    G1A = sc(lambda e: e.activation(out=t_pf[:, :, 8:10], in_=t_pf[:, :, 6:8],
                                    func=AF.Abs), waits=[X_CONST])
    G3A = sc(lambda e: e.activation(out=t_g3a[:, :], in_=t_g3[:, :],
                                    func=AF.Abs), waits=[X_CONST])
    def gbc_round(h):
        def fn(e):
            return e.matmul(p_m1[0:32, 0:512], t_onesr[0:1, 0:32],
                            t_gb2[0:1, h * 512:(h + 1) * 512],
                            start=True, stop=True)
        return fn
    def egb_round(h):
        def fn(e):
            return e.tensor_tensor(out=t_egb[0:32, h * 512:(h + 1) * 512],
                                   in0=t_egb[0:32, h * 512:(h + 1) * 512],
                                   in1=p_m1[0:32, 0:512], op=ALU.mult)
        return fn
    for h in range(4):
        m = te(gbc_round(h), waits=[X_CONST, MEMSET, GB2] if h == 0 else [s_v.at()])
        ve(egb_round(h), waits=[m])
    EGB = ve(lambda e: e.tensor_scalar(out=t_egb[32:33, :],
                                       in0=t_egb[32:33, :],
                                       scalar1=-1.0, scalar2=None, op0=ALU.mult),
             waits=[GB2B])

    SILU_C = sc(lambda e: e.activation(out=t_small[:, 0:NCC], in_=t_cvec[:, :],
                                       func=AF.Silu), waits=[X_CONST])
    CVB = ve(lambda e: e.tensor_copy(out=t_cvb[:, :], in_=t_small[:, 0:NCC]),
             waits=[SILU_C])
    wmod_r = d_wmod.ap().rearrange("(cc p) m -> p cc m", p=128)
    mm_done = {}
    ld_done = {}
    for jc in range(96):
        pb = jc % 2
        wts = [mm_done[jc - 2]] if jc >= 2 else []
        ld_done[jc] = dma("gpsimd", t_wbuf[:, pb, :, :],
                          wmod_r[:, :, jc * 128:(jc + 1) * 128], waits=wts)
        def mkmm(jc=jc, pb=pb):
            def fn(e):
                last = None
                for ic in range(NCC):
                    last = e.matmul(p_m1[:, jc:jc + 1], t_wbuf[:, pb, ic, :],
                                    t_cvb[:, ic:ic + 1],
                                    start=(ic == 0), stop=(ic == NCC - 1))
                return last
            return fn
        w_mm = [ld_done[jc]]
        if jc == 0:
            w_mm.append(CVB)
        mm_done[jc] = te(mkmm(), waits=w_mm)
    MODMM = mm_done[95]
    MODEV = ve(lambda e: e.tensor_copy(
        out=t_mod[:, :, :].rearrange("p m cc -> p (m cc)"), in_=p_m1[:, 0:96]),
        waits=[MODMM])
    if "modraw" in TAP:
        dma("sync", TAP["modraw"].ap(), t_mod[:, :, :]
            .rearrange("p m cc -> p (m cc)"), sem=s_o, waits=[MODEV])

    def mp1(e):
        e.tensor_scalar(out=t_mod[:, 1, :], in0=t_mod[:, 1, :], scalar1=1.0,
                        scalar2=None, op0=ALU.add)
        e.tensor_scalar(out=t_mod[:, 4, :], in0=t_mod[:, 4, :], scalar1=1.0,
                        scalar2=None, op0=ALU.add)
        last = None
        for i_, idx in enumerate((2, 5)):
            e.tensor_scalar(out=t_small[:, i_ * 32:i_ * 32 + 16],
                            in0=t_mod[:, idx, :], scalar1=0.0,
                            scalar2=2.0, op0=ALU.is_ge, op1=ALU.mult)
            last = e.tensor_scalar(out=t_small[:, i_ * 32 + 16:i_ * 32 + 32],
                                   in0=t_mod[:, idx, :],
                                   scalar1=-1.0, scalar2=None, op0=ALU.mult)
        return last
    MP1 = ve(mp1, waits=[MODEV])
    def mp2(e):
        last = None
        for i_, idx in enumerate((2, 5)):
            e.tensor_scalar(out=t_small[:, i_ * 32:i_ * 32 + 16],
                            in0=t_small[:, i_ * 32:i_ * 32 + 16], scalar1=-1.0,
                            scalar2=None, op0=ALU.add)
            last = e.tensor_tensor(out=t_small[:, i_ * 32 + 16:i_ * 32 + 32],
                                   in0=t_mod[:, idx, :],
                                   in1=t_small[:, i_ * 32 + 16:i_ * 32 + 32],
                                   op=ALU.max)
        return last
    MP2 = ve(mp2, waits=[MP1])
    def mp3(e):
        last = None
        for i_ in range(2):
            last = e.tensor_scalar(out=t_small[:, i_ * 32 + 16:i_ * 32 + 32],
                                   in0=t_small[:, i_ * 32 + 16:i_ * 32 + 32],
                                   scalar1=0.25, scalar2=None, op0=ALU.max)
        return last
    MP3 = ve(mp3, waits=[MP2])
    def mp4(e):
        last = None
        for i_, idx in enumerate((2, 5)):
            last = e.tensor_tensor(out=t_mod[:, idx, :],
                                   in0=t_small[:, i_ * 32:i_ * 32 + 16],
                                   in1=t_small[:, i_ * 32 + 16:i_ * 32 + 32],
                                   op=ALU.mult)
        return last
    MOD_DONE = ve(mp4, waits=[MP3])

    if "mod" in TAP:
        dma("sync", TAP["mod"][:, :, :], t_mod[:, :, :], sem=s_o, waits=[MOD_DONE])

    def hprev_build():
        def sq(e):
            return e.activation(out=t_small[:, 48:48 + NCC], in_=t_xprev[:, :],
                                func=AF.Square)
        sv = sc(sq, waits=[X_CONST])
        def mm(e):
            e.matmul(p_b[0:1, 0:NCC], t_onesf[:, :], t_xprev[:, :],
                     start=True, stop=True)
            return e.matmul(p_b[0:1, NCC:2 * NCC], t_onesf[:, :],
                            t_small[:, 48:48 + NCC], start=True, stop=True)
        mmv = te(mm, waits=[sv])
        def smath0(e):
            e.reduce_sum(out=t_row[0:1, 0:1], in_=p_b[0:1, 0:NCC], axis=AX)
            return e.reduce_sum(out=t_row[0:1, 1:2], in_=p_b[0:1, NCC:2 * NCC],
                                axis=AX)
        sm0 = ve(smath0, waits=[mmv])
        sma = ve(lambda e: e.tensor_scalar(out=t_row[0:1, 0:1],
                                           in0=t_row[0:1, 0:1],
                                           scalar1=1.0 / C, scalar2=None,
                                           op0=ALU.mult), waits=[sm0])
        smb = ve(lambda e: e.tensor_tensor(out=t_row[0:1, 2:3],
                                           in0=t_row[0:1, 0:1],
                                           in1=t_row[0:1, 0:1], op=ALU.mult),
                 waits=[sma])
        sm = ve(lambda e: e.scalar_tensor_tensor(out=t_row[0:1, 1:2],
                                                 in0=t_row[0:1, 1:2],
                                                 scalar=1.0 / C,
                                                 in1=t_row[0:1, 2:3],
                                                 op0=ALU.mult,
                                                 op1=ALU.subtract),
                waits=[smb])
        def rsf(e):
            e.activation(out=t_row[0:1, 1:2], in_=t_row[0:1, 1:2],
                         func=AF.Ln, bias=t_cbias[0:1, 0:1])
            return e.activation(out=t_row[0:1, 1:2], in_=t_row[0:1, 1:2],
                                func=AF.Exp, scale=-0.5)
        rf = sc(rsf, waits=[sm])
        def bc(e):
            return e.matmul(p_b[:, 2:4], t_onesrf[:, :], t_row[0:1, 0:2],
                            start=True, stop=True)
        bcv = te(bc, waits=[rf])
        h1 = ve(lambda e: e.tensor_scalar(out=t_hpcol[:, :], in0=t_xprev[:, :],
                                          scalar1=p_b[:, 2:3], scalar2=None,
                                          op0=ALU.subtract),
                waits=[bcv, MOD_DONE])
        h2_ = ve(lambda e: e.tensor_scalar(out=t_hpcol[:, :], in0=t_hpcol[:, :],
                                           scalar1=p_b[:, 3:4], scalar2=None,
                                           op0=ALU.mult), waits=[h1])
        h3 = ve(lambda e: e.tensor_tensor(out=t_hpcol[:, :], in0=t_hpcol[:, :],
                                          in1=t_mod[:, 1, :], op=ALU.mult),
                waits=[h2_])
        h4 = ve(lambda e: e.tensor_tensor(out=t_hpcol[:, :], in0=t_hpcol[:, :],
                                          in1=t_mod[:, 0, :], op=ALU.add),
                waits=[h3])
        return ve(lambda e: e.tensor_scalar(out=t_hpcol[:, :], in0=t_hpcol[:, :],
                                            scalar1=t_hmask[:, 0:1], scalar2=None,
                                            op0=ALU.mult), waits=[h4])
    HPREV = hprev_build()

    wmean_state = dict(i=0, lastred=None)
    WSRC = [d_w1, d_w2, d_w3]
    NWP = [DFF // 256, DFF // 256, C // 256]
    def emit_wmean(npieces):
        for _ in range(npieces):
            i = wmean_state["i"]
            if i >= 384:
                return
            mi, pi = i // 128, i % 128
            src = WSRC[mi].ap().rearrange("(cc p) m -> p cc m", p=128)
            ncol = NWP[mi]
            cc0, off = (pi // ncol) * 4, (pi % ncol) * 256
            piece = src[:, cc0:cc0 + 4, off:off + 256]
            wts = [wmean_state["lastred"]] if wmean_state["lastred"] else [MEMSET]
            ld = dma("scalar", t_wmp[:, :, :], piece, waits=wts)
            def red(e, i=i):
                dst = t_wacc[:, i:i + 1] if i < 64 else \
                    t_wacc2[:, i - 64:i - 63]
                return e.reduce_sum(out=dst,
                                    in_=t_wmp[:, :, :]
                                    .rearrange("p a bb -> p (a bb)"), axis=AX,
                                    apply_absolute_value=True)
            wmean_state["lastred"] = ve(red, waits=[ld])
            wmean_state["i"] = i + 1

    def wmean_finalize():
        emit_wmean(384)
        def redf(e):
            e.reduce_sum(out=t_wacc2[:, 320:321], in_=t_wacc[:, 0:64], axis=AX)
            e.reduce_sum(out=t_wacc2[:, 323:324], in_=t_wacc2[:, 0:64], axis=AX)
            e.reduce_sum(out=t_wacc2[:, 321:322], in_=t_wacc2[:, 64:192], axis=AX)
            return e.reduce_sum(out=t_wacc2[:, 322:323],
                                in_=t_wacc2[:, 192:320], axis=AX)
        rf0 = ve(redf, waits=[wmean_state["lastred"]])
        rf_ = ve(lambda e: e.tensor_tensor(out=t_wacc2[:, 320:321],
                                           in0=t_wacc2[:, 320:321],
                                           in1=t_wacc2[:, 323:324], op=ALU.add),
                 waits=[rf0])
        mm = te(lambda e: e.matmul(p_m1[0:1, 0:3], t_onesf[:, :],
                                   t_wacc2[:, 320:323], start=True, stop=True),
                waits=[rf_])
        NW = [float(C * DFF), float(C * DFF), float(DFF * C)]
        def wsc1(e):
            last = None
            for i in range(3):
                last = e.tensor_scalar(out=t_small[0:1, 54 + i:55 + i],
                                       in0=p_m1[0:1, i:i + 1],
                                       scalar1=1.0 / NW[i], scalar2=None,
                                       op0=ALU.mult)
            return last
        w1_ = ve(wsc1, waits=[mm])
        def wsc2(e):
            for i in range(3):
                e.tensor_scalar(out=t_small[0:1, 57 + i:58 + i],
                                in0=t_small[0:1, 54 + i:55 + i], scalar1=RMS_EPS,
                                scalar2=ATANH_HALF, op0=ALU.add, op1=ALU.mult)
            e.tensor_scalar(out=t_small[0:1, 63:66],
                            in0=t_small[0:1, 54:57], scalar1=1.0,
                            scalar2=None, op0=ALU.add)
            return e.tensor_scalar(out=t_small[0:1, 66:69],
                                   in0=t_small[0:1, 54:57], scalar1=-1.0,
                                   scalar2=-1.0, op0=ALU.mult, op1=ALU.subtract)
        w2_ = ve(wsc2, waits=[w1_])
        def wsc3(e):
            e.tensor_scalar(out=t_small[0:1, 60:63],
                            in0=t_small[0:1, 57:60],
                            scalar1=-1.0, scalar2=None, op0=ALU.mult)
            return e.reciprocal(out=t_small[0:1, 66:69],
                                in_=t_small[0:1, 66:69])
        w3_ = ve(wsc3, waits=[w2_])
        ws = ve(lambda e: e.tensor_tensor(out=t_small[0:1, 63:66],
                                          in0=t_small[0:1, 63:66],
                                          in1=t_small[0:1, 66:69], op=ALU.mult),
                waits=[w3_])
        AH = sc(lambda e: e.activation(out=t_small[0:1, 69:72],
                                       in_=t_small[0:1, 63:66],
                                       func=AF.Ln), waits=[ws])
        AH2 = ve(lambda e: e.tensor_scalar(out=t_small[0:1, 69:72],
                                           in0=t_small[0:1, 69:72], scalar1=0.5,
                                           scalar2=None, op0=ALU.mult), waits=[AH])
        NT = w3_
        def qbc(e):
            e.matmul(p_m1[:, 8:11], t_onesrf[:, :], t_small[0:1, 57:60],
                     start=True, stop=True)
            return e.matmul(p_m1[:, 11:14], t_onesrf[:, :], t_small[0:1, 60:63],
                            start=True, stop=True)
        qb = te(qbc, waits=[NT, AH2])
        QS = ve(lambda e: e.tensor_copy(out=t_qs[:, 0:6], in_=p_m1[:, 8:14]),
                waits=[qb])
        if "qs" in TAP:
            dma("sync", TAP["qs"][:, :], t_qs[:, 0:6], sem=s_o, waits=[QS])
        if "cells" in TAP:
            dma("sync", TAP["cells"].ap(), t_small[0:1, 48:72], sem=s_o,
                waits=[QS])
        if "wacc" in TAP:
            dma("sync", TAP["wacc"].ap(), t_wacc2[:, :], sem=s_o, waits=[QS])
        if "wacc1" in TAP:
            dma("sync", TAP["wacc1"].ap(), t_wacc[:, :], sem=s_o, waits=[QS])
        return QS, AH2

    wbuf_last_use = {0: [mm_done[94]], 1: [mm_done[95]]}
    CACHE_DONE = {}

    def ln_h_stage(xin_dram, toff, first_tile, sum_phase, xbf_free,
                   last_tile=False):
        xsl = xin_dram.ap().rearrange("(cc p) t -> p cc t", p=128)[:, :, toff:toff + TT]
        XL = dma("gpsimd", t_xbf[:, :, :], xsl, waits=xbf_free)
        if first_tile:
            if sum_phase:
                cp = ve(lambda e: e.memset(t_hext[:, :, 0:1], 0.0), waits=[MEMSET] + xbf_free)
            else:
                cp = ve(lambda e: e.tensor_copy(out=t_hext[:, :, 0:1],
                                                in_=t_hpcol[:, :]),
                        waits=[HPREV] + xbf_free)
        else:
            cp = ve(lambda e: e.tensor_copy(out=t_hext[:, :, 0:1], in_=t_hpc2[:, :]),
                    waits=xbf_free)
        for cc in range(NCC):
            def sqf(e, cc=cc):
                return e.activation(out=t_sqb[:, cc % 2, :], in_=t_xbf[:, cc, :],
                                    func=AF.Square)
            w = [XL] if cc < 2 else [s_t.at()]
            smk = sc(sqf, waits=w)
            def mmf(e, cc=cc):
                e.matmul(p_m1[0:1, 0:512], t_ones[:, :], t_xbf[:, cc, :],
                         start=(cc == 0), stop=(cc == NCC - 1))
                return e.matmul(p_a[0:1, 0:512], t_ones[:, :], t_sqb[:, cc % 2, :],
                                start=(cc == 0), stop=(cc == NCC - 1))
            te(mmf, waits=[smk] + ([XL] if cc == 0 else []))
        STATS_MM = s_t.at()
        def stat_math(e):
            e.tensor_scalar(out=t_row[0:1, 1024 - 512:1024], in0=p_m1[0:1, 0:512],
                            scalar1=1.0 / C, scalar2=None, op0=ALU.mult)
            e.tensor_tensor(out=t_row[0:1, 1024:1536], in0=t_row[0:1, 512:1024],
                            in1=t_row[0:1, 512:1024], op=ALU.mult)
            return e.scalar_tensor_tensor(out=t_row[0:1, 0:512], in0=p_a[0:1, 0:512],
                                          scalar=1.0 / C, in1=t_row[0:1, 1024:1536],
                                          op0=ALU.mult, op1=ALU.subtract)
        sm = ve(stat_math, waits=[STATS_MM])
        def raf(e):
            e.activation(out=t_row[0:1, 0:512], in_=t_row[0:1, 0:512],
                         func=AF.Ln, bias=t_cbias[0:1, 0:1])
            return e.activation(out=t_row[0:1, 0:512], in_=t_row[0:1, 0:512],
                                func=AF.Exp, scale=-0.5)
        ra = sc(raf, waits=[sm])
        def bcf(e):
            e.matmul(p_m1[:, 0:512], t_onesrf[:, :], t_row[0:1, 0:512],
                     start=True, stop=True)
            return e.matmul(p_a[:, 0:512], t_onesrf[:, :], t_row[0:1, 512:1024],
                            start=True, stop=True)
        bc = te(bcf, waits=[ra])
        for cc in range(NCC):
            def hf(e, cc=cc):
                e.tensor_tensor(out=t_ev[:, 0, :], in0=t_xbf[:, cc, :],
                                in1=p_a[:, 0:512], op=ALU.subtract)
                e.tensor_tensor(out=t_ev[:, 0, :], in0=t_ev[:, 0, :],
                                in1=p_m1[:, 0:512], op=ALU.mult)
                return e.tensor_scalar(out=t_hext[:, cc, 1:TT + 1], in0=t_ev[:, 0, :],
                                       scalar1=modc(1, cc), scalar2=modc(0, cc),
                                       op0=ALU.mult, op1=ALU.add)
            ve(hf, waits=[bc, cp, MOD_DONE] if cc == 0 else [])
        H_DONE = s_v.at()
        BND = ve(lambda e: e.tensor_copy(out=t_hpc2[:, :],
                                         in_=t_hext[:, :, TT:TT + 1]))
        HH_DONE = ve(lambda e: e.tensor_tensor(out=t_hh[:, :, :],
                                               in0=t_hext[:, :, 0:TT],
                                               in1=t_hext[:, :, 1:TT + 1],
                                               op=ALU.subtract))
        lntap = []
        if last_tile and not sum_phase:
            if "h" in TAP:
                lntap.append(dma("sync", TAP["h"].ap(), t_hext[:, :, 1:TT + 1],
                                 sem=s_o, waits=[H_DONE]))
            if "hh" in TAP:
                lntap.append(dma("sync", TAP["hh"].ap(), t_hh[:, :, :],
                                 sem=s_o, waits=[HH_DONE]))
        return dict(h=H_DONE, hh=HH_DONE, x_loaded=XL, bnd=BND, lntap=lntap)

    def maa_stage(marks):
        for cc in range(NCC):
            def xf(e, cc=cc):
                return e.scalar_tensor_tensor(out=t_xm[:, cc, :], in0=t_hh[:, cc, :],
                                              scalar=pfc("maa_x", cc),
                                              in1=t_hext[:, cc, 1:TT + 1],
                                              op0=ALU.mult, op1=ALU.add)
            ve(xf, waits=[marks["hh"], W_CONST] if cc == 0 else [])
        XXX = s_v.at()
        def amm(e):
            last = None
            for ic in range(NCC):
                e.matmul(p_b[:, 0:512], t_mw1[:, ic, 0:128], t_xm[:, ic, :],
                         start=(ic == 0), stop=(ic == NCC - 1))
            for ic in range(NCC):
                last = e.matmul(p_a[0:32, 0:512], t_mw1[:, ic, 128:160],
                                t_xm[:, ic, :],
                                start=(ic == 0), stop=(ic == NCC - 1))
            return last
        AMM = te(amm, waits=[XXX])
        def atanh(e):
            e.activation(out=t_a1[0:96, :], in_=p_b[0:96, 0:512], func=AF.Tanh)
            e.activation(out=t_a2[0:32, :], in_=p_b[96:128, 0:512], func=AF.Tanh)
            return e.activation(out=t_a2[32:64, :], in_=p_a[0:32, 0:512],
                                func=AF.Tanh)
        return sc(atanh, waits=[AMM])

    def xm_build(f_idx, ATANH, after=()):
        marks = []
        for oc in range(NCC):
            bank = [p_a, p_b, p_y1, p_y2][oc % 4]
            def mmf(e, oc=oc, bank=bank):
                return e.matmul(bank[:, 0:512], mw2_block(f_idx, oc), a_slice(f_idx),
                                start=True, stop=True)
            wts = ([ATANH] + list(after)) if oc < 4 else [marks[oc - 4]]
            mmm = te(mmf, waits=wts)
            def xmf(e, oc=oc, bank=bank):
                e.scalar_tensor_tensor(out=t_ev[:, oc % 2, :], in0=bank[:, 0:512],
                                       scalar=pfc(MAA_F[f_idx], oc),
                                       in1=t_hh[:, oc, :], op0=ALU.add, op1=ALU.mult)
                return e.tensor_tensor(out=t_xm[:, oc, :], in0=t_ev[:, oc % 2, :],
                                       in1=t_hext[:, oc, 1:TT + 1], op=ALU.add)
            marks.append(ve(xmf, waits=[mmm]))
        return s_v.at()

    def decay_proj(XM):
        def dmm(e):
            last = None
            for ic in range(NCC):
                last = e.matmul(p_a[0:64, 0:512], t_dw1[:, ic, :], t_xm[:, ic, :],
                                start=(ic == 0), stop=(ic == NCC - 1))
            return last
        DM = te(dmm, waits=[XM])
        return sc(lambda e: e.activation(out=t_tanhd[0:64, :], in_=p_a[0:64, 0:512],
                                         func=AF.Tanh), waits=[DM])

    def full_proj(m, dst, XM, silu=False, first_use=False):
        wrr = d_wproj[m].ap().rearrange("(cc p) o -> p cc o", p=128)
        ev_marks = []
        banks4 = [p_a, p_b, p_y1, p_y2]
        for pc in range(NCC):
            wts = list(wbuf_last_use.get(pc % 2, []))
            if first_use:
                src = wrr[:, :, pc * 128:(pc + 1) * 128]
            else:
                src = s_wc[m][pc]
            piece = dma("gpsimd", t_wbuf[:, pc % 2, :, :], src,
                        waits=wts + ([] if first_use else [CACHE_DONE[m]]))
            cw = None
            if first_use:
                cw = dma("sync", s_wc[m][pc], t_wbuf[:, pc % 2, :, :],
                         waits=[piece])
            bank = banks4[pc % 4]
            def pmm(e, pc=pc, bank=bank):
                last = None
                for fi in range(NCC):
                    last = e.matmul(bank[:, 0:512], t_wbuf[:, pc % 2, fi, :],
                                    t_xm[:, fi, :], start=(fi == 0),
                                    stop=(fi == NCC - 1))
                return last
            wts2 = [piece, XM]
            if pc >= 4:
                wts2.append(ev_marks[pc - 4])
            mmdone = te(pmm, waits=wts2)
            wbuf_last_use[pc % 2] = [mmdone] + ([cw] if cw else [])
            if silu:
                def evf(e, pc=pc, bank=bank, dst=dst):
                    return e.activation(out=dst[:, pc, :], in_=bank[:, 0:512],
                                        func=AF.Silu)
                ev_marks.append(sc(evf, waits=[mmdone]))
            else:
                def evf(e, pc=pc, bank=bank, dst=dst):
                    return e.tensor_copy(out=dst[:, pc, :], in_=bank[:, 0:512])
                ev_marks.append(ve(evf, waits=[mmdone]))
        if first_use:
            CACHE_DONE[m] = s_dc.at()
        return ev_marks[-1]

    def wkv_chunk(c0, sum_phase, PD, prev_yev):
        for half in range(2):
            def vtf(e, half=half):
                last = None
                for j in range(8):
                    cc = half * 8 + j
                    last = e.transpose(p_m2[:, j * 128:(j + 1) * 128],
                                       t_v[:, cc, c0:c0 + LCH], t_id[:, :])
                return last
            tt_ = te(vtf, waits=[PD["v"], s_s.at()] + prev_yev)
            def vev(e, half=half):
                return e.copy(out=t_vtr[:, half * 8:(half + 1) * 8, :],
                              in_=p_m2[:, 0:1024])
            sc(vev, waits=[tt_] + prev_yev)
        VTR = s_s.at()
        for q in range(4):
            bank = p_a if q % 2 == 0 else p_b
            def dmm(e, q=q, bank=bank):
                last = None
                for j in range(4):
                    cc = q * 4 + j
                    last = e.matmul(bank[:, j * 128:(j + 1) * 128],
                                    t_dw2[:, cc * 128:(cc + 1) * 128],
                                    t_tanhd[:, c0:c0 + LCH], start=True, stop=True)
                return last
            lastp = PD.get("g", PD.get("v"))
            dm = te(dmm, waits=([PD["w"], lastp] + prev_yev) if q < 2
                    else [s_s.at()])
            def dev(e, q=q, bank=bank):
                return e.activation(out=t_d[:, q * 4:(q + 1) * 4, :],
                                    in_=bank[:, 0:512], func=AF.Exp)
            sc(dev, waits=[dm] + (prev_yev if q == 0 else []))
        DEXP = s_s.at()
        def scans(e):
            e.memset(t_cs[:, :, 0:1], 0.0)
            last = None
            for cc in range(NCC):
                last = e.tensor_tensor_scan(out=t_cs[:, cc, 1:LCH + 1],
                                            data0=t_d[:, cc, :], data1=t_d[:, cc, :],
                                            initial=0.0, op0=ALU.add, op1=ALU.bypass)
            return last
        SCAN = ve(scans, waits=[DEXP, MEMSET] + (PD.get("lntap", [])
                                                 if c0 == 0 else []))
        def facts(e):
            e.activation(out=t_ptot[:, :],
                         in_=t_cs[:, :, LCH:LCH + 1]
                         .rearrange("p c one -> p (c one)"),
                         func=AF.Exp, scale=-1.0)
            if not sum_phase:
                e.activation(out=t_fr[:, :, :], in_=t_cs[:, :, 0:LCH], func=AF.Exp,
                             scale=-1.0)
            return e.activation(out=t_fk[:, :, :], in_=t_cs[:, :, 1:LCH + 1],
                                func=AF.Exp)
        FACT = sc(facts, waits=[SCAN])
        def velem(e):
            if not sum_phase:
                if c0 == 0:
                    e.memset(t_rtE[64:128, :, :], 0.0)
                    e.memset(t_rtO[0:64, :, :], 0.0)
                e.tensor_tensor(out=t_rtE[0:64, :, :],
                                in0=t_r[0:64, :, c0:c0 + LCH],
                                in1=t_fr[0:64, :, :], op=ALU.mult)
                e.tensor_tensor(out=t_rtO[64:128, :, :],
                                in0=t_r[64:128, :, c0:c0 + LCH],
                                in1=t_fr[64:128, :, :], op=ALU.mult)
            e.tensor_tensor(out=t_kt[:, :, :], in0=t_k[:, :, c0:c0 + LCH],
                            in1=t_fk[:, :, :], op=ALU.mult)
            for cc in range(NCC):
                e.tensor_scalar(out=t_kp[:, cc, :], in0=t_kt[:, cc, :],
                                scalar1=t_ptot[:, cc:cc + 1], scalar2=None,
                                op0=ALU.mult)
            return e.tensor_copy(out=t_sbf[:, :], in_=t_sall[:, :])
        wts = [FACT] + ([PD["r"], PD["k"]] if not sum_phase else [PD["k"]]) + prev_yev
        if c0 == 0:
            wts += PD.get("lntap", [])
        VEL = ve(velem, waits=wts)
        RKU = None
        if not sum_phase:
            def velem2(e):
                for cc in range(NCC):
                    e.tensor_scalar(out=t_rku[:, cc, :], in0=t_k[:, cc, c0:c0 + LCH],
                                    scalar1=t_u[:, cc:cc + 1], scalar2=None,
                                    op0=ALU.mult)
                return e.tensor_tensor(out=t_rku[:, :, :], in0=t_rku[:, :, :],
                                       in1=t_r[:, :, c0:c0 + LCH], op=ALU.mult)
            RKU = ve(velem2)
        for half in range(2):
            def ktf(e, half=half):
                last = None
                for j in range(8):
                    cc = half * 8 + j
                    last = e.transpose(p_m2[:, j * 128:(j + 1) * 128],
                                       t_kp[:, cc, :], t_id[:, :])
                return last
            tt_ = te(ktf, waits=[VEL, s_s.at()])
            def kev(e, half=half):
                return e.copy(out=t_ktr[:, half * 8:(half + 1) * 8, :],
                              in_=p_m2[:, 0:1024])
            sc(kev, waits=[tt_])
        KTR = s_s.at()
        sup_marks = []
        for part in range(2):
            pt = p_t1 if part == 0 else p_t2
            def tmm(e, part=part, pt=pt):
                last = None
                for pr in range(8):
                    pair = part * 8 + pr
                    for hh_ in range(2):
                        hof = hh_ * 64
                        last = e.matmul(pt[hof:hof + 64, pr * 64:(pr + 1) * 64],
                                        t_ktr[:, pair, hof:hof + 64],
                                        t_vtr[:, pair, hof:hof + 64],
                                        start=True, stop=True,
                                        tile_position=(0, hof))
                return last
            TMM = te(tmm, waits=[KTR] + (PD.get("gn", []) if part == 0 else []))
            def supd(e, part=part, pt=pt):
                last = None
                for pr in range(8):
                    pair = part * 8 + pr
                    last = e.scalar_tensor_tensor(
                        out=t_sall[:, pair * HD:(pair + 1) * HD],
                        in0=t_sall[:, pair * HD:(pair + 1) * HD],
                        scalar=t_ptot[:, pair:pair + 1],
                        in1=pt[:, pr * 64:(pr + 1) * 64],
                        op0=ALU.mult, op1=ALU.add)
                return last
            sup_marks.append(ve(supd, waits=[TMM]))
        SUPD = s_v.at()
        if sum_phase:
            return dict(supd=SUPD, yev=[SUPD])
        amk = []
        for rnd in range(8):
            bank = p_a if rnd % 2 == 0 else p_b
            def af(e, rnd=rnd, bank=bank):
                last = None
                for j in range(4):
                    head = rnd * 4 + j
                    pair, hh_ = head // 2, head % 2
                    rte = t_rtE if hh_ == 0 else t_rtO
                    last = e.matmul(bank[:, j * 128:(j + 1) * 128],
                                    t_kt[:, pair, :],
                                    rte[:, pair, :],
                                    start=True, stop=True)
                return last
            wts = [VEL]
            if rnd >= 2:
                wts.append(amk[rnd - 2])
            AMM_ = te(af, waits=wts)
            def amev(e, rnd=rnd, bank=bank):
                return e.tensor_tensor(out=t_am[:, rnd * 4:(rnd + 1) * 4, :],
                                       in0=bank[:, 0:512]
                                       .rearrange("p (c l) -> p c l", c=4),
                                       in1=t_mst[:, :]
                                       .rearrange("p (c l) -> p c l", c=4),
                                       op=ALU.mult)
            amk.append(ve(amev, waits=[AMM_]))
        AMDONE = s_v.at()
        def emm(e):
            last = None
            for cc in range(NCC):
                last = e.matmul(p_m1[0:32, 0:128], t_ect[:, cc, :], t_rku[:, cc, :],
                                start=(cc == 0), stop=(cc == NCC - 1))
            return last
        EMM = te(emm, waits=[RKU])
        RKH = ve(lambda e: e.tensor_copy(out=t_rkuhb[:, :], in_=p_m1[0:32, 0:128]),
                 waits=[EMM])
        last_grp_ev = {}
        for grp in range(4):
            py = p_y1 if grp % 2 == 0 else p_y2
            pu = p_a if grp % 2 == 0 else p_b
            def bcg(e, grp=grp, pu=pu):
                last = None
                for j in range(4):
                    cc = grp * 4 + j
                    last = e.matmul(pu[:, j * 128:(j + 1) * 128],
                                    t_ecttp[:, cc * 128:(cc + 1) * 128],
                                    t_rkuhb[:, :], start=True, stop=True)
                return last
            wts = [RKH, ECTTP, AMDONE]
            if grp >= 1:
                wts.append(last_grp_ev[grp - 1])
            BCG = te(bcg, waits=wts)
            def ydg(e, grp=grp, pu=pu):
                return e.tensor_tensor(out=t_ev[:, grp % 2, :]
                                       .rearrange("p (c l) -> p c l", c=4),
                                       in0=pu[:, 0:512]
                                       .rearrange("p (c l) -> p c l", c=4),
                                       in1=t_v[:, grp * 4:(grp + 1) * 4, c0:c0 + LCH],
                                       op=ALU.mult)
            YDG = ve(ydg, waits=[BCG, PD["v"]])
            gmk = []
            for pr in range(4):
                pair = grp * 4 + pr
                def ymm(e, pair=pair, pr=pr, py=py):
                    last = None
                    for hh_ in range(2):
                        hof = hh_ * 64
                        rte = t_rtE if hh_ == 0 else t_rtO
                        e.matmul(py[hof:hof + 64, pr * 128:(pr + 1) * 128],
                                 t_sbf[:, pair * HD:(pair + 1) * HD],
                                 rte[:, pair, :],
                                 start=True, stop=False,
                                 tile_position=(0, hof))
                        last = e.matmul(py[hof:hof + 64, pr * 128:(pr + 1) * 128],
                                        t_vtr[:, pair, hof:hof + 64],
                                        t_am[:, 2 * pair + hh_, :],
                                        start=False, stop=True,
                                        tile_position=(0, hof))
                    return last
                wts2 = [AMDONE, BCG]
                if grp >= 2 and pr == 0:
                    wts2.append(last_grp_ev[grp - 2])
                gmk.append(te(ymm, waits=wts2))
            def yev(e, grp=grp, py=py):
                return e.tensor_tensor(out=t_y[:, grp * 4:(grp + 1) * 4, c0:c0 + LCH],
                                       in0=py[:, 0:512]
                                       .rearrange("p (c l) -> p c l", c=4),
                                       in1=t_ev[:, grp % 2, :]
                                       .rearrange("p (c l) -> p c l", c=4),
                                       op=ALU.add)
            last_grp_ev[grp] = ve(yev, waits=[gmk[-1], YDG])
        return dict(supd=SUPD, yev=[last_grp_ev[3]])

    t_ecttp = SB("t_ecttp", (32, NCC * 128), BF16)
    d_ecttp = DI("ecttp", (32, NCC * 128), BF16)
    dma("sync", t_ecttp[:, :], d_ecttp[:, :])
    ECTTP = s_dx.at()

    def post_stage(toff, PD, YEV, first_use_o):
        for cc in range(NCC):
            def sqf(e, cc=cc):
                return e.activation(out=t_sqb[:, cc % 2, :], in_=t_y[:, cc, :],
                                    func=AF.Square)
            w = list(YEV) if cc < 2 else [s_t.at()]
            smk = sc(sqf, waits=w)
            def mmf(e, cc=cc):
                e.matmul(p_t1[0:32, 0:512], t_ect[:, cc, :], t_y[:, cc, :],
                         start=(cc == 0), stop=(cc == NCC - 1))
                return e.matmul(p_t2[0:32, 0:512], t_ect[:, cc, :],
                                t_sqb[:, cc % 2, :],
                                start=(cc == 0), stop=(cc == NCC - 1))
            te(mmf, waits=[smk] + (list(YEV) if cc == 0 else []))
        GN_MM = s_t.at()
        def gmath(e):
            e.tensor_scalar(out=t_ev[0:32, 0, :], in0=p_t1[0:32, 0:512],
                            scalar1=1.0 / HD, scalar2=None, op0=ALU.mult)
            e.tensor_tensor(out=t_ev[0:32, 1, :], in0=t_ev[0:32, 0, :],
                            in1=t_ev[0:32, 0, :], op=ALU.mult)
            return e.scalar_tensor_tensor(out=t_ev[0:32, 1, :],
                                          in0=p_t2[0:32, 0:512],
                                          scalar=1.0 / HD, in1=t_ev[0:32, 1, :],
                                          op0=ALU.mult, op1=ALU.subtract)
        gm = ve(gmath, waits=[GN_MM])
        def gaf(e):
            e.activation(out=t_ev[0:32, 1, :], in_=t_ev[0:32, 1, :],
                         func=AF.Ln, bias=t_cbias[0:32, 1:2])
            return e.activation(out=t_ev[0:32, 1, :], in_=t_ev[0:32, 1, :],
                                func=AF.Exp, scale=-0.5)
        ga_ = sc(gaf, waits=[gm])
        def grows(e):
            e.tensor_copy(out=t_rs33[0:32, :], in_=t_ev[0:32, 1, :])
            e.tensor_tensor(out=t_ev[0:32, 0, :], in0=t_ev[0:32, 0, :],
                            in1=t_ev[0:32, 1, :], op=ALU.mult)
            return e.tensor_copy(out=t_mu33[0:32, :], in_=t_ev[0:32, 0, :])
        gr = ve(grows, waits=[ga_])
        prev_mm = {}
        for cc in range(NCC):
            def bmm(e, cc=cc):
                e.matmul(p_m1[:, 0:512], t_egb[:, cc * 128:(cc + 1) * 128],
                         t_rs33[:, :], start=True, stop=True)
                return e.matmul(p_b[:, 0:512], t_egb[:, cc * 128:(cc + 1) * 128],
                                t_mu33[:, :], start=True, stop=True)
            wts = [gr, EGB] if cc == 0 else [prev_mm[cc - 1]]
            bm = te(bmm, waits=wts)
            def yng(e, cc=cc):
                e.tensor_tensor(out=t_ev[:, 0, :], in0=t_y[:, cc, :], in1=p_m1[:, 0:512],
                                op=ALU.mult)
                e.tensor_tensor(out=t_ev[:, 0, :], in0=t_ev[:, 0, :], in1=p_b[:, 0:512],
                                op=ALU.subtract)
                return e.tensor_tensor(out=t_y[:, cc, :], in0=t_ev[:, 0, :],
                                       in1=t_g[:, cc, :], op=ALU.mult)
            prev_mm[cc] = ve(yng, waits=[bm, PD["g"]])
        YNG = s_v.at()
        wrr = d_wproj["o"].ap().rearrange("(cc p) o -> p cc o", p=128)
        xj = {}
        x2w = {}
        ev_marks = []
        for pc in range(NCC):
            wts = list(wbuf_last_use.get(pc % 2, []))
            if first_use_o:
                src = wrr[:, :, pc * 128:(pc + 1) * 128]
            else:
                src = s_wc["o"][pc]
            piece = dma("gpsimd", t_wbuf[:, pc % 2, :, :], src,
                        waits=wts + ([] if first_use_o
                                     else [CACHE_DONE["o"]]))
            cw = None
            if first_use_o:
                cw = dma("sync", s_wc["o"][pc], t_wbuf[:, pc % 2, :, :],
                         waits=[piece])
            xj[pc] = dma("sync",
                         t_x32[:, pc % 2, :],
                         d_x.ap().rearrange("(cc p) t -> p cc t", p=128)
                         [:, pc, toff:toff + TT],
                         waits=[x2w[pc - 2]] if pc >= 2 else list(YEV))
            bank = [p_a, p_b, p_y1, p_y2][pc % 4]
            def pmm(e, pc=pc, bank=bank):
                last = None
                for fi in range(NCC):
                    last = e.matmul(bank[:, 0:512], t_wbuf[:, pc % 2, fi, :],
                                    t_y[:, fi, :], start=(fi == 0),
                                    stop=(fi == NCC - 1))
                return last
            wts2 = [piece, YNG]
            if pc >= 4:
                wts2.append(ev_marks[pc - 4])
            mmdone = te(pmm, waits=wts2)
            wbuf_last_use[pc % 2] = [mmdone] + ([cw] if cw else [])
            def evf(e, pc=pc, bank=bank):
                return e.scalar_tensor_tensor(out=t_ev[:, pc % 2, :],
                                              in0=bank[:, 0:512],
                                              scalar=modc(2, pc),
                                              in1=t_x32[:, pc % 2, :],
                                              op0=ALU.mult, op1=ALU.add)
            evm = ve(evf, waits=[mmdone, xj[pc]])
            ev_marks.append(evm)
            x2w[pc] = dma("sync",
                          s_x2.ap().rearrange("(cc p) t -> p cc t", p=128)
                          [:, pc, toff:toff + TT],
                          t_ev[:, pc % 2, :], waits=[evm])
        if first_use_o:
            CACHE_DONE["o"] = s_dc.at()
        GN_VE = s_v.at()
        return [x2w[NCC - 1], x2w[NCC - 2]], GN_VE

    def finish_early():
        dm = dma("sync", d_out[0:128, 0:512], t_ev[:, 0, :], sem=s_o,
                 waits=[s_v.at(), s_t.at(), s_s.at(), s_g.at(), s_dw.at(),
                        s_dc.at()])
        b.emit("sync", lambda e: None,
               waits=[s_o.at(), s_dw.at(), s_dx.at(), s_dc.at()], incs=[])
        b.finish()
        return nc, TAP

    def attn_phase(xin, n_tiles, sum_phase, entry_tail, entry_yev):
        prev_tail = list(entry_tail)
        post_marks = []
        prev_yev = list(entry_yev)
        gn_mark = []
        for it in range(n_tiles):
            toff = it * TT
            marks = ln_h_stage(xin, toff, it == 0, sum_phase, prev_tail,
                               last_tile=(it == n_tiles - 1))
            ATANH = maa_stage(marks)
            PD = {"gn": gn_mark, "lntap": marks["lntap"]}
            XMw = xm_build(0, ATANH)
            if it == n_tiles - 1 and not sum_phase and "a1" in TAP:
                PD["lntap"].append(dma("sync", TAP["a1"].ap(), t_a1[0:96, :],
                                       sem=s_o, waits=[ATANH]))
            if it == n_tiles - 1 and not sum_phase and "xmw" in TAP:
                PD["lntap"].append(dma("sync", TAP["xmw"].ap(), t_xm[:, :, :],
                                       sem=s_o, waits=[XMw]))
            PD["w"] = decay_proj(XMw)
            XMk = xm_build(1, ATANH, [PD["w"]])
            first_kv = sum_phase and it == 0
            PD["k"] = full_proj("k", t_k, XMk, first_use=first_kv)
            XMv = xm_build(2, ATANH, [PD["k"]])
            PD["v"] = full_proj("v", t_v, XMv, first_use=first_kv)
            if not sum_phase:
                first_rg = it == 0
                XMr = xm_build(3, ATANH, [PD["v"]])
                PD["r"] = full_proj("r", t_r, XMr, first_use=first_rg)
                XMg = xm_build(4, ATANH, [PD["r"]])
                if it == n_tiles - 1 and "xmg" in TAP:
                    PD["lntap"].append(dma("sync", TAP["xmg"].ap(),
                                           t_xm[:, :, :], sem=s_o, waits=[XMg]))
                PD["g"] = full_proj("g", t_g, XMg, silu=True, first_use=first_rg)
            if not sum_phase and stop_after == "mproj":
                return post_marks, prev_tail, prev_yev
            chunk_marks = None
            for ch in range(NCH):
                chunk_marks = wkv_chunk(ch * LCH, sum_phase, PD, prev_yev)
                prev_yev = chunk_marks["yev"]
            if not sum_phase and stop_after == "mwkv":
                return post_marks, prev_tail, prev_yev
            if sum_phase:
                prev_tail = [chunk_marks["supd"]]
                if it == n_tiles - 1:
                    def smsk(e):
                        return e.tensor_scalar(out=t_sall[:, :], in0=t_sall[:, :],
                                               scalar1=t_smask[:, 0:1],
                                               scalar2=None, op0=ALU.mult)
                    SM = ve(smsk, waits=[chunk_marks["supd"]])
                    prev_tail = [SM]
                    prev_yev = [SM]
            else:
                pm, gnv = post_stage(toff, PD, chunk_marks["yev"],
                                     first_use_o=(it == 0))
                gn_mark = [gnv]
                post_marks.append(pm)
                prev_tail = [chunk_marks["supd"], pm[0]]
                if it == n_tiles - 1:
                    for nm, buf in [("r", t_r),
                                    ("k", t_k), ("v", t_v), ("g", t_g),
                                    ("tanhd", t_tanhd),
                                    ("d", t_d), ("cs", t_cs), ("am", t_am),
                                    ("sall", t_sall), ("y", t_y)]:
                        if nm in TAP:
                            if nm == "h":
                                src = buf[:, :, 1:TT + 1]
                            elif nm == "sall":
                                src = buf[:, :]
                            elif nm == "tanhd":
                                src = buf[0:65, :]
                            else:
                                src = buf[:, :, :]
                            wq_ = [s_v.at(), s_s.at(), s_t.at()]
                            dma("sync", TAP[nm].ap(), src, sem=s_o, waits=wq_)
            emit_wmean(48)
        return post_marks, prev_tail, prev_yev

    if stop_after == "p1":
        return finish_early()
    entry_t, entry_y = [MOD_DONE], []
    if n_sum_tiles > 0:
        _, entry_t, entry_y = attn_phase(d_xs, n_sum_tiles, True,
                                         entry_t, entry_y)
    if stop_after == "sum":
        return finish_early()
    post_marks, _, _ = attn_phase(d_x, n_main_tiles, False, entry_t, entry_y)
    if stop_after in ("attn", "mproj", "mwkv"):
        return finish_early()
    QS_MARK, AW_MARK = wmean_finalize()

    def ffn_b0():
        B0_last = []
        for sl in range(NSL):
            t0 = sl * 512
            x2r = s_x2.ap().rearrange("(cc p) t -> p cc t", p=128)[:, :, t0:t0 + 512]
            if B0_last:
                w0 = list(B0_last)
            else:
                w0 = list(post_marks[-1]) + [s_v.at(), s_s.at(), s_t.at(),
                                             s_g.at(), s_o.at()]
            XL32 = dma("sync", t_x2f[:, :, :], x2r, waits=w0)
            for cc in range(NCC):
                def sqf(e, cc=cc):
                    return e.activation(out=t_ev[:, cc % 2, :], in_=t_x2f[:, cc, :],
                                        func=AF.Square)
                w = [XL32] if cc < 2 else [s_t.at()]
                smk = sc(sqf, waits=w)
                def mmf(e, cc=cc):
                    e.matmul(p_m1[0:1, 0:512], t_onesf[:, :], t_x2f[:, cc, :],
                             start=(cc == 0), stop=(cc == NCC - 1))
                    return e.matmul(p_a[0:1, 0:512], t_onesf[:, :],
                                    t_ev[:, cc % 2, :],
                                    start=(cc == 0), stop=(cc == NCC - 1))
                te(mmf, waits=[smk] + ([XL32] if cc == 0 else []))
            ST2 = s_t.at()
            def sm2(e):
                e.tensor_scalar(out=t_row[0:1, 512:1024], in0=p_m1[0:1, 0:512],
                                scalar1=1.0 / C, scalar2=None, op0=ALU.mult)
                e.tensor_tensor(out=t_row[0:1, 1024:1536], in0=t_row[0:1, 512:1024],
                                in1=t_row[0:1, 512:1024], op=ALU.mult)
                return e.scalar_tensor_tensor(out=t_row[0:1, 0:512],
                                              in0=p_a[0:1, 0:512],
                                              scalar=1.0 / C,
                                              in1=t_row[0:1, 1024:1536],
                                              op0=ALU.mult, op1=ALU.subtract)
            s2 = ve(sm2, waits=[ST2])
            def r2f(e):
                e.activation(out=t_row[0:1, 0:512], in_=t_row[0:1, 0:512],
                             func=AF.Ln, bias=t_cbias[0:1, 0:1])
                return e.activation(out=t_row[0:1, 0:512], in_=t_row[0:1, 0:512],
                                    func=AF.Exp, scale=-0.5)
            r2 = sc(r2f, waits=[s2])
            def bc2(e):
                e.matmul(p_m1[:, 0:512], t_onesrf[:, :], t_row[0:1, 0:512],
                         start=True, stop=True)
                return e.matmul(p_a[:, 0:512], t_onesrf[:, :], t_row[0:1, 512:1024],
                                start=True, stop=True)
            b2 = te(bc2, waits=[r2])
            def zmx(e):
                e.memset(t_x32[:, :, :], 0.0)
                return e.memset(t_sqb[:, :, :], 0.0)
            zm = ve(zmx)
            for cc in range(NCC):
                def h2f(e, cc=cc):
                    e.tensor_tensor(out=t_ev[:, 0, :], in0=t_x2f[:, cc, :],
                                    in1=p_a[:, 0:512], op=ALU.subtract)
                    e.tensor_tensor(out=t_ev[:, 0, :], in0=t_ev[:, 0, :],
                                    in1=p_m1[:, 0:512], op=ALU.mult)
                    e.tensor_scalar(out=t_x2f[:, cc, :], in0=t_ev[:, 0, :],
                                    scalar1=modc(4, cc), scalar2=modc(3, cc),
                                    op0=ALU.mult, op1=ALU.add)
                    e.tensor_scalar(out=t_ev[:, 1, :], in0=t_x2f[:, cc, :],
                                    scalar1=pfc("g1a", cc), scalar2=None,
                                    op0=ALU.mult)
                    e.tensor_tensor(out=t_x32[:, 0, :], in0=t_ev[:, 1, :],
                                    in1=t_x32[:, 0, :], op=ALU.max)
                    e.tensor_tensor(out=t_sqb[:, 0, :], in0=t_ev[:, 1, :],
                                    in1=t_sqb[:, 0, :], op=ALU.min)
                    e.tensor_scalar(out=t_ev[:, 1, :], in0=t_x2f[:, cc, :],
                                    scalar1=pfc("g2a", cc), scalar2=None,
                                    op0=ALU.mult)
                    e.tensor_tensor(out=t_x32[:, 1, :], in0=t_ev[:, 1, :],
                                    in1=t_x32[:, 1, :], op=ALU.max)
                    return e.tensor_tensor(out=t_sqb[:, 1, :], in0=t_ev[:, 1, :],
                                           in1=t_sqb[:, 1, :], op=ALU.min)
                ve(h2f, waits=[b2, zm, MOD_DONE, G1A] if cc == 0 else [])
            H2 = s_v.at()
            for cc in range(NCC):
                def sq2(e, cc=cc):
                    return e.activation(out=t_ev[:, cc % 2, :], in_=t_x2f[:, cc, :],
                                        func=AF.Square)
                w = [H2] if cc < 2 else [s_t.at()]
                sk = sc(sq2, waits=w)
                def mm2(e, cc=cc):
                    return e.matmul(p_b[0:1, 0:512], t_onesf[:, :],
                                    t_ev[:, cc % 2, :],
                                    start=(cc == 0), stop=(cc == NCC - 1))
                te(mm2, waits=[sk])
            RMS_MM = s_t.at()
            TRM = {}
            for q in range(2):
                def cmf(e, q=q):
                    e.tensor_scalar(out=t_a1[:, :], in0=t_sqb[:, q, :],
                                    scalar1=-1.0, scalar2=None, op0=ALU.mult)
                    return e.tensor_tensor(out=t_sqb[:, q, :],
                                           in0=t_x32[:, q, :],
                                           in1=t_a1[:, 0:512], op=ALU.max)
                cm = ve(cmf, waits=[H2])
                trs = []
                for tb in range(4):
                    def trf(e, q=q, tb=tb):
                        return e.transpose(p_m2[:, tb * 128:(tb + 1) * 128],
                                           t_sqb[:, q, tb * 128:(tb + 1) * 128],
                                           t_id[:, :])
                    trs.append(te(trf, waits=[cm]))
                def rmx(e, q=q):
                    last = None
                    for tb in range(4):
                        last = e.reduce_max(out=t_small[:, 48 + tb:49 + tb],
                                            in_=p_m2[:, tb * 128:(tb + 1) * 128],
                                            axis=AX)
                    return last
                rm = ve(rmx, waits=[trs[-1]])
                cpk = ve(lambda e: e.tensor_copy(out=t_sqb[:, 0, 0:4],
                                                 in_=t_small[:, 48:52]),
                         waits=[rm])
                def tb2f(e):
                    last = None
                    for g in range(4):
                        last = e.matmul(p_m1[0:1, g * 128:(g + 1) * 128],
                                        t_sqb[:, 0, g:g + 1], t_id[:, :],
                                        start=True, stop=True)
                    return last
                tb2 = te(tb2f, waits=[cpk])
                def rrow(e, q=q):
                    return e.tensor_copy(
                        out=t_row[0:1, (1 + q) * 512:(2 + q) * 512],
                        in_=p_m1[0:1, 0:512])
                TRM[q] = ve(rrow, waits=[tb2])
            qv = ve(lambda e: e.tensor_scalar(out=t_row[0:1, 0:512],
                                              in0=p_b[0:1, 0:512],
                                              scalar1=1.0 / C, scalar2=None,
                                              op0=ALU.mult), waits=[RMS_MM])
            def qaf(e):
                e.activation(out=t_row[0:1, 0:512], in_=t_row[0:1, 0:512],
                             func=AF.Ln, bias=t_cbias[0:1, 2:3])
                return e.activation(out=t_row[0:1, 0:512], in_=t_row[0:1, 0:512],
                                    func=AF.Exp, scale=-0.5)
            qa = sc(qaf, waits=[qv])
            def srow(e, t0=t0):
                last = None
                for q in range(2):
                    e.tensor_tensor(out=t_row[0:1, (1 + q) * 512:(2 + q) * 512],
                                    in0=t_row[0:1, (1 + q) * 512:(2 + q) * 512],
                                    in1=t_row[0:1, 0:512], op=ALU.mult)
                    e.tensor_scalar(out=t_row[0:1, (1 + q) * 512:(2 + q) * 512],
                                    in0=t_row[0:1, (1 + q) * 512:(2 + q) * 512],
                                    scalar1=QCLIP, scalar2=None, op0=ALU.max)
                    e.tensor_scalar(out=(t_inv1 if q == 0 else t_inv2)
                                    [0:1, t0:t0 + 512],
                                    in0=t_row[0:1, (1 + q) * 512:(2 + q) * 512],
                                    scalar1=t_small[0:1, 69 + q:70 + q],
                                    scalar2=-1.0 / 127.0,
                                    op0=ALU.mult, op1=ALU.mult)
                    e.reciprocal(out=t_row[0:1, (1 + q) * 512:(2 + q) * 512],
                                 in_=t_row[0:1, (1 + q) * 512:(2 + q) * 512])
                    e.tensor_tensor(out=t_row[0:1, (1 + q) * 512:(2 + q) * 512],
                                    in0=t_row[0:1, 0:512],
                                    in1=t_row[0:1, (1 + q) * 512:(2 + q) * 512],
                                    op=ALU.mult)
                    last = e.tensor_scalar(
                        out=t_row[0:1, (1 + q) * 512:(2 + q) * 512],
                        in0=t_row[0:1, (1 + q) * 512:(2 + q) * 512],
                        scalar1=127.0, scalar2=None, op0=ALU.mult)
                return last
            sr = ve(srow, waits=[qa, TRM[0], TRM[1], AW_MARK])
            def bq(e):
                e.matmul(p_m1[:, 0:512], t_onesrf[:, :], t_row[0:1, 512:1024],
                         start=True, stop=True)
                return e.matmul(p_a[:, 0:512], t_onesrf[:, :],
                                t_row[0:1, 1024:1536], start=True, stop=True)
            bq_ = te(bq, waits=[sr])
            last_xq = None
            for cc in range(NCC):
                def xqf(e, cc=cc, sl=sl, t0=t0):
                    e.scalar_tensor_tensor(out=t_ev[:, 0, :], in0=t_x2f[:, cc, :],
                                           scalar=pfc("g1", cc), in1=p_m1[:, 0:512],
                                           op0=ALU.mult, op1=ALU.mult)
                    e.tensor_scalar(out=t_xq1[:, cc, t0:t0 + 512],
                                    in0=t_ev[:, 0, :], scalar1=RND_C, scalar2=RND_C,
                                    op0=ALU.add, op1=ALU.subtract)
                    e.scalar_tensor_tensor(out=t_ev[:, 0, :], in0=t_x2f[:, cc, :],
                                           scalar=pfc("g2", cc), in1=p_a[:, 0:512],
                                           op0=ALU.mult, op1=ALU.mult)
                    return e.tensor_scalar(out=t_xq2[:, cc, t0:t0 + 512],
                                           in0=t_ev[:, 0, :], scalar1=RND_C,
                                           scalar2=RND_C,
                                           op0=ALU.add, op1=ALU.subtract)
                last_xq = ve(xqf, waits=[bq_] if cc == 0 else [])
            B0_last = [last_xq]
        return s_v.at()

    if stop_after == "wm":
        return finish_early()
    B0_DONE = ffn_b0()
    if stop_after == "b0":
        return finish_early()
    if "cells2" in TAP:
        dma("sync", TAP["cells2"].ap(), t_small[0:1, 48:72], sem=s_o,
            waits=[B0_DONE])
    if "xq1" in TAP:
        dma("sync", TAP["xq1"].ap(), t_xq1[:, :, :], sem=s_o, waits=[B0_DONE])
    if "xq2" in TAP:
        dma("sync", TAP["xq2"].ap(), t_xq2[:, :, :], sem=s_o, waits=[B0_DONE])
    if "inv1" in TAP:
        dma("sync", TAP["inv1"].ap(), t_inv1[0:1, 0:T_MAIN], sem=s_o,
            waits=[B0_DONE])
    if "inv2" in TAP:
        dma("sync", TAP["inv2"].ap(), t_inv2[0:1, 0:T_MAIN], sem=s_o,
            waits=[B0_DONE])

    def ffn_y12():
        bm = []
        for q in range(NQ):
            def bcf(e, q=q):
                e.matmul(p_m1[:, 0:512], t_onesr[:, :],
                         t_inv1[0:1, q * 512:(q + 1) * 512], start=True, stop=True)
                return e.matmul(p_a[:, 0:512], t_onesr[:, :],
                                t_inv2[0:1, q * 512:(q + 1) * 512],
                                start=True, stop=True)
            m1 = te(bcf, waits=[B0_DONE] if q == 0 else [bm[-1]])
            def bev(e, q=q):
                e.tensor_copy(out=t_bc1[:, q * 512:(q + 1) * 512], in_=p_m1[:, 0:512])
                return e.tensor_copy(out=t_bc2[:, q * 512:(q + 1) * 512],
                                     in_=p_a[:, 0:512])
            bm.append(ve(bev, waits=[m1]))
        BC12 = bm[-1]
        ZM3 = ve(lambda e: e.memset(t_m3[:, :], 0.0))

        q1_done = {}
        q2_done = {}
        mm_last = {}
        hmw = {}
        sqm = {}
        m3m = {}
        prev_e1 = {0: None, 1: None}
        prev_e2 = {0: None, 1: None}
        prev_hm2 = {0: None, 1: None}
        prev_sl = None
        prev_hm = None
        prev_m3 = None
        prev_sq = None
        prev_ssmm = None
        prev_hmdma = None
        prev_m3sq = None
        prev_m3max = None
        w1r = d_w1.ap().rearrange("(cc p) m -> p cc m", p=128)
        w2r = d_w2.ap().rearrange("(cc p) m -> p cc m", p=128)
        for blk in range(NOC):
            wts1 = [q1_done[blk - 1], q2_done[blk - 1]] if blk \
                else [QS_MARK, B0_DONE]
            l1 = dma("gpsimd", t_wf32[:, :, :],
                     w1r[:, :, blk * 128:(blk + 1) * 128], waits=wts1)
            def qa1(e, blk=blk):
                return e.tensor_scalar(out=t_wq[:, 0, :, :], in0=t_wf32[:, :, :],
                                       scalar1=t_qs[:, 0:1], scalar2=None,
                                       op0=ALU.is_ge)
            m = ve(qa1, waits=[l1] + ([mm_last[blk - 1]] if blk else []))
            def qb1(e, blk=blk):
                return e.scalar_tensor_tensor(out=t_wq[:, 0, :, :],
                                              in0=t_wf32[:, :, :],
                                              scalar=t_qs[:, 3:4],
                                              in1=t_wq[:, 0, :, :],
                                              op0=ALU.is_le, op1=ALU.subtract)
            q1_done[blk] = ve(qb1, waits=[m])
            l2 = dma("gpsimd", t_wf32[:, :, :],
                     w2r[:, :, blk * 128:(blk + 1) * 128], waits=[q1_done[blk]])
            def qa2(e, blk=blk):
                return e.tensor_scalar(out=t_wq[:, 1, :, :], in0=t_wf32[:, :, :],
                                       scalar1=t_qs[:, 1:2], scalar2=None,
                                       op0=ALU.is_ge)
            m2_ = ve(qa2, waits=[l2] + ([mm_last[blk - 1]] if blk else []))
            def qb2(e, blk=blk):
                return e.scalar_tensor_tensor(out=t_wq[:, 1, :, :],
                                              in0=t_wf32[:, :, :],
                                              scalar=t_qs[:, 4:5],
                                              in1=t_wq[:, 1, :, :],
                                              op0=ALU.is_le, op1=ALU.subtract)
            q2_done[blk] = ve(qb2, waits=[m2_])
            for q in range(NQ):
                py = p_y1 if q % 2 == 0 else p_y2
                pz = p_a if q % 2 == 0 else p_b
                def mmy1(e, q=q, py=py):
                    last = None
                    for ic in range(NCC):
                        last = e.matmul(py[:, 0:512], t_wq[:, 0, ic, :],
                                        t_xq1[:, ic, q * 512:(q + 1) * 512],
                                        start=(ic == 0), stop=(ic == NCC - 1))
                    return last
                wA = [q1_done[blk]]
                if blk and q < 2:
                    wA.append(sqm[(blk - 1, NQ - 1)])
                if prev_e1[q % 2] is not None:
                    wA.append(prev_e1[q % 2])
                mA = te(mmy1, waits=wA)
                def mmy2(e, q=q, pz=pz):
                    last = None
                    for ic in range(NCC):
                        last = e.matmul(pz[:, 0:512], t_wq[:, 1, ic, :],
                                        t_xq2[:, ic, q * 512:(q + 1) * 512],
                                        start=(ic == 0), stop=(ic == NCC - 1))
                    return last
                wB = [q2_done[blk]]
                if prev_e2[q % 2] is not None:
                    wB.append(prev_e2[q % 2])
                mB = te(mmy2, waits=wB)
                mm_last[blk] = mB
                def e1f(e, q=q, py=py):
                    return e.tensor_tensor(out=t_ev[:, 0, :], in0=py[:, 0:512],
                                           in1=t_bc1[:, q * 512:(q + 1) * 512],
                                           op=ALU.mult)
                wE1 = [mA, BC12]
                if prev_sl is not None:
                    wE1.append(prev_sl)
                if prev_m3 is not None:
                    wE1.append(prev_m3)
                E1 = ve(e1f, waits=wE1)
                wSL = [E1]
                if prev_hm is not None:
                    wSL.append(prev_hm)
                SL = sc(lambda e: e.activation(out=t_ev[:, 1, :], in_=t_ev[:, 0, :],
                                               func=AF.Silu), waits=wSL)
                prev_sl = SL
                def e2f(e, q=q, pz=pz):
                    return e.tensor_tensor(out=t_x32[:, q % 2, :], in0=pz[:, 0:512],
                                           in1=t_bc2[:, q * 512:(q + 1) * 512],
                                           op=ALU.mult)
                wE2 = [mB, BC12]
                if prev_hm2[q % 2] is not None:
                    wE2.append(prev_hm2[q % 2])
                E2 = ve(e2f, waits=wE2)
                prev_e1[q % 2] = E1
                prev_e2[q % 2] = E2
                def hmf(e, q=q, blk=blk):
                    return e.tensor_tensor(out=t_hmst[:, :], in0=t_ev[:, 1, :],
                                           in1=t_x32[:, q % 2, :], op=ALU.mult)
                wH = [SL, E2]
                if prev_hmdma is not None:
                    wH.append(prev_hmdma)
                if prev_sq is not None:
                    wH.append(prev_sq)
                if prev_m3 is not None:
                    wH.append(prev_m3)
                HM = ge(hmf, waits=wH)
                prev_hm = HM
                prev_hm2[q % 2] = HM
                def sqf(e, q=q):
                    return e.activation(out=t_sqst[:, :], in_=t_hmst[:, :],
                                        func=AF.Square)
                SQ = sc(sqf, waits=[HM] + ([prev_ssmm] if prev_ssmm else []))
                prev_sq = SQ
                def ssmm(e, q=q, blk=blk):
                    pt = p_t1 if q < 2 else p_t2
                    r0 = (q % 2) * 32
                    return e.matmul(pt[r0:r0 + 1, 0:512], t_ones[:, :],
                                    t_sqst[:, :], start=(blk == 0),
                                    stop=(blk == NOC - 1), tile_position=(0, r0))
                sqm[(blk, q)] = te(ssmm, waits=[SQ])
                prev_ssmm = sqm[(blk, q)]
                M3A = ge(lambda e, blk=blk: e.tensor_scalar(
                    out=t_sqb[:, 1, :], in0=t_hmst[:, :],
                    scalar1=t_g3a[:, blk:blk + 1], scalar2=None, op0=ALU.mult),
                    waits=[HM, G3A] + ([prev_m3sq] if prev_m3sq else []))
                M3B = sc(lambda e: e.activation(out=t_a1[:, :], in_=t_sqb[:, 1, :],
                                                func=AF.Square),
                         waits=[M3A] + ([prev_m3max] if prev_m3max else []))
                prev_m3sq = M3B
                def m3f(e, q=q):
                    return e.tensor_tensor(out=t_m3[:, q * 512:(q + 1) * 512],
                                           in0=t_a1[:, 0:512],
                                           in1=t_m3[:, q * 512:(q + 1) * 512],
                                           op=ALU.max)
                m3m[(blk, q)] = ve(m3f, waits=[M3B, ZM3])
                prev_m3max = m3m[(blk, q)]
                prev_m3 = m3m[(blk, q)]
                prev_hmdma = dma("sync",
                                 s_hm[blk * 128:(blk + 1) * 128,
                                      q * 512:(q + 1) * 512],
                                 t_hmst[:, :], waits=[HM])
                hmw[(blk, q)] = prev_hmdma
        return hmw[(NOC - 1, NQ - 1)], sqm, m3m

    HMW_LAST, sqm_marks, m3m_marks = ffn_y12()
    if stop_after == "y12":
        return finish_early()

    def ffn_y3_prep():
        def ssrow(e):
            last = None
            for q in range(NQ):
                pt = p_t1 if q < 2 else p_t2
                r0 = (q % 2) * 32
                last = e.tensor_copy(out=t_s3row[0:1, q * 512:(q + 1) * 512],
                                     in_=pt[r0:r0 + 1, 0:512])
            return last
        SS = ve(ssrow, waits=[sqm_marks[(NOC - 1, q)] for q in range(NQ)])
        def rs3f(e):
            e.activation(out=t_s3row[0:1, 0:T_MAIN], in_=t_s3row[0:1, 0:T_MAIN],
                         func=AF.Ln, bias=t_cbias[0:1, 2:3], scale=1.0 / DFF)
            return e.activation(out=t_s3row[0:1, 0:T_MAIN],
                                in_=t_s3row[0:1, 0:T_MAIN],
                                func=AF.Exp, scale=-0.5)
        RS3 = sc(rs3f, waits=[SS])
        trm = []
        for g2_ in range(T_MAIN // 512):
            cm = ve(lambda e, g2_=g2_: e.tensor_copy(out=t_sqb[:, 0, :],
                                                     in_=t_m3[:, g2_ * 512:
                                                              (g2_ + 1) * 512]),
                    waits=[m3m_marks[(NOC - 1, g2_)]] + ([HMW_LAST] if g2_ == 0 else []))
            trs = []
            for tb in range(4):
                def trf(e, tb=tb):
                    return e.transpose(p_m2[:, tb * 128:(tb + 1) * 128],
                                       t_sqb[:, 0, tb * 128:(tb + 1) * 128],
                                       t_id[:, :])
                trs.append(te(trf, waits=[cm]))
            def rmx(e):
                last = None
                for tb in range(4):
                    last = e.reduce_max(out=t_small[:, 48 + tb:49 + tb],
                                        in_=p_m2[:, tb * 128:(tb + 1) * 128],
                                        axis=AX)
                return last
            rm = ve(rmx, waits=[trs[-1]])
            cpk = ve(lambda e: e.tensor_copy(out=t_sqb[:, 0, 0:4],
                                             in_=t_small[:, 48:52]), waits=[rm])
            def tb2f(e):
                last = None
                for g in range(4):
                    last = e.matmul(p_m1[0:1, g * 128:(g + 1) * 128],
                                    t_sqb[:, 0, g:g + 1], t_id[:, :],
                                    start=True, stop=True)
                return last
            tb2 = te(tb2f, waits=[cpk])
            def rrow(e, g2_=g2_):
                return e.tensor_copy(out=t_mx3row[0:1, g2_ * 512:(g2_ + 1) * 512],
                                     in_=p_m1[0:1, 0:512])
            trm.append(ve(rrow, waits=[tb2]))
        m3tap = []
        if "m3" in TAP:
            m3tap.append(dma("sync", TAP["m3"].ap(), t_m3[:, 0:T_MAIN],
                             sem=s_o, waits=trm))
        SQRT3 = sc(lambda e: e.activation(out=t_mx3row[0:1, 0:T_MAIN],
                                          in_=t_mx3row[0:1, 0:T_MAIN],
                                          func=AF.Sqrt), waits=trm)
        def r3(e):
            e.tensor_tensor(out=t_mx3row[0:1, 0:T_MAIN], in0=t_mx3row[0:1, 0:T_MAIN],
                            in1=t_s3row[0:1, 0:T_MAIN], op=ALU.mult)
            e.tensor_scalar(out=t_mx3row[0:1, 0:T_MAIN], in0=t_mx3row[0:1, 0:T_MAIN],
                            scalar1=QCLIP, scalar2=None, op0=ALU.max)
            e.tensor_scalar(out=t_inv3[0:1, 0:T_MAIN], in0=t_mx3row[0:1, 0:T_MAIN],
                            scalar1=t_small[0:1, 71:72], scalar2=-1.0 / 127.0,
                            op0=ALU.mult, op1=ALU.mult)
            last = None
            for qq in range(T_MAIN // 512):
                sl_ = slice(qq * 512, (qq + 1) * 512)
                e.tensor_copy(out=t_row[0:1, 0:512], in_=t_mx3row[0:1, sl_])
                e.reciprocal(out=t_row[0:1, 0:512], in_=t_row[0:1, 0:512])
                e.tensor_tensor(out=t_row[0:1, 512:1024],
                                in0=t_row[0:1, 0:512],
                                in1=t_s3row[0:1, sl_], op=ALU.mult)
                last = e.tensor_scalar(out=t_s3row[0:1, sl_],
                                       in0=t_row[0:1, 512:1024],
                                       scalar1=127.0, scalar2=None, op0=ALU.mult)
            return last
        return ve(r3, waits=trm + [RS3, AW_MARK, SQRT3] + m3tap)
    ROWS3 = ffn_y3_prep()
    if stop_after == "prep":
        return finish_early()
    y3tapm = []
    for nm, buf in [("s3row", t_s3row), ("mx3row", t_mx3row),
                    ("inv3", t_inv3)]:
        if nm in TAP:
            y3tapm.append(dma("sync", TAP[nm].ap(),
                              buf[0:1, 0:T_MAIN] if nm != "m3"
                              else buf[:, 0:T_MAIN],
                              sem=s_o, waits=[ROWS3]))

    def ffn_y3():
        w3r = d_w3.ap().rearrange("(dc p) c -> p dc c", p=128)
        out_r = d_out.ap().rearrange("(cc p) t -> p cc t", p=128)
        x2rr = s_x2.ap().rearrange("(cc p) t -> p cc t", p=128)
        hmr = s_hm.ap().rearrange("(oc p) t -> p oc t", p=128)
        prev_pass = [ROWS3] + y3tapm
        final_marks = []
        q_last = {}
        prev_evm = {}
        prev_q3 = None
        xq3tapm = []
        for ps in range(Y3P):
            tp0 = ps * Y3T
            NQP = Y3T // 512
            bcm = []
            for q in range(NQP):
                def bcf2(e, q=q, tp0=tp0):
                    return e.matmul(p_m1[:, 0:512], t_onesr[:, :],
                                    t_s3row[0:1, tp0 + q * 512:tp0 + (q + 1) * 512],
                                    start=True, stop=True)
                m1 = te(bcf2, waits=prev_pass if q == 0 else [bcm[-1]])
                def bcf3(e, q=q, tp0=tp0):
                    return e.matmul(p_a[:, 0:512], t_onesr[:, :],
                                    t_inv3[0:1, tp0 + q * 512:tp0 + (q + 1) * 512],
                                    start=True, stop=True)
                m2_ = te(bcf3)
                def bev(e, q=q):
                    e.tensor_copy(out=t_q3bc[:, q * 512:(q + 1) * 512],
                                  in_=p_m1[:, 0:512])
                    return e.tensor_copy(out=t_bc3[:, q * 512:(q + 1) * 512],
                                         in_=p_a[:, 0:512])
                bcm.append(ve(bev, waits=[m2_]))
            BCP = bcm[-1]
            xl = dma("gpsimd", t_xq3[:, :, 0:Y3T], hmr[:, :, tp0:tp0 + Y3T],
                     waits=prev_pass + xq3tapm)
            xq_marks = []
            for oc in range(NOC):
                def xq3f(e, oc=oc):
                    e.scalar_tensor_tensor(out=t_xq3[:, oc, 0:Y3T],
                                           in0=t_xq3[:, oc, 0:Y3T],
                                           scalar=t_g3[:, oc:oc + 1],
                                           in1=t_q3bc[:, 0:Y3T],
                                           op0=ALU.mult, op1=ALU.mult)
                    return e.tensor_scalar(out=t_xq3[:, oc, 0:Y3T],
                                           in0=t_xq3[:, oc, 0:Y3T],
                                           scalar1=RND_C, scalar2=RND_C,
                                           op0=ALU.add, op1=ALU.subtract)
                xq_marks.append(ve(xq3f, waits=[xl, BCP] if oc == 0 else []))
            XQ3 = xq_marks[-1]
            if ps == 0 and "xq3" in TAP:
                xq3tapm.append(dma("sync", TAP["xq3"].ap(), t_xq3[:, :, :],
                                   sem=s_o, waits=[XQ3]))
            for pc in range(NCC):
                mm_chain = []
                for sub in range(4):
                    pi = pc * 4 + sub
                    slot = pi % 2
                    wts = list(q_last[slot]) if slot in q_last else [ROWS3]
                    if ps == 0:
                        wld = wts + ([prev_q3] if prev_q3 is not None else [])
                        ld = dma("gpsimd", t_wf32[:, :, :],
                                 w3r[:, sub * 16:(sub + 1) * 16,
                                     pc * 128:(pc + 1) * 128], waits=wld)
                        def qa3(e, slot=slot):
                            return e.tensor_scalar(out=t_wq[:, slot, :, :],
                                                   in0=t_wf32[:, :, :],
                                                   scalar1=t_qs[:, 2:3],
                                                   scalar2=None, op0=ALU.is_ge)
                        mq = ve(qa3, waits=[ld])
                        def qb3(e, slot=slot):
                            return e.scalar_tensor_tensor(out=t_wq[:, slot, :, :],
                                                          in0=t_wf32[:, :, :],
                                                          scalar=t_qs[:, 5:6],
                                                          in1=t_wq[:, slot, :, :],
                                                          op0=ALU.is_le,
                                                          op1=ALU.subtract)
                        QD = ve(qb3, waits=[mq])
                        prev_q3 = QD
                        cw = dma("sync", s_w3q[pi], t_wq[:, slot, :, :],
                                 waits=[QD])
                        ready = QD
                    else:
                        ld = dma("gpsimd", t_wq[:, slot, :, :], s_w3q[pi],
                                 waits=wts + [CACHE3_DONE])
                        ready = ld
                    for q in range(NQP):
                        py = p_y1 if q % 2 == 0 else p_y2
                        def mmy3(e, sub=sub, slot=slot, q=q, py=py):
                            last = None
                            for dcc in range(16):
                                last = e.matmul(
                                    py[:, 0:512], t_wq[:, slot, dcc, :],
                                    t_xq3[:, sub * 16 + dcc, q * 512:(q + 1) * 512],
                                    start=(sub == 0 and dcc == 0),
                                    stop=(sub == 3 and dcc == 15))
                            return last
                        wm = [ready, XQ3]
                        if sub == 0 and prev_evm.get(q) is not None:
                            wm.append(prev_evm[q])
                        mm_chain.append(te(mmy3, waits=wm))
                    q_last[slot] = [mm_chain[-1]] + ([cw] if ps == 0 else [])
                for q in range(NQP):
                    py = p_y1 if q % 2 == 0 else p_y2
                    xld = dma("sync", t_x32[:, q % 2, :],
                              x2rr[:, pc, tp0 + q * 512:tp0 + (q + 1) * 512],
                              waits=[final_marks[-2]] if len(final_marks) >= 2
                              else [B0_DONE])
                    def evf(e, pc=pc, q=q, py=py):
                        e.tensor_tensor(out=t_ev[:, q % 2, :], in0=py[:, 0:512],
                                        in1=t_bc3[:, q * 512:(q + 1) * 512],
                                        op=ALU.mult)
                        return e.scalar_tensor_tensor(out=t_ev[:, q % 2, :],
                                                      in0=t_ev[:, q % 2, :],
                                                      scalar=modc(5, pc),
                                                      in1=t_x32[:, q % 2, :],
                                                      op0=ALU.mult, op1=ALU.add)
                    EVM = ve(evf, waits=[mm_chain[-(NQP - q)], xld])
                    prev_evm[q] = EVM
                    final_marks.append(dma("sync",
                                           out_r[:, pc, tp0 + q * 512:
                                                 tp0 + (q + 1) * 512],
                                           t_ev[:, q % 2, :], sem=s_o,
                                           waits=[EVM]))
            if ps == 0:
                CACHE3_DONE = s_dc.at()
            prev_pass = [final_marks[-1]]
        return final_marks

    FINALS = ffn_y3()

    b.emit("sync", lambda e: None,
           waits=[s_o.at(), s_dw.at(), s_dx.at(), s_dc.at()], incs=[])
    b.finish()
    return nc, TAP


def _feat_cols(v):
    return np.ascontiguousarray(v.reshape(NCC, 128).T)


def _prep_inputs(inputs, n_sum_tiles=4, n_main_tiles=4):
    f32 = lambda a: np.ascontiguousarray(np.asarray(a), dtype=np.float32)
    x = f32(inputs["L0_attn_x"])
    T_MAIN = n_main_tiles * TT
    T_SUM = n_sum_tiles * TT

    wmod = f32(inputs["L0_wmod"])
    wT = {m: np.ascontiguousarray(f32(inputs["L0_attn_W_" + m]).T) for m in "rkvgo"}
    w1T = np.ascontiguousarray(f32(inputs["L0_ffn_w1"]).T)
    w2T = np.ascontiguousarray(f32(inputs["L0_ffn_w2"]).T)
    w3T = np.ascontiguousarray(f32(inputs["L0_ffn_w3"]).T)
    maa_w1 = f32(inputs["L0_attn_time_maa_w1"])
    maa_w2 = np.ascontiguousarray(f32(inputs["L0_attn_time_maa_w2"]).reshape(160, C))
    dw1 = f32(inputs["L0_attn_time_decay_w1"])
    dw2 = np.ascontiguousarray(np.concatenate(
        [f32(inputs["L0_attn_time_decay_w2"]),
         f32(inputs["L0_attn_time_decay"]).reshape(1, C)], axis=0))
    pf = np.zeros((128, NCC, NPF), np.float32)
    for nm, key in [("maa_w", "time_maa_w"), ("maa_k", "time_maa_k"),
                    ("maa_v", "time_maa_v"), ("maa_r", "time_maa_r"),
                    ("maa_g", "time_maa_g"), ("maa_x", "time_maa_x")]:
        pf[:, :, PF[nm]] = _feat_cols(f32(inputs["L0_attn_" + key]).reshape(C))
    pf[:, :, PF["g1"]] = _feat_cols(f32(inputs["L0_ffn_g1"]))
    pf[:, :, PF["g2"]] = _feat_cols(f32(inputs["L0_ffn_g2"]))
    g3c = np.ascontiguousarray(f32(inputs["L0_ffn_g3"]).reshape(NOC, 128).T)
    u = _feat_cols(f32(inputs["L0_attn_time_faaaa"]).reshape(C))
    gb2 = np.ascontiguousarray(np.stack(
        [f32(inputs["L0_attn_ln_gamma"]), f32(inputs["L0_attn_ln_beta"])], axis=0))
    cmat = f32(inputs["L0_c"])

    ones1 = np.ones((128, 1), bfdt)
    onesf = np.ones((128, 1), np.float32)
    onesr = np.ones((1, 128), bfdt)
    onesrf = np.ones((1, 128), np.float32)
    id128 = np.eye(128, dtype=bfdt)
    mst = np.triu(np.ones((128, 128), np.float32), 1).astype(bfdt)
    mst4 = np.ascontiguousarray(np.tile(mst, (1, 4)))
    ect = np.zeros((128, NCC, 32), bfdt)
    for cc in range(NCC):
        ect[0:64, cc, 2 * cc] = 1
        ect[64:128, cc, 2 * cc + 1] = 1
    ectt = np.zeros((32, NCC * 128), bfdt)
    for cc in range(NCC):
        ectt[2 * cc, cc * 128:cc * 128 + 64] = 1
        ectt[2 * cc + 1, cc * 128 + 64:cc * 128 + 128] = 1

    cbias = np.tile(np.array([[LN_EPS, GN_EPS, RMS_EPS]], np.float32), (128, 1))
    shared = dict(cbias=cbias, wmod=wmod, wr=wT["r"], wk=wT["k"], wv=wT["v"], wg=wT["g"],
                  wo=wT["o"], w1=w1T, w2=w2T, w3=w3T, maa_w1=maa_w1,
                  maa_w2=maa_w2, dec_w1=dw1, dec_w2=dw2, pfeat=pf, g3vec=g3c,
                  uvec=u, gambet=gb2, ones1=ones1, onesf=onesf, onesr=onesr,
                  onesrf=onesrf, id128=id128, mst4=mst4, ect=ect, ectt=ectt,
                  ecttp=ectt)
    per_core = []
    for core in range(8):
        bb, half = core // 2, core % 2
        t0 = half * 2048
        xT = np.ascontiguousarray(x[bb, t0:t0 + T_MAIN, :].T)
        xsT = np.ascontiguousarray(x[bb, 0:T_SUM, :].T) if T_SUM > 0 else \
            np.zeros((C, 1), np.float32)
        if half == 1:
            xprev = _feat_cols(x[bb, t0 - 1, :])
            hmask = np.ones((128, 1), np.float32)
            smask = np.ones((128, 1), np.float32)
        else:
            xprev = np.zeros((128, NCC), np.float32)
            hmask = np.zeros((128, 1), np.float32)
            smask = np.zeros((128, 1), np.float32)
        im = dict(shared)
        im.update(xT=xT, xsT=xsT, xprev=xprev, hmask=hmask, smask=smask,
                  cvec=_feat_cols(cmat[bb]))
        per_core.append(im)
    return per_core


def kernel(**inputs):
    n_sum, n_main = 4, 4
    per_core = _prep_inputs(inputs, n_sum, n_main)
    nc, _ = build_program(n_sum, n_main)
    res = run_bass_kernel_spmd(nc, per_core, core_ids=list(range(8)))
    B, T = 4, 4096
    T_MAIN = n_main * TT
    out = np.zeros((B, T, C), np.float32)
    for core in range(8):
        bb, half = core // 2, core % 2
        out[bb, half * T_MAIN:(half + 1) * T_MAIN, :] = \
            np.asarray(res.results[core]["outT"]).T
    return out
